# version 06
# baseline (speedup 1.0000x reference)
import contextlib
import math

import numpy as np
import ml_dtypes

import concourse.bass as bass
import concourse.mybir as mybir
from concourse.bass_utils import run_bass_kernel_spmd

F32 = mybir.dt.float32
BF16 = mybir.dt.bfloat16
AF = mybir.ActivationFunctionType
ALU = mybir.AluOpType
AX = mybir.AxisListType

B, S, D = 4, 2048, 4096
HQ, HKV, HD, GRP = 32, 8, 128, 4
NB = 16
OWN = 8
VIS = 614
QKV = 6144
SCL = 1.0 / math.sqrt(HD)
NEG = -1.0e20

BF = ml_dtypes.bfloat16


def _perm_qkv():
    ph = np.concatenate([np.arange(0, HD, 2), np.arange(1, HD, 2)])
    perm = np.arange(QKV)
    for h in range(HQ):
        perm[h * HD:(h + 1) * HD] = h * HD + ph
    for g in range(HKV):
        base = HQ * HD + g * HD
        perm[base:base + HD] = base + ph
    return perm


def _blocks(w, bw):
    d, n = w.shape[0], w.shape[1] // bw
    return np.ascontiguousarray(
        w.reshape(32, 128, n, bw).transpose(2, 1, 0, 3).reshape(n, 128, 32 * bw))


def _rope_tables(pos):
    inv = (1.0 / (np.float32(10000.0) **
                  (np.arange(0, HD, 2, dtype=np.float32) / np.float32(HD))))
    ang = pos.astype(np.float32)[None, :] * inv.astype(np.float32)[:, None]
    return np.cos(ang).astype(np.float32), np.sin(ang).astype(np.float32)


def _prep_inputs(inp):
    x = np.asarray(inp["L0_attn_x"], np.float32).reshape(B, S, D)
    perm = _perm_qkv()
    wqkvT = np.asarray(inp["L0_attn_w_qkv"], np.float32).T[:, perm]
    bqkv = np.asarray(inp["L0_attn_b_qkv"], np.float32)[perm]
    bqkv_cols = np.ascontiguousarray(bqkv.reshape(48, 128).T)
    bv_row = np.ascontiguousarray(
        np.asarray(inp["L0_attn_b_qkv"], np.float32)[HQ * HD + HKV * HD:][None, :])
    wqL2 = _blocks(wqkvT[:, :D], 512)
    wkL2 = _blocks(wqkvT[:, D:D + 1024], 512)
    wvL = _blocks(wqkvT[:, D + 1024:], 512)
    wvoL = _blocks(wqkvT[:, D + 1024:], 256)
    bqkv_row = np.ascontiguousarray(bqkv[None, :])
    woL = _blocks(np.asarray(inp["L0_attn_w_o"], np.float32).T, 512)
    bo_row = np.ascontiguousarray(np.asarray(inp["L0_attn_b_o"], np.float32)[None, :])
    w1 = np.asarray(inp["L0_ffn_w1"], np.float32)
    w2 = np.asarray(inp["L0_ffn_w2"], np.float32)
    w1L = np.stack([_blocks(w1[e], 128) for e in range(3)])
    w2L = np.stack([_blocks(w2[e], 512) for e in range(3)])
    b1cols = np.ascontiguousarray(np.asarray(inp["L0_ffn_b1"], np.float32)
                                  .reshape(3, 32, 128).transpose(2, 0, 1).reshape(128, 96))
    b2 = np.ascontiguousarray(np.asarray(inp["L0_ffn_b2"], np.float32))
    gw = np.asarray(inp["L0_ffn_gate_w"], np.float32)
    gw_cols = np.ascontiguousarray(gw[:D].reshape(32, 128, 3)
                                   .transpose(1, 0, 2).reshape(128, 96))
    glast = np.ascontiguousarray(gw[D][None, :])
    gb3 = np.ascontiguousarray(np.asarray(inp["L0_ffn_gate_b"], np.float32)[None, :])

    idt = np.eye(128, dtype=BF)
    ones16r = np.ones((1, 128), dtype=BF)
    ones32r = np.ones((1, 128), dtype=np.float32)
    qi, ki = np.arange(128)[:, None], np.arange(128)[None, :]
    tri = np.where(ki <= qi, np.float32(0.0), np.float32(NEG)).astype(BF)

    shared = {
        "wqL2": wqL2, "wkL2": wkL2, "wvL": wvL, "wvoL": wvoL, "woL": woL,
        "bqkv_cols": bqkv_cols, "bqkv_row": bqkv_row, "bo_row": bo_row,
        "w1L": w1L, "w2L": w2L, "b1cols": b1cols, "b2": b2,
        "gw_cols": gw_cols, "glast": glast, "gb3": gb3,
        "idt": idt, "ones16r": ones16r, "ones32r": ones32r, "tri": tri,
        "fconsts": np.tile(np.array([[1.0 / D, 1.0e-5, -0.5, SCL,
                                      0.0, 0.0, 0.0, 0.0]], np.float32), (128, 1)),
    }
    in_maps, metas = [], []
    for c in range(8):
        b, par = c // 2, c % 2
        own_g = [2 * t + par for t in range(OWN)]
        oth_g = [2 * t + 1 - par for t in range(OWN)]
        lay = own_g + oth_g
        rows = np.concatenate([np.arange(g * 128, (g + 1) * 128) for g in lay])
        pos = rows.astype(np.float32)
        cosk, sink = _rope_tables(pos)
        def _tm(tab, ntile):
            return np.ascontiguousarray(
                tab.T.reshape(ntile, 128, 64).transpose(1, 0, 2).reshape(128, ntile * 64))
        cosk_tm, sink_tm = _tm(cosk, 16), _tm(sink, 16)
        cosq_tm, sinq_tm = _tm(cosk[:, :1024], 8), _tm(sink[:, :1024], 8)
        vm = (pos < VIS).astype(np.float32)
        vt = np.zeros((128, 32), np.float32)
        for L in range(NB):
            vt[:, 2 * L] = vm[L * 128:(L + 1) * 128]
            vt[:, 2 * L + 1] = 1.0 - vm[L * 128:(L + 1) * 128]
        m = dict(shared)
        m.update({
            "x": np.ascontiguousarray(x[b][rows]),
            "t0row": np.ascontiguousarray(x[b, 0][None, :]),
            "t0chunk": np.ascontiguousarray(x[b, 0].reshape(32, 128).T),
            "cosq_tm": cosq_tm, "sinq_tm": sinq_tm,
            "cosk_tm": cosk_tm, "sink_tm": sink_tm,
            "pat": np.full((128, 128), np.float32(NEG) if par == 0 else np.float32(0.0), dtype=BF),
            "vt_mask": vt,
        })
        in_maps.append(m)
        metas.append({"b": b, "own_g": own_g})
    return in_maps, metas


class SemC:
    def __init__(self, ctx, nc, name):
        self.h = ctx.enter_context(nc.semaphore(name))
        self.n = 0


def build_program(stop_after="all", debug=(), serialize=False):
    nc = bass.Bass("TRN2", target_bir_lowering=False)
    dbg = set(debug)

    DT = {}
    def din(name, shape, dtype=F32):
        DT[name] = nc.dram_tensor(name, shape, dtype, kind="ExternalInput")
        return DT[name]

    x = din("x", [S, D])
    t0row_d = din("t0row", [1, D])
    t0chunk_d = din("t0chunk", [128, 32])
    wqL2_d = din("wqL2", [8, 128, 16384])
    wkL2_d = din("wkL2", [2, 128, 16384])
    wvL_d = din("wvL", [2, 128, 16384])
    wvoL_d = din("wvoL", [4, 128, 8192])
    woL_d = din("woL", [8, 128, 16384])
    bqkv_cols_d = din("bqkv_cols", [128, 48])
    bqkv_row_d = din("bqkv_row", [1, QKV])
    bo_row_d = din("bo_row", [1, D])
    w1L_d = din("w1L", [3, 32, 128, D])
    w2L_d = din("w2L", [3, 8, 128, 16384])
    b1cols_d = din("b1cols", [128, 96])
    b2_d = din("b2", [3, D])
    gw_cols_d = din("gw_cols", [128, 96])
    glast_d = din("glast", [1, 3])
    gb3_d = din("gb3", [1, 3])
    cosq_d = din("cosq_tm", [128, 512])
    sinq_d = din("sinq_tm", [128, 512])
    cosk_d = din("cosk_tm", [128, 1024])
    sink_d = din("sink_tm", [128, 1024])
    idt_d = din("idt", [128, 128], BF16)
    ones16_d = din("ones16r", [1, 128], BF16)
    ones32_d = din("ones32r", [1, 128], F32)
    tri_d = din("tri", [128, 128], BF16)
    pat_d = din("pat", [128, 128], BF16)
    vt_mask_d = din("vt_mask", [128, 32])
    fconsts_d = din("fconsts", [128, 8])

    xn_oth_h = nc.dram_tensor("xn_oth_h", [128, 32768], BF16, kind="Internal")
    xnA_h = nc.dram_tensor("xnA_h", [128, 16384], BF16, kind="Internal")
    xnB_h = nc.dram_tensor("xnB_h", [128, 16384], BF16, kind="Internal")
    av_h = nc.dram_tensor("av_h", [128, 32768], BF16, kind="Internal")
    xa_h = nc.dram_tensor("xa_h", [1024, D], F32, kind="Internal")
    y = nc.dram_tensor("y", [1024, D], F32, kind="ExternalOutput")

    dbg_outs = {}
    def mkdbg(name, shape, dtype=F32):
        if name in dbg:
            dbg_outs[name] = (nc.dram_tensor("dbg_" + name, shape, dtype,
                                             kind="ExternalOutput"), None)
    mkdbg("xnA", [128, 16384], BF16)
    mkdbg("xnB", [128, 16384], BF16)
    mkdbg("xn_oth", [128, 32768], BF16)
    mkdbg("k_sb", [128, 16384], BF16)
    mkdbg("v_sb", [128, 16384], BF16)
    mkdbg("rs_cols", [128, 16])
    mkdbg("gate_sb", [128, 24])
    mkdbg("c_cols", [128, 24])
    mkdbg("misc", [128, 32])
    mkdbg("row_sc", [1, 64])
    mkdbg("av", [128, 32768], BF16)
    mkdbg("xa", [1024, D])

    ctx = contextlib.ExitStack()
    with ctx:
        pe = SemC(ctx, nc, "pe")
        dv = SemC(ctx, nc, "dv")
        ac = SemC(ctx, nc, "ac")
        qx = SemC(ctx, nc, "qx")
        qw = SemC(ctx, nc, "qw")
        qo = SemC(ctx, nc, "qo")
        sgc = SemC(ctx, nc, "sgc")
        skc = SemC(ctx, nc, "skc")
        sxt2 = [SemC(ctx, nc, f"sxt{i}") for i in range(2)]
        soth2 = [SemC(ctx, nc, f"soth{i}") for i in range(2)]
        so5 = SemC(ctx, nc, "so5")
        swk = SemC(ctx, nc, "swk")
        swv2 = [SemC(ctx, nc, f"swv{i}") for i in range(2)]
        swvo2 = [SemC(ctx, nc, f"swvo{i}") for i in range(2)]
        swq = SemC(ctx, nc, "swq")
        sav = SemC(ctx, nc, "sav")
        sp3 = SemC(ctx, nc, "sp3")
        swo2 = [SemC(ctx, nc, f"swo{i}") for i in range(2)]
        sxo2 = [SemC(ctx, nc, f"sxo{i}") for i in range(2)]
        sxa2 = [SemC(ctx, nc, f"sxa{i}") for i in range(2)]
        sxh = SemC(ctx, nc, "sxh")
        sw12 = [SemC(ctx, nc, f"sw1{i}") for i in range(2)]
        sw22 = [SemC(ctx, nc, f"sw2{i}") for i in range(2)]
        sb2s = SemC(ctx, nc, "sb2s")
        sxf2 = [SemC(ctx, nc, f"sxf{i}") for i in range(2)]
        sy2 = [SemC(ctx, nc, f"sy{i}") for i in range(2)]
        ALLSEMS = ([pe, dv, ac, qx, qw, qo, sgc, skc, so5, swk, swq, sav, sp3,
                    sxh, sb2s] + sxt2 + soth2 + swv2 + swvo2 + swo2 + sxo2 +
                   sxa2 + sw12 + sw22 + sxf2 + sy2)

        idt16 = ctx.enter_context(nc.sbuf_tensor("idt16", [128, 128], BF16))
        tri16 = ctx.enter_context(nc.sbuf_tensor("tri16", [128, 128], BF16))
        pat16 = ctx.enter_context(nc.sbuf_tensor("pat16", [128, 128], BF16))
        ones16 = ctx.enter_context(nc.sbuf_tensor("ones16", [1, 128], BF16))
        ones32 = ctx.enter_context(nc.sbuf_tensor("ones32", [1, 128], F32))
        bqkv_cols = ctx.enter_context(nc.sbuf_tensor("bqkv_colsS", [128, 48], F32))
        sbq_cols = ctx.enter_context(nc.sbuf_tensor("sbq_cols", [128, 32], F32))
        vt_mask = ctx.enter_context(nc.sbuf_tensor("vt_maskS", [128, 32], F32))
        gw16 = ctx.enter_context(nc.sbuf_tensor("gw16", [128, 96], BF16))
        gw32 = ctx.enter_context(nc.sbuf_tensor("gw32", [128, 96], F32))
        bqr16 = ctx.enter_context(nc.sbuf_tensor("bqr16", [1, QKV], BF16))
        rs_cols = ctx.enter_context(nc.sbuf_tensor("rs_colsS", [128, 16], F32))
        gate_sb = ctx.enter_context(nc.sbuf_tensor("gate_sbS", [128, 24], F32))
        c_cols = ctx.enter_context(nc.sbuf_tensor("c_colsS", [128, 24], F32))
        misc = ctx.enter_context(nc.sbuf_tensor("miscS", [128, 32], F32))
        row_sc = ctx.enter_context(nc.sbuf_tensor("row_scS", [1, 64], F32))
        t0c_sb = ctx.enter_context(nc.sbuf_tensor("t0c_sb", [128, 32], F32))
        t0n = ctx.enter_context(nc.sbuf_tensor("t0n", [128, 32], F32))
        glast_sb = ctx.enter_context(nc.sbuf_tensor("glast_sb", [1, 3], F32))
        gb_sb = ctx.enter_context(nc.sbuf_tensor("gb_sb", [1, 3], F32))
        fc_sb = ctx.enter_context(nc.sbuf_tensor("fc_sb", [128, 8], F32))
        for _i, _v in enumerate([1.0 / D, 1.0e-5, -0.5, SCL]):
            nc.const_aps.register(F32, _v, fc_sb[:, _i:_i + 1])

        blk = ctx.enter_context(nc.Block())
        T, V, A, G, Y = blk.tensor, blk.vector, blk.scalar, blk.gpsimd, blk.sync
        rec = {}

        def phase_barrier():
            tgt = [(s.h, s.n) for s in ALLSEMS if s.n > 0]
            for emit in (T, V, A, G, Y):
                def body(e, tgt=tuple(tgt)):
                    for h, n in tgt:
                        e.wait_ge(h, n)
                emit(body)

        def g_consts(g):
            for dst, src in [(idt16, idt_d), (tri16, tri_d), (pat16, pat_d),
                             (ones16, ones16_d), (ones32, ones32_d),
                             (bqkv_cols, bqkv_cols_d), (vt_mask, vt_mask_d),
                             (gw16, gw_cols_d), (gw32, gw_cols_d),
                             (bqr16, bqkv_row_d)]:
                g.dma_start(dst[:], src[:]).then_inc(sgc.h, 16)
        G(g_consts)
        sgc.n += 160
        rec["consts"] = sgc.n

        def s_consts(sy):
            for dst, src in [(t0c_sb, t0chunk_d),
                             (glast_sb, glast_d), (gb_sb, gb3_d),
                             (fc_sb, fconsts_d)]:
                sy.dma_start(dst[:], src[:]).then_inc(skc.h, 16)
        Y(s_consts)
        skc.n += 64
        rec["sconsts"] = skc.n

        def v_sbq(v):
            v.wait_ge(sgc.h, rec["consts"])
            v.tensor_scalar_mul(sbq_cols[:], bqkv_cols[:, 0:32], SCL).then_inc(dv.h, 1)
        V(v_sbq)
        dv.n += 1

        kv_scope = contextlib.ExitStack()
        xnA = kv_scope.enter_context(nc.sbuf_tensor("xnA", [128, 16384], BF16))
        xnB = kv_scope.enter_context(nc.sbuf_tensor("xnB", [128, 16384], BF16))

        p1a = contextlib.ExitStack()
        with p1a:
            xbuf = [p1a.enter_context(nc.sbuf_tensor(f"xbuf{i}", [128, D], F32)) for i in range(2)]
            xcb = p1a.enter_context(nc.sbuf_tensor("xcb", [128, D], F32))
            sqb = p1a.enter_context(nc.sbuf_tensor("sqb", [128, D], F32))
            xn16 = [p1a.enter_context(nc.sbuf_tensor(f"xn16_{i}", [128, D], BF16)) for i in range(2)]
            oth_st = [p1a.enter_context(nc.sbuf_tensor(f"oth_st{i}", [128, D], BF16)) for i in range(2)]
            stats = p1a.enter_context(nc.sbuf_tensor("stats", [128, 16], F32))
            t0r_sb = p1a.enter_context(nc.sbuf_tensor("t0r_sb", [1, D], F32))
            t0xc = p1a.enter_context(nc.sbuf_tensor("t0xc", [1, D], F32))
            tr_ps = [p1a.enter_context(nc.psum_tensor(f"tr_ps{i}", [128, 128], BF16)) for i in range(2)]
            g_ps = p1a.enter_context(nc.psum_tensor("g_ps", [128, 3], F32))
            vt_ps = p1a.enter_context(nc.psum_tensor("vt_ps", [1, 2], F32))
            bc_ps = p1a.enter_context(nc.psum_tensor("bc_ps", [128, 8], F32))
            t0_ps = p1a.enter_context(nc.psum_tensor("t0_ps", [1, 4], F32))

            def s_t0r(sy):
                sy.dma_start(t0r_sb[:], t0row_d[:]).then_inc(skc.h, 16)
            Y(s_t0r)
            skc.n += 16
            rec["t0r"] = skc.n

            NS, NM, SSQ, RSTD, SAB = 0, 1, 2, 3, 4
            tr_g = 0

            for L in range(NB):
                if serialize:
                    phase_barrier()
                o = 8 * (L % 2)

                def s_load(sy, L=L):
                    if L >= 2:
                        sy.wait_ge(dv.h, rec[("negmean", L - 2)])
                        sy.wait_ge(ac.h, rec[("xbfree_a", L - 2)])
                    sy.dma_start(xbuf[L % 2][:], x[L * 128:(L + 1) * 128, :]).then_inc(sxt2[L % 2].h, 16)
                Y(s_load)
                sxt2[L % 2].n += 16
                rec[("xload", L)] = sxt2[L % 2].n

                def v_stats(v, L=L, o=o):
                    v.wait_ge(sxt2[L % 2].h, rec[("xload", L)])
                    v.reduce_sum(stats[:, o + NS:o + NS + 1], xbuf[L % 2][:],
                                 axis=AX.X, negate=True).then_inc(dv.h, 1)
                    v.wait_ge(dv.h, dv.n + 1)
                    v.tensor_scalar_mul(stats[:, o + NM:o + NM + 1],
                                        stats[:, o + NS:o + NS + 1],
                                        1.0 / D).then_inc(dv.h, 1)
                V(v_stats)
                dv.n += 2
                rec[("negmean", L)] = dv.n

                def a_norm(s, L=L, o=o):
                    s.wait_ge(dv.h, rec[("negmean", L)])
                    if L >= 1:
                        s.wait_ge(dv.h, rec[("v2done", L - 1)])
                    s.activation(xcb[:], xbuf[L % 2][:], AF.Identity,
                                 bias=stats[:, o + NM:o + NM + 1]).then_inc(ac.h, 1)
                    s.activation(sqb[:], xcb[:], AF.Square,
                                 accum_out=stats[:, o + SSQ:o + SSQ + 1])
                    s.activation(stats[:, o + RSTD:o + RSTD + 1],
                                 stats[:, o + SSQ:o + SSQ + 1],
                                 AF.Ln, scale=1.0 / D, bias=1.0e-5)
                    s.activation(stats[:, o + RSTD:o + RSTD + 1],
                                 stats[:, o + RSTD:o + RSTD + 1],
                                 AF.Exp, scale=-0.5).then_inc(ac.h, 1)
                A(a_norm)
                ac.n += 1
                rec[("xbfree_a", L)] = ac.n
                ac.n += 1
                rec[("anorm", L)] = ac.n

                def v2(v, L=L, o=o):
                    v.wait_ge(ac.h, rec[("anorm", L)])
                    if L >= 2:
                        v.wait_ge(pe.h, rec[("trdone", L - 2)])
                    v.reduce_sum(stats[:, o + SAB:o + SAB + 1], xcb[:], axis=AX.X,
                                 apply_absolute_value=True).then_inc(dv.h, 1)
                    v.tensor_scalar_mul(xn16[L % 2][:], xcb[:],
                                        stats[:, o + RSTD:o + RSTD + 1]).then_inc(dv.h, 1)
                    v.wait_ge(dv.h, dv.n + 1)
                    v.tensor_mul(rs_cols[:, L:L + 1], stats[:, o + SAB:o + SAB + 1],
                                 stats[:, o + RSTD:o + RSTD + 1]).then_inc(dv.h, 1)
                V(v2)
                dv.n += 3
                rec[("rs", L)] = rec[("v2done", L)] = dv.n

                for c in range(32):
                    rec[("evac_g", tr_g + c)] = ac.n + c + 1

                def t_blk(t, L=L, tr_g0=tr_g):
                    t.wait_ge(dv.h, rec[("rs", L)])
                    t.matmul(vt_ps[:], rs_cols[:, L:L + 1], vt_mask[:, 2 * L:2 * L + 2],
                             start=(L == 0), stop=(L == NB - 1)).then_inc(pe.h, 1)
                    for c in range(32):
                        gidx = tr_g0 + c
                        if gidx >= 2:
                            t.wait_ge(ac.h, rec[("evac_g", gidx - 2)])
                        t.transpose(tr_ps[gidx % 2][:],
                                    xn16[L % 2][:, c * 128:(c + 1) * 128],
                                    idt16[:]).then_inc(pe.h, 1)
                T(t_blk)
                pe.n += 1
                rec[("vt", L)] = pe.n
                for c in range(32):
                    pe.n += 1
                    rec[("tr_g", tr_g + c)] = pe.n
                rec[("trdone", L)] = pe.n

                def a_evac(s, L=L, tr_g0=tr_g):
                    for c in range(32):
                        gidx = tr_g0 + c
                        s.wait_ge(pe.h, rec[("tr_g", gidx)])
                        if L >= 10 and c == 0:
                            s.wait_ge(soth2[L % 2].h, rec[("othst", L - 2)])
                        if L < 8:
                            half = xnA if L < 4 else xnB
                            dst = half[:, c * 512 + (L % 4) * 128:
                                       c * 512 + (L % 4) * 128 + 128]
                        else:
                            dst = oth_st[L % 2][:, c * 128:(c + 1) * 128]
                        s.copy(dst, tr_ps[gidx % 2][:]).then_inc(ac.h, 1)
                A(a_evac)
                ac.n += 32
                assert ac.n == rec[("evac_g", tr_g + 31)]
                tr_g += 32
                rec[("evacdone", L)] = ac.n

                if L >= 8:
                    def s_oth(sy, L=L):
                        sy.wait_ge(ac.h, rec[("evacdone", L)])
                        dst = xn_oth_h[:].rearrange("p (c t) -> p c t", t=1024)[
                            :, :, (L - 8) * 128:(L - 8) * 128 + 128]
                        src = oth_st[L % 2][:].rearrange("p (c t) -> p c t", t=128)
                        sy.dma_start(dst, src).then_inc(soth2[L % 2].h, 16)
                    Y(s_oth)
                    soth2[L % 2].n += 16
                    rec[("othst", L)] = soth2[L % 2].n
                else:
                    def t_gate(t, L=L):
                        t.wait_ge(ac.h, rec[("evacdone", L)])
                        if L >= 1:
                            t.wait_ge(dv.h, rec[("gev", L - 1)])
                        half = xnA if L < 4 else xnB
                        for c in range(32):
                            mm = t.matmul(
                                g_ps[:],
                                half[:, c * 512 + (L % 4) * 128:
                                     c * 512 + (L % 4) * 128 + 128],
                                gw16[:, 3 * c:3 * c + 3],
                                start=(c == 0), stop=(c == 31))
                        mm.then_inc(pe.h, 1)
                    T(t_gate)
                    pe.n += 1
                    rec[("gmm", L)] = pe.n

                    def v_gev(v, L=L):
                        v.wait_ge(pe.h, rec[("gmm", L)])
                        v.tensor_copy(gate_sb[:, 3 * L:3 * L + 3], g_ps[:]).then_inc(dv.h, 1)
                    V(v_gev)
                    dv.n += 1
                    rec[("gev", L)] = dv.n

            def v_ratio(v):
                n0, k = dv.n, 0
                def d(instr):
                    nonlocal k
                    k += 1
                    instr.then_inc(dv.h, 1)
                    v.wait_ge(dv.h, n0 + k)
                v.wait_ge(pe.h, rec[("vt", NB - 1)])
                d(v.tensor_copy(row_sc[0:1, 0:2], vt_ps[:]))
                d(v.tensor_scalar_mul(row_sc[0:1, 2:3], row_sc[0:1, 0:1], 1.0 / (VIS * D)))
                d(v.tensor_scalar_add(row_sc[0:1, 3:4], row_sc[0:1, 2:3], 1.0e-8))
                d(v.reciprocal(row_sc[0:1, 4:5], row_sc[0:1, 3:4]))
                d(v.tensor_scalar_mul(row_sc[0:1, 5:6], row_sc[0:1, 1:2],
                                      1.0 / ((S - VIS) * D)))
                d(v.tensor_mul(row_sc[0:1, 6:7], row_sc[0:1, 5:6], row_sc[0:1, 4:5]))
                v.wait_ge(skc.h, rec["sconsts"])
                d(v.tensor_scalar_mul(row_sc[0:1, 8:11], glast_sb[:], row_sc[0:1, 6:7]))
                d(v.tensor_add(row_sc[0:1, 8:11], row_sc[0:1, 8:11], gb_sb[:]))
                v.wait_ge(skc.h, rec["t0r"])
                d(v.reduce_sum(row_sc[0:1, 12:13], t0r_sb[:], axis=AX.X, negate=True))
                d(v.tensor_scalar_mul(row_sc[0:1, 13:14], row_sc[0:1, 12:13], 1.0 / D))
                d(v.tensor_scalar_add(t0xc[:], t0r_sb[:], row_sc[0:1, 13:14]))
            V(v_ratio)
            dv.n += 11
            rec["t0xc"] = dv.n

            def a_t0(s):
                s.wait_ge(dv.h, rec["t0xc"])
                s.activation(t0r_sb[:], t0xc[:], AF.Square, accum_out=row_sc[0:1, 15:16])
                s.activation(row_sc[0:1, 14:15], row_sc[0:1, 15:16],
                             AF.Ln, scale=1.0 / D, bias=1.0e-5)
                s.activation(row_sc[0:1, 14:15], row_sc[0:1, 14:15],
                             AF.Exp, scale=-0.5).then_inc(ac.h, 1)
            A(a_t0)
            ac.n += 1
            rec["t0rstd"] = ac.n

            def t_t0bc(t):
                t.wait_ge(ac.h, rec["t0rstd"])
                t.matmul(bc_ps[:, 0:2], ones32[:], row_sc[0:1, 13:15],
                         start=True, stop=True)
                t.matmul(bc_ps[:, 2:5], ones32[:], row_sc[0:1, 8:11],
                         start=False, stop=True).then_inc(pe.h, 1)
            T(t_t0bc)
            pe.n += 1
            rec["t0bc"] = pe.n

            def v_t0n(v):
                n0, k = dv.n, 0
                def d(instr):
                    nonlocal k
                    k += 1
                    instr.then_inc(dv.h, 1)
                    v.wait_ge(dv.h, n0 + k)
                v.wait_ge(pe.h, rec["t0bc"])
                d(v.tensor_copy(misc[:, 0:5], bc_ps[:, 0:5]))
                d(v.tensor_scalar_add(t0n[:], t0c_sb[:], misc[:, 0:1]))
                d(v.tensor_scalar_mul(t0n[:], t0n[:], misc[:, 1:2]))
            V(v_t0n)
            dv.n += 3
            rec["t0n"] = dv.n

            def t_t0g(t):
                t.wait_ge(dv.h, rec["t0n"])
                for c in range(32):
                    mm = t.matmul(t0_ps[0:1, 0:3], t0n[:, c:c + 1],
                                  gw32[:, 3 * c:3 * c + 3],
                                  start=(c == 0), stop=(c == 31))
                mm.then_inc(pe.h, 1)
            T(t_t0g)
            pe.n += 1
            rec["t0g"] = pe.n

            def v_t0fin(v):
                n0, k = dv.n, 0
                def d(instr):
                    nonlocal k
                    k += 1
                    instr.then_inc(dv.h, 1)
                    v.wait_ge(dv.h, n0 + k)
                v.wait_ge(pe.h, rec["t0g"])
                d(v.tensor_copy(row_sc[0:1, 16:19], t0_ps[0:1, 0:3]))
                d(v.tensor_add(row_sc[0:1, 20:23], row_sc[0:1, 16:19], row_sc[0:1, 8:11]))
                d(v.reduce_max(row_sc[0:1, 24:25], row_sc[0:1, 20:23],
                               axis=AX.X, negate=True))
            V(v_t0fin)
            dv.n += 3
            rec["t0lg"] = dv.n

            def a_t0e(s):
                s.wait_ge(dv.h, rec["t0lg"])
                s.activation(row_sc[0:1, 25:28], row_sc[0:1, 20:23], AF.Exp,
                             bias=row_sc[0:1, 24:25],
                             accum_out=row_sc[0:1, 28:29]).then_inc(ac.h, 1)
            A(a_t0e)
            ac.n += 1
            rec["t0e"] = ac.n

            def v_t0sel(v):
                n0, k = dv.n, 0
                def d(instr):
                    nonlocal k
                    k += 1
                    instr.then_inc(dv.h, 1)
                    v.wait_ge(dv.h, n0 + k)
                v.wait_ge(ac.h, rec["t0e"])
                d(v.reciprocal(row_sc[0:1, 29:30], row_sc[0:1, 28:29]))
                d(v.tensor_scalar_mul(row_sc[0:1, 30:33], row_sc[0:1, 25:28],
                                      row_sc[0:1, 29:30]))
                d(v.reduce_max(row_sc[0:1, 33:34], row_sc[0:1, 30:33], axis=AX.X))
                d(v.tensor_tensor(row_sc[0:1, 34:35], row_sc[0:1, 30:31],
                                  row_sc[0:1, 31:32], ALU.min))
                d(v.tensor_tensor(row_sc[0:1, 35:36], row_sc[0:1, 30:31],
                                  row_sc[0:1, 31:32], ALU.max))
                d(v.tensor_tensor(row_sc[0:1, 36:37], row_sc[0:1, 35:36],
                                  row_sc[0:1, 32:33], ALU.min))
                d(v.tensor_tensor(row_sc[0:1, 37:38], row_sc[0:1, 34:35],
                                  row_sc[0:1, 36:37], ALU.max))
                d(v.tensor_scalar(row_sc[0:1, 38:41], row_sc[0:1, 30:33],
                                  row_sc[0:1, 33:34], None, ALU.is_equal))
                d(v.tensor_scalar(row_sc[0:1, 41:44], row_sc[0:1, 30:33],
                                  row_sc[0:1, 37:38], None, ALU.is_equal))
            V(v_t0sel)
            dv.n += 9
            rec["t0sel"] = dv.n

            def t_ibc(t):
                t.wait_ge(dv.h, rec["t0sel"])
                t.matmul(bc_ps[:, 0:6], ones32[:], row_sc[0:1, 38:44],
                         start=True, stop=True).then_inc(pe.h, 1)
            T(t_ibc)
            pe.n += 1
            rec["ibc"] = pe.n

            def v_ibc(v):
                v.wait_ge(pe.h, rec["ibc"])
                v.tensor_copy(misc[:, 8:14], bc_ps[:, 0:6]).then_inc(dv.h, 1)
            V(v_ibc)
            dv.n += 1
            rec["ibcv"] = dv.n

            for tt in range(OWN):
                def v_ga(v, tt=tt):
                    n0 = dv.n
                    if tt == 0:
                        v.wait_ge(dv.h, rec["ibcv"])
                        v.wait_ge(dv.h, rec[("gev", OWN - 1)])
                    else:
                        v.wait_ge(ac.h, rec[("gexp", tt - 1)])
                    v.tensor_add(misc[:, 16:19], gate_sb[:, 3 * tt:3 * tt + 3],
                                 misc[:, 2:5]).then_inc(dv.h, 1)
                    v.wait_ge(dv.h, n0 + 1)
                    v.reduce_max(misc[:, 19:20], misc[:, 16:19],
                                 axis=AX.X, negate=True).then_inc(dv.h, 1)
                V(v_ga)
                dv.n += 2
                rec[("glg", tt)] = dv.n

                def a_gexp(s, tt=tt):
                    s.wait_ge(dv.h, rec[("glg", tt)])
                    s.activation(misc[:, 20:23], misc[:, 16:19], AF.Exp,
                                 bias=misc[:, 19:20],
                                 accum_out=misc[:, 23:24]).then_inc(ac.h, 1)
                A(a_gexp)
                ac.n += 1
                rec[("gexp", tt)] = ac.n

                def v_gtv(v, tt=tt):
                    n0, k = dv.n, 0
                    def d(instr):
                        nonlocal k
                        k += 1
                        instr.then_inc(dv.h, 1)
                        v.wait_ge(dv.h, n0 + k)
                    v.wait_ge(ac.h, rec[("gexp", tt)])
                    d(v.reciprocal(misc[:, 24:25], misc[:, 23:24]))
                    d(v.tensor_scalar_mul(misc[:, 25:28], misc[:, 20:23], misc[:, 24:25]))
                    d(v.reduce_max(misc[:, 28:29], misc[:, 25:28], axis=AX.X))
                    d(v.tensor_tensor(misc[:, 29:30], misc[:, 25:26], misc[:, 26:27], ALU.min))
                    d(v.tensor_tensor(misc[:, 30:31], misc[:, 25:26], misc[:, 26:27], ALU.max))
                    d(v.tensor_tensor(misc[:, 31:32], misc[:, 30:31], misc[:, 27:28], ALU.min))
                    d(v.tensor_tensor(misc[:, 29:30], misc[:, 29:30], misc[:, 31:32], ALU.max))
                    d(v.tensor_add(misc[:, 30:31], misc[:, 28:29], misc[:, 29:30]))
                    d(v.tensor_scalar_add(misc[:, 30:31], misc[:, 30:31], 1.0e-8))
                    d(v.reciprocal(misc[:, 31:32], misc[:, 30:31]))
                    d(v.tensor_mul(misc[:, 28:29], misc[:, 28:29], misc[:, 31:32]))
                    d(v.tensor_mul(misc[:, 29:30], misc[:, 29:30], misc[:, 31:32]))
                    for e in range(3):
                        d(v.tensor_mul(misc[:, 30:31], misc[:, 28:29], misc[:, 8 + e:9 + e]))
                        d(v.scalar_tensor_tensor(
                            c_cols[:, 3 * tt + e:3 * tt + e + 1],
                            misc[:, 29:30], misc[:, 11 + e:12 + e],
                            misc[:, 30:31], ALU.mult, ALU.add))
                V(v_gtv)
                dv.n += 18
                rec[("gtv", tt)] = dv.n

        phase_barrier()

        k_sb = kv_scope.enter_context(nc.sbuf_tensor("k_sb", [128, 16384], BF16))

        p1b = contextlib.ExitStack()
        with p1b:
            ps_v = [p1b.enter_context(nc.psum_tensor(f"ps_v{i}", [128, 512], F32)) for i in range(2)]

            p1bk = contextlib.ExitStack()
            oth512k = p1bk.enter_context(nc.sbuf_tensor("oth512k", [128, 16384], BF16))
            cosk = p1bk.enter_context(nc.sbuf_tensor("coskS", [128, 1024], F32))
            sink = p1bk.enter_context(nc.sbuf_tensor("sinkS", [128, 1024], F32))
            wk_sb = p1bk.enter_context(nc.sbuf_tensor("wk0", [128, 16384], BF16))
            kst = [p1bk.enter_context(nc.sbuf_tensor(f"kst{i}", [128, 512], F32)) for i in range(2)]
            ktm = [p1bk.enter_context(nc.sbuf_tensor(f"ktm{i}", [128, 512], BF16)) for i in range(2)]
            ur = p1bk.enter_context(nc.sbuf_tensor("ur", [128, 256], F32))
            ui = p1bk.enter_context(nc.sbuf_tensor("ui", [128, 256], F32))
            trk_ps = [p1bk.enter_context(nc.psum_tensor(f"trk{i}", [128, 128], BF16)) for i in range(2)]

            def hview(ap, w):
                return ap.rearrange("p (h x) -> p h x", x=w)

            def bcast4(ap2d):
                return bass.AP(ap2d.tensor, ap2d.offset,
                               [list(ap2d.ap[0]), [0, 4], list(ap2d.ap[1])])

            def s_cos(sy):
                sy.dma_start(cosk[:], cosk_d[:]).then_inc(skc.h, 16)
                sy.dma_start(sink[:], sink_d[:]).then_inc(skc.h, 16)
            Y(s_cos)
            skc.n += 32
            rec["cosk"] = skc.n

            kv_i = 0
            ktr_g = 0
            for kb in range(2):
                def g_wk(g, kb=kb):
                    if kb >= 1:
                        g.wait_ge(pe.h, rec[("kmm", kv_i - 1)])
                    g.dma_start(wk_sb[:], wkL2_d[kb]).then_inc(swk.h, 16)
                G(g_wk)
                swk.n += 16
                rec[("wk", kb)] = swk.n

                for tt in range(16):
                    if tt >= 8 and (tt - 8) % 4 == 0:
                        oh = (tt - 8) // 4
                        def s_o512(sy, oh=oh):
                            prev = rec.get(("o512_user",))
                            if prev is not None:
                                sy.wait_ge(pe.h, prev)
                            src = xn_oth_h[:].rearrange("p (c t) -> p c t", t=1024)[
                                :, :, oh * 512: oh * 512 + 512]
                            sy.dma_start(oth512k[:].rearrange("p (c t) -> p c t", t=512),
                                         src).then_inc(so5.h, 16)
                        Y(s_o512)
                        so5.n += 16
                        rec[("o512k", kb, oh)] = so5.n

                    def t_k(t, kb=kb, tt=tt, kv_i=kv_i):
                        t.wait_ge(swk.h, rec[("wk", kb)])
                        if tt >= 8:
                            t.wait_ge(so5.h, rec[("o512k", kb, (tt - 8) // 4)])
                        if kv_i >= 2:
                            t.wait_ge(ac.h, rec[("kev", kv_i - 2)])
                        for dc in range(32):
                            if tt < 8:
                                half = xnA if tt < 4 else xnB
                                xsrc = half[:, dc * 512 + (tt % 4) * 128:
                                            dc * 512 + (tt % 4) * 128 + 128]
                            else:
                                t4 = (tt - 8) % 4
                                xsrc = oth512k[:, dc * 512 + t4 * 128:
                                               dc * 512 + t4 * 128 + 128]
                            t.matmul(ps_v[kv_i % 2][:], xsrc,
                                     wk_sb[:, dc * 512:(dc + 1) * 512],
                                     start=(dc == 0), stop=False)
                        mm = t.matmul(ps_v[kv_i % 2][:], ones16[:],
                                      bqr16[0:1, D + kb * 512: D + (kb + 1) * 512],
                                      start=False, stop=True)
                        mm.then_inc(pe.h, 1)
                    T(t_k)
                    pe.n += 1
                    rec[("kmm", kv_i)] = pe.n
                    if tt >= 8:
                        rec[("o512_user",)] = pe.n

                    def a_k(s, kv_i=kv_i):
                        s.wait_ge(pe.h, rec[("kmm", kv_i)])
                        if kv_i >= 2:
                            s.wait_ge(dv.h, rec[("krope", kv_i - 2)])
                        s.activation(kst[kv_i % 2][:], ps_v[kv_i % 2][:],
                                     AF.Identity).then_inc(ac.h, 1)
                    A(a_k)
                    ac.n += 1
                    rec[("kev", kv_i)] = ac.n

                    def v_krope(v, tt=tt, kv_i=kv_i):
                        v.wait_ge(ac.h, rec[("kev", kv_i)])
                        v.wait_ge(skc.h, rec["cosk"])
                        if kv_i >= 2:
                            v.wait_ge(pe.h, rec[("ktrdone", kv_i - 2)])
                        stg = hview(kst[kv_i % 2][:], 128)
                        r, i = stg[:, :, 0:64], stg[:, :, 64:128]
                        cs = bcast4(cosk[:, tt * 64:(tt + 1) * 64])
                        sn = bcast4(sink[:, tt * 64:(tt + 1) * 64])
                        kt = hview(ktm[kv_i % 2][:], 128)
                        urv = hview(ur[:], 64)
                        uiv = hview(ui[:], 64)
                        v.tensor_tensor(urv, r, cs, ALU.mult)
                        v.tensor_tensor(uiv, i, sn, ALU.mult)
                        v.tensor_tensor(kt[:, :, 0:64], urv, uiv, ALU.subtract)
                        v.tensor_tensor(urv, r, sn, ALU.mult)
                        v.tensor_tensor(uiv, i, cs, ALU.mult)
                        v.tensor_tensor(kt[:, :, 64:128], urv, uiv, ALU.add).then_inc(dv.h, 1)
                    V(v_krope)
                    dv.n += 1
                    rec[("krope", kv_i)] = dv.n

                    for h in range(4):
                        rec[("ktrev_pre", ktr_g + h)] = None
                    def t_ktr(t, kb=kb, tt=tt, kv_i=kv_i, ktr_g0=ktr_g):
                        t.wait_ge(dv.h, rec[("krope", kv_i)])
                        for h in range(4):
                            gi = ktr_g0 + h
                            if gi >= 2:
                                t.wait_ge(ac.h, rec[("ktrev", gi - 2)])
                            t.transpose(trk_ps[gi % 2][:],
                                        ktm[kv_i % 2][:, h * 128:(h + 1) * 128],
                                        idt16[:]).then_inc(pe.h, 1)
                    for h in range(4):
                        rec[("ktrev", ktr_g + h)] = ac.n + h + 1
                    T(t_ktr)
                    for h in range(4):
                        pe.n += 1
                        rec[("ktr", ktr_g + h)] = pe.n
                    rec[("ktrdone", kv_i)] = pe.n

                    def a_ktr(s, kb=kb, tt=tt, ktr_g0=ktr_g):
                        for h in range(4):
                            gi = ktr_g0 + h
                            s.wait_ge(pe.h, rec[("ktr", gi)])
                            g128 = (4 * kb + h) * 2048
                            s.copy(k_sb[:, g128 + tt * 128: g128 + (tt + 1) * 128],
                                   trk_ps[gi % 2][:]).then_inc(ac.h, 1)
                    A(a_ktr)
                    ac.n += 4
                    assert ac.n == rec[("ktrev", ktr_g + 3)]
                    ktr_g += 4
                    kv_i += 1

            p1bk.close()
            phase_barrier()

            v_sb = kv_scope.enter_context(nc.sbuf_tensor("v_sb", [128, 16384], BF16))
            p1bv = contextlib.ExitStack()
            wv_sb = [p1bv.enter_context(nc.sbuf_tensor(f"wv{i}", [128, 16384], BF16)) for i in range(2)]
            vg = 0
            for vb in range(2):
                def g_wv(g, vb=vb):
                    g.dma_start(wv_sb[vb % 2][:], wvL_d[vb]).then_inc(swv2[vb % 2].h, 16)
                G(g_wv)
                swv2[vb % 2].n += 16
                rec[("wv", vb)] = swv2[vb % 2].n

                for tt in range(8):
                    def t_v(t, vb=vb, tt=tt, vg=vg):
                        t.wait_ge(swv2[vb % 2].h, rec[("wv", vb)])
                        if vg >= 2:
                            t.wait_ge(ac.h, rec[("vev", vg - 2)])
                        half = xnA if tt < 4 else xnB
                        for dc in range(32):
                            t.matmul(ps_v[vg % 2][:],
                                     half[:, dc * 512 + (tt % 4) * 128:
                                          dc * 512 + (tt % 4) * 128 + 128],
                                     wv_sb[vb % 2][:, dc * 512:(dc + 1) * 512],
                                     start=(dc == 0), stop=False)
                        mm = t.matmul(ps_v[vg % 2][:], ones16[:],
                                      bqr16[0:1, 5120 + vb * 512: 5120 + (vb + 1) * 512],
                                      start=False, stop=True)
                        mm.then_inc(pe.h, 1)
                    T(t_v)
                    pe.n += 1
                    rec[("vmm", vg)] = pe.n

                    def a_v(s, vb=vb, tt=tt, vg=vg):
                        s.wait_ge(pe.h, rec[("vmm", vg)])
                        s.copy(v_sb[:, tt * 1024 + vb * 512: tt * 1024 + (vb + 1) * 512],
                               ps_v[vg % 2][:]).then_inc(ac.h, 1)
                    A(a_v)
                    ac.n += 1
                    rec[("vev", vg)] = ac.n
                    vg += 1

            p1bv.close()
            phase_barrier()

            oth512 = p1b.enter_context(nc.sbuf_tensor("oth512", [128, 16384], BF16))
            wvo_sb = [p1b.enter_context(nc.sbuf_tensor(f"wvo{i}", [128, 8192], BF16)) for i in range(2)]
            for vb in range(4):
                def g_wvo(g, vb=vb):
                    if vb >= 2:
                        g.wait_ge(pe.h, rec[("wvo_user", vb - 2)])
                    g.dma_start(wvo_sb[vb % 2][:], wvoL_d[vb]).then_inc(swvo2[vb % 2].h, 16)
                G(g_wvo)
                swvo2[vb % 2].n += 16
                rec[("wvo", vb)] = swvo2[vb % 2].n

                for oh in range(2):
                    def s_o512v(sy, vb=vb, oh=oh):
                        prev = rec.get(("o512v_user",))
                        if prev is not None:
                            sy.wait_ge(pe.h, prev)
                        src = xn_oth_h[:].rearrange("p (c t) -> p c t", t=1024)[
                            :, :, oh * 512: oh * 512 + 512]
                        sy.dma_start(oth512[:].rearrange("p (c t) -> p c t", t=512),
                                     src).then_inc(so5.h, 16)
                    Y(s_o512v)
                    so5.n += 16
                    rec[("vo512", vb, oh)] = so5.n

                    for t4 in range(4):
                        tt = 8 + oh * 4 + t4
                        def t_vo(t, vb=vb, oh=oh, t4=t4, vg=vg):
                            t.wait_ge(swvo2[vb % 2].h, rec[("wvo", vb)])
                            t.wait_ge(so5.h, rec[("vo512", vb, oh)])
                            if vg >= 2:
                                t.wait_ge(ac.h, rec[("vev", vg - 2)])
                            for dc in range(32):
                                t.matmul(ps_v[vg % 2][:, 0:256],
                                         oth512[:, dc * 512 + t4 * 128:
                                                dc * 512 + t4 * 128 + 128],
                                         wvo_sb[vb % 2][:, dc * 256:(dc + 1) * 256],
                                         start=(dc == 0), stop=False)
                            mm = t.matmul(ps_v[vg % 2][:, 0:256], ones16[:],
                                          bqr16[0:1, 5120 + vb * 256: 5120 + (vb + 1) * 256],
                                          start=False, stop=True)
                            mm.then_inc(pe.h, 1)
                        T(t_vo)
                        pe.n += 1
                        rec[("vmm", vg)] = pe.n
                        rec[("o512v_user",)] = pe.n
                        rec[("wvo_user", vb)] = pe.n

                        def a_vo(s, tt=tt, vb=vb, vg=vg):
                            s.wait_ge(pe.h, rec[("vmm", vg)])
                            s.copy(v_sb[:, tt * 1024 + vb * 256: tt * 1024 + (vb + 1) * 256],
                                   ps_v[vg % 2][:, 0:256]).then_inc(ac.h, 1)
                        A(a_vo)
                        ac.n += 1
                        rec[("vev", vg)] = ac.n
                        vg += 1

        phase_barrier()

        if stop_after not in ("p1",):
            p2 = contextlib.ExitStack()
            with p2:
                cosq = p2.enter_context(nc.sbuf_tensor("cosqS", [128, 512], F32))
                sinq = p2.enter_context(nc.sbuf_tensor("sinqS", [128, 512], F32))
                wq_sb = [p2.enter_context(nc.sbuf_tensor("wq0", [128, 16384], BF16))] * 2
                qst = [p2.enter_context(nc.sbuf_tensor("qst0", [128, 512], F32))] * 2
                qtm = [p2.enter_context(nc.sbuf_tensor("qtm0", [128, 512], BF16))] * 2
                q_fm = [p2.enter_context(nc.sbuf_tensor(f"qfm{i}", [128, 4096], BF16)) for i in range(2)]
                urq = p2.enter_context(nc.sbuf_tensor("urq", [128, 256], F32))
                uiq = p2.enter_context(nc.sbuf_tensor("uiq", [128, 256], F32))
                p_sb = p2.enter_context(nc.sbuf_tensor("p_sb", [128, 2048], BF16))
                pt_sb = [p2.enter_context(nc.sbuf_tensor(f"pt{i}", [128, 128], BF16)) for i in range(2)]
                av_st = [p2.enter_context(nc.sbuf_tensor("avst0", [128, 1024], BF16))] * 2
                astat = p2.enter_context(nc.sbuf_tensor("astat", [128, 16], F32))
                ps_s = p2.enter_context(nc.psum_tensor("ps_s", [128, 2048], F32))
                ps_q = p2.enter_context(nc.psum_tensor("ps_q", [128, 512], F32))
                trp = [p2.enter_context(nc.psum_tensor(f"trp{i}", [128, 128], BF16)) for i in range(2)]
                av_ps = p2.enter_context(nc.psum_tensor("av_ps", [128, 128], F32))

                def s_cosq(sy):
                    sy.dma_start(cosq[:], cosq_d[:]).then_inc(skc.h, 16)
                    sy.dma_start(sinq[:], sinq_d[:]).then_inc(skc.h, 16)
                Y(s_cosq)
                skc.n += 32
                rec["cosq"] = skc.n

                def ps_prev(head, t_):
                    if t_ in (0, 1):
                        return (head - 1, 7) if head >= 1 else None
                    if t_ in (2, 3):
                        return (head, t_ - 2)
                    if t_ == 4:
                        return (head, 3)
                    return (head, t_ - 1)

                def ps_base(t_):
                    return ((t_ % 2) * 1024, (t_ % 2) * 1024 + 512) if t_ <= 3 \
                        else (0, 1024)

                trp_i = 0
                qg = 0

                def emit_transpose(t, src_slice, gi):
                    if gi >= 2:
                        t.wait_ge(dv.h, rec[("trpev", gi - 2)])
                    t.transpose(trp[gi % 2][:], src_slice, idt16[:]).then_inc(pe.h, 1)

                for qb in range(8):
                    def g_wq(g, qb=qb):
                        if qb >= 1:
                            g.wait_ge(pe.h, rec[("wq_user", qb - 1)])
                        g.dma_start(wq_sb[qb % 2][:], wqL2_d[qb]).then_inc(swq.h, 16)
                    G(g_wq)
                    swq.n += 16
                    rec[("wq", qb)] = swq.n

                    for tt in range(8):
                        def t_q(t, qb=qb, tt=tt, qg=qg):
                            t.wait_ge(swq.h, rec[("wq", qb)])
                            if qg >= 2:
                                t.wait_ge(dv.h, rec[("qev", qg - 2)])
                            half = xnA if tt < 4 else xnB
                            for dc in range(32):
                                t.matmul(ps_q[:],
                                         half[:, dc * 512 + (tt % 4) * 128:
                                              dc * 512 + (tt % 4) * 128 + 128],
                                         wq_sb[qb % 2][:, dc * 512:(dc + 1) * 512],
                                         start=(dc == 0), stop=False)
                            mm = t.matmul(ps_q[:], ones16[:],
                                          bqr16[0:1, qb * 512:(qb + 1) * 512],
                                          start=False, stop=True)
                            mm.then_inc(pe.h, 1)
                        T(t_q)
                        pe.n += 1
                        rec[("qmm", qg)] = pe.n

                        def v_qev(v, qg=qg):
                            v.wait_ge(pe.h, rec[("qmm", qg)])
                            if qg >= 1:
                                v.wait_ge(dv.h, rec[("qrope", qg - 1)])
                            v.tensor_scalar_mul(qst[qg % 2][:], ps_q[:], SCL).then_inc(dv.h, 1)
                        V(v_qev)
                        dv.n += 1
                        rec[("qev", qg)] = dv.n

                        def v_qrope(v, tt=tt, qg=qg):
                            v.wait_ge(skc.h, rec["cosq"])
                            if qg >= 1:
                                v.wait_ge(pe.h, rec[("qtrdone", qg - 1)])
                            stg = qst[qg % 2][:].rearrange("p (h x) -> p h x", x=128)
                            r, i = stg[:, :, 0:64], stg[:, :, 64:128]
                            csq = cosq[:, tt * 64:(tt + 1) * 64]
                            snq = sinq[:, tt * 64:(tt + 1) * 64]
                            cs = bass.AP(csq.tensor, csq.offset,
                                         [list(csq.ap[0]), [0, 4], list(csq.ap[1])])
                            sn = bass.AP(snq.tensor, snq.offset,
                                         [list(snq.ap[0]), [0, 4], list(snq.ap[1])])
                            qt = qtm[qg % 2][:].rearrange("p (h x) -> p h x", x=128)
                            urv = urq[:].rearrange("p (h x) -> p h x", x=64)
                            uiv = uiq[:].rearrange("p (h x) -> p h x", x=64)
                            v.tensor_tensor(urv, r, cs, ALU.mult)
                            v.tensor_tensor(uiv, i, sn, ALU.mult)
                            v.tensor_tensor(qt[:, :, 0:64], urv, uiv, ALU.subtract)
                            v.tensor_tensor(urv, r, sn, ALU.mult)
                            v.tensor_tensor(uiv, i, cs, ALU.mult)
                            v.tensor_tensor(qt[:, :, 64:128], urv, uiv, ALU.add).then_inc(dv.h, 1)
                        V(v_qrope)
                        dv.n += 1
                        rec[("qrope", qg)] = dv.n

                        for h in range(4):
                            rec[("trpev", trp_i + h)] = dv.n + h + 1
                        def t_qtr(t, qg=qg, trp_i0=trp_i):
                            t.wait_ge(dv.h, rec[("qrope", qg)])
                            for h in range(4):
                                emit_transpose(t, qtm[qg % 2][:, h * 128:(h + 1) * 128],
                                               trp_i0 + h)
                        T(t_qtr)
                        for h in range(4):
                            pe.n += 1
                            rec[("trp", trp_i + h)] = pe.n
                        rec[("qtrdone", qg)] = pe.n

                        def v_qtr(v, qb=qb, tt=tt, trp_i0=trp_i):
                            for h in range(4):
                                gi = trp_i0 + h
                                v.wait_ge(pe.h, rec[("trp", gi)])
                                v.tensor_copy(q_fm[qb % 2][:, h * 1024 + tt * 128:
                                              h * 1024 + (tt + 1) * 128],
                                              trp[gi % 2][:]).then_inc(dv.h, 1)
                        V(v_qtr)
                        dv.n += 4
                        assert dv.n == rec[("trpev", trp_i + 3)]
                        trp_i += 4
                        qg += 1
                    rec[("qfm_ready", qb)] = dv.n

                    for h4 in range(4):
                        head = qb * 4 + h4
                        g = head // GRP
                        for t_ in range(9):
                            if t_ < 8:
                                def t_s2(t, qb=qb, h4=h4, g=g, t_=t_, head=head):
                                    if t_ == 0:
                                        t.wait_ge(dv.h, rec[("qfm_ready", qb)])
                                    pv = ps_prev(head, t_)
                                    if pv is not None:
                                        t.wait_ge(ac.h, rec[("sexp", pv[0], pv[1])])
                                    L = (t_ + 1) * 128
                                    bo, be = ps_base(t_)
                                    qsl = q_fm[qb % 2][:, h4 * 1024 + t_ * 128:
                                                       h4 * 1024 + (t_ + 1) * 128]
                                    last = None
                                    for kbase, pb in ((0, bo), (1024, be)):
                                        c0 = 0
                                        while c0 < L:
                                            c1 = min(L, (c0 // 512 + 1) * 512)
                                            last = t.matmul(ps_s[:, pb + c0:pb + c1], qsl,
                                                            k_sb[:, g * 2048 + kbase + c0:
                                                                 g * 2048 + kbase + c1],
                                                            start=(c0 % 512 == 0), stop=False)
                                            c0 = c1
                                        mk = tri16 if kbase == 0 else pat16
                                        last = t.matmul(ps_s[:, pb + t_ * 128:pb + L],
                                                        idt16[:], mk[:],
                                                        start=False, stop=True)
                                    last.then_inc(pe.h, 1)
                                T(t_s2)
                                pe.n += 1
                                rec[("smm", head, t_)] = pe.n

                            if t_ >= 1:
                                tp = t_ - 1
                                ntile = 2 * (tp + 1)
                                peek = dv.n + (1 if t_ < 8 else 0)
                                for j in range(ntile):
                                    rec[("ptev", trp_i + j)] = peek + j + 1
                                    rec[("trpev", trp_i + j)] = peek + j + 1
                                rec[("avev", head, tp)] = peek + ntile + 1

                                def t_av(t, head=head, g=g, tp=tp, trp_i0=trp_i):
                                    t.wait_ge(dv.h, rec[("p2", head, tp)])
                                    ntile = 2 * (tp + 1)
                                    bo, be = ps_base(tp)
                                    for j in range(ntile):
                                        own = j <= tp
                                        src_c = bo + j * 128 if own else be + (j - tp - 1) * 128
                                        emit_transpose(t, p_sb[:, src_c:src_c + 128],
                                                       trp_i0 + j)
                                        if j >= 1:
                                            jj = j - 1
                                            own2 = jj <= tp
                                            kt = jj if own2 else 8 + (jj - tp - 1)
                                            t.wait_ge(dv.h, rec[("ptev", trp_i0 + jj)])
                                            t.matmul(av_ps[:],
                                                     v_sb[:, kt * 1024 + g * 128:
                                                          kt * 1024 + (g + 1) * 128],
                                                     pt_sb[(trp_i0 + jj) % 2][:],
                                                     start=(jj == 0), stop=False)
                                    jj = ntile - 1
                                    kt = jj if jj <= tp else 8 + (jj - tp - 1)
                                    t.wait_ge(dv.h, rec[("ptev", trp_i0 + jj)])
                                    t.matmul(av_ps[:],
                                             v_sb[:, kt * 1024 + g * 128:
                                                  kt * 1024 + (g + 1) * 128],
                                             pt_sb[(trp_i0 + jj) % 2][:],
                                             start=False, stop=True).then_inc(pe.h, 1)
                                T(t_av)
                                for j in range(ntile):
                                    pe.n += 1
                                    rec[("trp", trp_i + j)] = pe.n
                                pe.n += 1
                                rec[("avmm", head, tp)] = pe.n
                                rec[("trdone", "att", head, tp)] = rec[("trp", trp_i + ntile - 1)]

                            if t_ < 8:
                                def v_sm(v, head=head, t_=t_):
                                    v.wait_ge(pe.h, rec[("smm", head, t_)])
                                    L = (t_ + 1) * 128
                                    bo, be = ps_base(t_)
                                    step = be - bo
                                    a0 = ps_s[:, bo:bo + 2 * step]
                                    vv = a0.rearrange("p (b c) -> p b c", b=2)[:, :, 0:L]
                                    ao = 8 * (t_ % 2)
                                    v.reduce_max(astat[:, ao + 2:ao + 3], vv,
                                                 axis=AX.XY, negate=True).then_inc(dv.h, 1)
                                V(v_sm)
                                dv.n += 1
                                rec[("negm", head, t_)] = dv.n

                            if t_ >= 1:
                                tp = t_ - 1
                                ntile = 2 * (tp + 1)
                                def v_pt(v, head=head, tp=tp, trp_i0=trp_i):
                                    ntile = 2 * (tp + 1)
                                    for j in range(ntile):
                                        gi = trp_i0 + j
                                        v.wait_ge(pe.h, rec[("trp", gi)])
                                        v.tensor_copy(pt_sb[gi % 2][:], trp[gi % 2][:]).then_inc(dv.h, 1)
                                    v.wait_ge(pe.h, rec[("avmm", head, tp)])
                                    if tp == 0 and head >= 1:
                                        v.wait_ge(sav.h, rec[("avst", head - 1)])
                                    v.tensor_copy(av_st[head % 2][:, tp * 128:(tp + 1) * 128],
                                                  av_ps[:]).then_inc(dv.h, 1)
                                V(v_pt)
                                dv.n += ntile + 1
                                assert dv.n == rec[("avev", head, tp)]
                                trp_i += ntile

                            if t_ < 8:
                                def a_exp(s, head=head, t_=t_):
                                    s.wait_ge(dv.h, rec[("negm", head, t_)])
                                    pv = ps_prev(head, t_)
                                    if pv is not None:
                                        s.wait_ge(dv.h, rec[("p2", pv[0], pv[1])])
                                        s.wait_ge(pe.h, rec[("trdone", "att", pv[0], pv[1])])
                                    L = (t_ + 1) * 128
                                    bo, be = ps_base(t_)
                                    step = be - bo
                                    vin = ps_s[:, bo:bo + 2 * step].rearrange(
                                        "p (b c) -> p b c", b=2)[:, :, 0:L]
                                    vout = p_sb[:, bo:bo + 2 * step].rearrange(
                                        "p (b c) -> p b c", b=2)[:, :, 0:L]
                                    ao = 8 * (t_ % 2)
                                    s.activation(vout, vin, AF.Exp,
                                                 bias=astat[:, ao + 2:ao + 3],
                                                 accum_out=astat[:, ao + 5:ao + 6]).then_inc(ac.h, 1)
                                A(a_exp)
                                ac.n += 1
                                rec[("sexp", head, t_)] = ac.n

                                def v_p2(v, head=head, t_=t_):
                                    n0 = dv.n
                                    v.wait_ge(ac.h, rec[("sexp", head, t_)])
                                    L = (t_ + 1) * 128
                                    ao = 8 * (t_ % 2)
                                    v.reciprocal(astat[:, ao + 6:ao + 7],
                                                 astat[:, ao + 5:ao + 6]).then_inc(dv.h, 1)
                                    v.wait_ge(dv.h, n0 + 1)
                                    bo, be = ps_base(t_)
                                    step = be - bo
                                    vio = p_sb[:, bo:bo + 2 * step].rearrange(
                                        "p (b c) -> p b c", b=2)[:, :, 0:L]
                                    v.tensor_scalar_mul(vio, vio,
                                                        astat[:, ao + 6:ao + 7]).then_inc(dv.h, 1)
                                V(v_p2)
                                dv.n += 2
                                rec[("p2", head, t_)] = dv.n

                        def s_av(sy, head=head):
                            sy.wait_ge(dv.h, rec[("avev", head, 7)])
                            sy.dma_start(av_h[:, head * 1024:(head + 1) * 1024],
                                         av_st[head % 2][:]).then_inc(sav.h, 16)
                        Y(s_av)
                        sav.n += 16
                        rec[("avst", head)] = sav.n

                    rec[("wq_user", qb)] = pe.n

                def s_xnb(sy):
                    sy.dma_start(xnA_h[:], xnA[:]).then_inc(sav.h, 16)
                    sy.dma_start(xnB_h[:], xnB[:]).then_inc(sav.h, 16)
                Y(s_xnb)
                sav.n += 32
                rec["xnbounce"] = sav.n

            phase_barrier()
            kv_scope.close()

        if stop_after not in ("p1", "attn"):
            p3 = contextlib.ExitStack()
            with p3:
                av_sb = p3.enter_context(nc.sbuf_tensor("av_sb", [128, 32768], BF16))
                wo_sb = [p3.enter_context(nc.sbuf_tensor(f"wo{i}", [128, 16384], BF16)) for i in range(2)]
                xo_sb = [p3.enter_context(nc.sbuf_tensor(f"xo{i}", [128, 512], F32)) for i in range(2)]
                xa_sb = [p3.enter_context(nc.sbuf_tensor(f"xas{i}", [128, 512], F32)) for i in range(2)]
                bo16 = p3.enter_context(nc.sbuf_tensor("bo16", [1, D], BF16))
                po_ps = [p3.enter_context(nc.psum_tensor(f"po{i}", [128, 512], F32)) for i in range(3)]

                def g_p3c(g):
                    g.dma_start(bo16[:], bo_row_d[:]).then_inc(sp3.h, 16)
                    g.dma_start(av_sb[:], av_h[:]).then_inc(sp3.h, 16)
                G(g_p3c)
                sp3.n += 32
                rec["p3c"] = sp3.n

                og = 0
                for jg in range(8):
                    def g_wo(g, jg=jg):
                        if jg >= 2:
                            g.wait_ge(pe.h, rec[("wo_user", jg - 2)])
                        g.dma_start(wo_sb[jg % 2][:], woL_d[jg]).then_inc(swo2[jg % 2].h, 16)
                    G(g_wo)
                    swo2[jg % 2].n += 16
                    rec[("wo", jg)] = swo2[jg % 2].n

                    for tt in range(8):
                        def s_xo(sy, jg=jg, tt=tt, og=og):
                            if og >= 2:
                                sy.wait_ge(dv.h, rec[("xaev", og - 2)])
                            sy.dma_start(xo_sb[og % 2][:],
                                         x[tt * 128:(tt + 1) * 128,
                                           jg * 512:(jg + 1) * 512]).then_inc(sxo2[og % 2].h, 16)
                        Y(s_xo)
                        sxo2[og % 2].n += 16
                        rec[("xo", og)] = sxo2[og % 2].n

                        def t_o(t, jg=jg, tt=tt, og=og):
                            t.wait_ge(swo2[jg % 2].h, rec[("wo", jg)])
                            t.wait_ge(sp3.h, rec["p3c"])
                            if og >= 3:
                                t.wait_ge(dv.h, rec[("xaev", og - 3)])
                            for dc in range(32):
                                t.matmul(po_ps[og % 3][:],
                                         av_sb[:, dc * 1024 + tt * 128:
                                               dc * 1024 + (tt + 1) * 128],
                                         wo_sb[jg % 2][:, dc * 512:(dc + 1) * 512],
                                         start=(dc == 0), stop=False)
                            mm = t.matmul(po_ps[og % 3][:], ones16[:],
                                          bo16[0:1, jg * 512:(jg + 1) * 512],
                                          start=False, stop=True)
                            mm.then_inc(pe.h, 1)
                        T(t_o)
                        pe.n += 1
                        rec[("omm", og)] = pe.n

                        def v_xa(v, jg=jg, tt=tt, og=og):
                            v.wait_ge(pe.h, rec[("omm", og)])
                            v.wait_ge(sxo2[og % 2].h, rec[("xo", og)])
                            if og >= 2:
                                v.wait_ge(sxa2[og % 2].h, rec[("xast", og - 2)])
                            v.tensor_add(xa_sb[og % 2][:], po_ps[og % 3][:],
                                         xo_sb[og % 2][:]).then_inc(dv.h, 1)
                        V(v_xa)
                        dv.n += 1
                        rec[("xaev", og)] = dv.n

                        def s_xa(sy, jg=jg, tt=tt, og=og):
                            sy.wait_ge(dv.h, rec[("xaev", og)])
                            sy.dma_start(xa_h[tt * 128:(tt + 1) * 128,
                                              jg * 512:(jg + 1) * 512],
                                         xa_sb[og % 2][:]).then_inc(sxa2[og % 2].h, 16)
                        Y(s_xa)
                        sxa2[og % 2].n += 16
                        rec[("xast", og)] = sxa2[og % 2].n
                        og += 1
                    rec[("wo_user", jg)] = pe.n

            phase_barrier()

        if stop_after not in ("p1", "attn", "outproj"):
            p4 = contextlib.ExitStack()
            with p4:
                b1c_sb = p4.enter_context(nc.sbuf_tensor("b1c_sb", [128, 96], F32))
                act_sb = p4.enter_context(nc.sbuf_tensor("act_sb", [128, 49152], BF16))
                acc_sb = p4.enter_context(nc.sbuf_tensor("acc_sb", [128, 2048], F32))
                xf_sb = [p4.enter_context(nc.sbuf_tensor(f"xf{i}", [128, 512], F32)) for i in range(2)]
                rs_st = [p4.enter_context(nc.sbuf_tensor(f"rsst{i}", [128, 512], F32)) for i in range(2)]
                ftmp = p4.enter_context(nc.sbuf_tensor("ftmp", [128, 512], F32))
                pf_ps = [p4.enter_context(nc.psum_tensor(f"pf{i}", [128, 512], F32)) for i in range(3)]
                p2_ps = [p4.enter_context(nc.psum_tensor(f"p2p{i}", [128, 512], F32)) for i in range(3)]

                def g_p4c(g):
                    g.dma_start(b1c_sb[:], b1cols_d[:]).then_inc(sgc.h, 16)
                G(g_p4c)
                sgc.n += 16
                rec["p4c"] = sgc.n

                AFS = [AF.Tanh, AF.Gelu, AF.Silu]
                w1g = 0
                f1g = 0
                f2g = 0
                for H in range(2):
                    p4a = contextlib.ExitStack()
                    xnh = p4a.enter_context(nc.sbuf_tensor(f"xnh{H}", [128, 16384], BF16))
                    w1_sb = [p4a.enter_context(nc.sbuf_tensor(f"w1b{H}_{i}", [128, D], BF16)) for i in range(2)]
                    def s_xnh(sy, H=H):
                        if H >= 1:
                            sy.wait_ge(pe.h, rec[("ffn1_done", H - 1)])
                        sy.dma_start(xnh[:], (xnA_h if H == 0 else xnB_h)[:]).then_inc(sxh.h, 16)
                    Y(s_xnh)
                    sxh.n += 16
                    rec[("xnh", H)] = sxh.n

                    for e in range(3):
                        for fc in range(32):
                            def g_w1(g, H=H, e=e, fc=fc, w1g=w1g):
                                if w1g >= 2:
                                    g.wait_ge(pe.h, rec[("f1mm", w1g - 2)])
                                g.dma_start(w1_sb[w1g % 2][:], w1L_d[e, fc]).then_inc(sw12[w1g % 2].h, 16)
                            G(g_w1)
                            sw12[w1g % 2].n += 16
                            rec[("w1", w1g)] = sw12[w1g % 2].n

                            def t_f1(t, H=H, e=e, fc=fc, w1g=w1g, f1g=f1g):
                                t.wait_ge(sw12[w1g % 2].h, rec[("w1", w1g)])
                                t.wait_ge(sxh.h, rec[("xnh", H)])
                                t.wait_ge(sgc.h, rec["p4c"])
                                if f1g >= 3:
                                    t.wait_ge(ac.h, rec[("f1ev", f1g - 3)])
                                for dc in range(32):
                                    mm = t.matmul(pf_ps[f1g % 3][:],
                                                  w1_sb[w1g % 2][:, dc * 128:(dc + 1) * 128],
                                                  xnh[:, dc * 512:(dc + 1) * 512],
                                                  start=(dc == 0), stop=(dc == 31))
                                mm.then_inc(pe.h, 1)
                            T(t_f1)
                            pe.n += 1
                            rec[("f1mm", w1g)] = pe.n

                            def a_f1(s, e=e, fc=fc, f1g=f1g, w1g=w1g):
                                s.wait_ge(pe.h, rec[("f1mm", w1g)])
                                s.activation(act_sb[:, e * 16384 + fc * 512:
                                                    e * 16384 + (fc + 1) * 512],
                                             pf_ps[f1g % 3][:], AFS[e],
                                             bias=b1c_sb[:, e * 32 + fc:e * 32 + fc + 1]
                                             ).then_inc(ac.h, 1)
                            A(a_f1)
                            ac.n += 1
                            rec[("f1ev", f1g)] = ac.n
                            f1g += 1
                            w1g += 1
                    rec[("ffn1_done", H)] = pe.n
                    phase_barrier()
                    p4a.close()
                    p4b = contextlib.ExitStack()
                    w2_sb = [p4b.enter_context(nc.sbuf_tensor(f"w2b{H}_{i}", [128, 16384], BF16)) for i in range(2)]
                    b2c16 = p4b.enter_context(nc.sbuf_tensor(f"b2c{H}", [1, D], BF16))

                    for jg in range(8):
                        for e in range(3):
                            w2i = (H * 8 + jg) * 3 + e
                            def g_w2(g, e=e, jg=jg, w2i=w2i, H=H):
                                if w2i >= 2 and ("f2_w2user", w2i - 2) in rec:
                                    g.wait_ge(pe.h, rec[("f2_w2user", w2i - 2)])
                                g.dma_start(w2_sb[w2i % 2][:], w2L_d[e, jg]).then_inc(sw22[w2i % 2].h, 16)
                                if w2i >= 1 and ("f2_w2user", w2i - 1) in rec:
                                    g.wait_ge(pe.h, rec[("f2_w2user", w2i - 1)])
                                g.dma_start(b2c16[:], b2_d[e:e + 1, :]).then_inc(sb2s.h, 16)
                            G(g_w2)
                            sw22[w2i % 2].n += 16
                            sb2s.n += 16
                            rec[("w2", w2i)] = sw22[w2i % 2].n
                            rec[("b2l", w2i)] = sb2s.n

                            for tt in range(4):
                                if e == 0:
                                    def s_xf(sy, H=H, jg=jg, tt=tt, f2g=f2g):
                                        if f2g >= 2:
                                            sy.wait_ge(dv.h, rec[("f2acc", f2g - 2)])
                                        sy.dma_start(
                                            xf_sb[f2g % 2][:],
                                            xa_h[H * 512 + tt * 128: H * 512 + (tt + 1) * 128,
                                                 jg * 512:(jg + 1) * 512]).then_inc(sxf2[tt % 2].h, 16)
                                    Y(s_xf)
                                    sxf2[tt % 2].n += 16
                                    rec[("xf", H, jg, tt)] = sxf2[tt % 2].n

                                def t_f2(t, H=H, e=e, jg=jg, tt=tt, w2i=w2i, f2g=f2g):
                                    t.wait_ge(sw22[w2i % 2].h, rec[("w2", w2i)])
                                    t.wait_ge(sb2s.h, rec[("b2l", w2i)])
                                    t.wait_ge(ac.h, rec[("f1ev", (H * 3 + e) * 32 + 31)])
                                    if f2g >= 3:
                                        t.wait_ge(dv.h, rec[("f2acc", f2g - 3)])
                                    for fc in range(32):
                                        t.matmul(p2_ps[f2g % 3][:],
                                                 act_sb[:, e * 16384 + fc * 512 + tt * 128:
                                                        e * 16384 + fc * 512 + tt * 128 + 128],
                                                 w2_sb[w2i % 2][:, fc * 512:(fc + 1) * 512],
                                                 start=(fc == 0), stop=False)
                                    mm = t.matmul(p2_ps[f2g % 3][:], ones16[:],
                                                  b2c16[0:1, jg * 512:(jg + 1) * 512],
                                                  start=False, stop=True)
                                    mm.then_inc(pe.h, 1)
                                T(t_f2)
                                pe.n += 1
                                rec[("f2mm", f2g)] = pe.n
                                rec[("f2_w2user", w2i)] = pe.n

                                def v_f2(v, H=H, e=e, jg=jg, tt=tt, f2g=f2g):
                                    v.wait_ge(pe.h, rec[("f2mm", f2g)])
                                    ccol = c_cols[:, 3 * (H * 4 + tt) + e:
                                                  3 * (H * 4 + tt) + e + 1]
                                    asl = acc_sb[:, tt * 512:(tt + 1) * 512]
                                    if e == 0:
                                        v.wait_ge(sxf2[tt % 2].h, rec[("xf", H, jg, tt)])
                                        v.scalar_tensor_tensor(
                                            asl, p2_ps[f2g % 3][:], ccol,
                                            xf_sb[f2g % 2][:],
                                            ALU.mult, ALU.add).then_inc(dv.h, 1)
                                    elif e == 1:
                                        v.scalar_tensor_tensor(
                                            asl, p2_ps[f2g % 3][:], ccol, asl,
                                            ALU.mult, ALU.add).then_inc(dv.h, 1)
                                    else:
                                        pv = rec.get(("yst", H, jg, tt - 2)) or \
                                             rec.get(("yst", H, jg - 1, tt + 2))
                                        if pv:
                                            v.wait_ge(sy2[tt % 2].h, pv)
                                        v.scalar_tensor_tensor(
                                            rs_st[tt % 2][:], p2_ps[f2g % 3][:], ccol,
                                            asl, ALU.mult, ALU.add).then_inc(dv.h, 1)
                                V(v_f2)
                                dv.n += 1
                                rec[("f2acc", f2g)] = dv.n

                                if e == 2:
                                    def s_y(sy, H=H, jg=jg, tt=tt, f2g=f2g):
                                        sy.wait_ge(dv.h, rec[("f2acc", f2g)])
                                        sy.dma_start(
                                            y[H * 512 + tt * 128: H * 512 + (tt + 1) * 128,
                                              jg * 512:(jg + 1) * 512],
                                            rs_st[tt % 2][:]).then_inc(sy2[tt % 2].h, 16)
                                    Y(s_y)
                                    sy2[tt % 2].n += 16
                                    rec[("yst", H, jg, tt)] = sy2[tt % 2].n
                                f2g += 1
                    phase_barrier()
                    p4b.close()

        def s_dump(sy):
            sy.wait_ge(pe.h, pe.n)
            sy.wait_ge(dv.h, dv.n)
            sy.wait_ge(ac.h, ac.n)
            sb_map = {"xnA": xnA, "xnB": xnB, "k_sb": k_sb, "v_sb": v_sb,
                      "rs_cols": rs_cols, "gate_sb": gate_sb, "c_cols": c_cols,
                      "misc": misc, "row_sc": row_sc}
            dr_map = {"xn_oth": xn_oth_h, "av": av_h, "xa": xa_h}
            for name, (dst, _) in dbg_outs.items():
                src = sb_map.get(name) or dr_map.get(name)
                sy.dma_start(dst[:], src[:]).then_inc(qo.h, 16)
                qo.n += 16
            for s_ in ALLSEMS:
                if s_.n:
                    sy.wait_ge(s_.h, s_.n)
        Y(s_dump)
        kv_scope.close()
    return nc, dbg_outs


def kernel(**inputs):
    in_maps, metas = _prep_inputs(inputs)
    nc, _ = build_program(stop_after="all")
    res = run_bass_kernel_spmd(nc, in_maps, core_ids=list(range(8)))
    out = np.zeros((B, S, D), np.float32)
    for c in range(8):
        yc = res.results[c]["y"]
        for t, g in enumerate(metas[c]["own_g"]):
            out[metas[c]["b"], g * 128:(g + 1) * 128] = yc[t * 128:(t + 1) * 128]
    return out


# version 07
# speedup vs baseline: 1.2131x; 1.2131x over previous
import contextlib
import math

import numpy as np
import ml_dtypes

import concourse.bass as bass
import concourse.mybir as mybir
from concourse.bass_utils import run_bass_kernel_spmd

F32 = mybir.dt.float32
BF16 = mybir.dt.bfloat16
AF = mybir.ActivationFunctionType
ALU = mybir.AluOpType
AX = mybir.AxisListType

B, S, D = 4, 2048, 4096
HQ, HKV, HD, GRP = 32, 8, 128, 4
NB = 16
OWN = 8
VIS = 614
QKV = 6144
SCL = 1.0 / math.sqrt(HD)
NEG = -1.0e20

BF = ml_dtypes.bfloat16


def _perm_qkv():
    ph = np.concatenate([np.arange(0, HD, 2), np.arange(1, HD, 2)])
    perm = np.arange(QKV)
    for h in range(HQ):
        perm[h * HD:(h + 1) * HD] = h * HD + ph
    for g in range(HKV):
        base = HQ * HD + g * HD
        perm[base:base + HD] = base + ph
    return perm


def _blocks(w, bw):
    d, n = w.shape[0], w.shape[1] // bw
    return np.ascontiguousarray(
        w.reshape(32, 128, n, bw).transpose(2, 1, 0, 3).reshape(n, 128, 32 * bw))


def _rope_tables(pos):
    inv = (1.0 / (np.float32(10000.0) **
                  (np.arange(0, HD, 2, dtype=np.float32) / np.float32(HD))))
    ang = pos.astype(np.float32)[None, :] * inv.astype(np.float32)[:, None]
    return np.cos(ang).astype(np.float32), np.sin(ang).astype(np.float32)


def _prep_inputs(inp):
    x = np.asarray(inp["L0_attn_x"], np.float32).reshape(B, S, D)
    perm = _perm_qkv()
    wqkvT = np.asarray(inp["L0_attn_w_qkv"], np.float32).T[:, perm]
    bqkv = np.asarray(inp["L0_attn_b_qkv"], np.float32)[perm]
    bqkv_cols = np.ascontiguousarray(bqkv.reshape(48, 128).T)
    bv_row = np.ascontiguousarray(
        np.asarray(inp["L0_attn_b_qkv"], np.float32)[HQ * HD + HKV * HD:][None, :])
    wqL2 = _blocks(wqkvT[:, :D], 512)
    wkL2 = _blocks(wqkvT[:, D:D + 1024], 512)
    wvL = _blocks(wqkvT[:, D + 1024:], 512)
    wvoL = _blocks(wqkvT[:, D + 1024:], 256)
    bqkv_row = np.ascontiguousarray(bqkv[None, :])
    woL = _blocks(np.asarray(inp["L0_attn_w_o"], np.float32).T, 512)
    bo_row = np.ascontiguousarray(np.asarray(inp["L0_attn_b_o"], np.float32)[None, :])
    w1 = np.asarray(inp["L0_ffn_w1"], np.float32)
    w2 = np.asarray(inp["L0_ffn_w2"], np.float32)
    w1L = np.stack([_blocks(w1[e], 128) for e in range(3)])
    w2L = np.stack([_blocks(w2[e], 512) for e in range(3)])
    b1cols = np.ascontiguousarray(np.asarray(inp["L0_ffn_b1"], np.float32)
                                  .reshape(3, 32, 128).transpose(2, 0, 1).reshape(128, 96))
    b2 = np.ascontiguousarray(np.asarray(inp["L0_ffn_b2"], np.float32))
    gw = np.asarray(inp["L0_ffn_gate_w"], np.float32)
    gw_cols = np.ascontiguousarray(gw[:D].reshape(32, 128, 3)
                                   .transpose(1, 0, 2).reshape(128, 96))
    glast = np.ascontiguousarray(gw[D][None, :])
    gb3 = np.ascontiguousarray(np.asarray(inp["L0_ffn_gate_b"], np.float32)[None, :])

    idt = np.eye(128, dtype=BF)
    ones16r = np.ones((1, 128), dtype=BF)
    ones32r = np.ones((1, 128), dtype=np.float32)
    qi, ki = np.arange(128)[:, None], np.arange(128)[None, :]
    tri = np.where(ki <= qi, np.float32(0.0), np.float32(NEG)).astype(BF)

    shared = {
        "wqL2": wqL2, "wkL2": wkL2, "wvL": wvL, "wvoL": wvoL, "woL": woL,
        "bqkv_cols": bqkv_cols, "bqkv_row": bqkv_row, "bo_row": bo_row,
        "w1L": w1L, "w2L": w2L, "b1cols": b1cols, "b2": b2,
        "gw_cols": gw_cols, "glast": glast, "gb3": gb3,
        "idt": idt, "ones16r": ones16r, "ones32r": ones32r, "tri": tri,
        "fconsts": np.tile(np.array([[1.0 / D, 1.0e-5, -0.5, SCL,
                                      0.0, 0.0, 0.0, 0.0]], np.float32), (128, 1)),
    }
    in_maps, metas = [], []
    for c in range(8):
        b, par = c // 2, c % 2
        own_g = [2 * t + par for t in range(OWN)]
        oth_g = [2 * t + 1 - par for t in range(OWN)]
        lay = own_g + oth_g
        rows = np.concatenate([np.arange(g * 128, (g + 1) * 128) for g in lay])
        pos = rows.astype(np.float32)
        cosk, sink = _rope_tables(pos)
        def _tm(tab, ntile):
            return np.ascontiguousarray(
                tab.T.reshape(ntile, 128, 64).transpose(1, 0, 2).reshape(128, ntile * 64))
        cosk_tm, sink_tm = _tm(cosk, 16), _tm(sink, 16)
        cosq_tm, sinq_tm = _tm(cosk[:, :1024], 8), _tm(sink[:, :1024], 8)
        vm = (pos < VIS).astype(np.float32)
        vt = np.zeros((128, 32), np.float32)
        for L in range(NB):
            vt[:, 2 * L] = vm[L * 128:(L + 1) * 128]
            vt[:, 2 * L + 1] = 1.0 - vm[L * 128:(L + 1) * 128]
        m = dict(shared)
        m.update({
            "x": np.ascontiguousarray(x[b][rows]),
            "t0row": np.ascontiguousarray(x[b, 0][None, :]),
            "t0chunk": np.ascontiguousarray(x[b, 0].reshape(32, 128).T),
            "cosq_tm": cosq_tm, "sinq_tm": sinq_tm,
            "cosk_tm": cosk_tm, "sink_tm": sink_tm,
            "pat": np.full((128, 128), np.float32(NEG) if par == 0 else np.float32(0.0), dtype=BF),
            "vt_mask": vt,
        })
        in_maps.append(m)
        metas.append({"b": b, "own_g": own_g})
    return in_maps, metas


class SemC:
    def __init__(self, ctx, nc, name):
        self.h = ctx.enter_context(nc.semaphore(name))
        self.n = 0


def build_program(stop_after="all", debug=(), serialize=False):
    nc = bass.Bass("TRN2", target_bir_lowering=False)
    dbg = set(debug)

    DT = {}
    def din(name, shape, dtype=F32):
        DT[name] = nc.dram_tensor(name, shape, dtype, kind="ExternalInput")
        return DT[name]

    x = din("x", [S, D])
    t0row_d = din("t0row", [1, D])
    t0chunk_d = din("t0chunk", [128, 32])
    wqL2_d = din("wqL2", [8, 128, 16384])
    wkL2_d = din("wkL2", [2, 128, 16384])
    wvL_d = din("wvL", [2, 128, 16384])
    wvoL_d = din("wvoL", [4, 128, 8192])
    woL_d = din("woL", [8, 128, 16384])
    bqkv_cols_d = din("bqkv_cols", [128, 48])
    bqkv_row_d = din("bqkv_row", [1, QKV])
    bo_row_d = din("bo_row", [1, D])
    w1L_d = din("w1L", [3, 32, 128, D])
    w2L_d = din("w2L", [3, 8, 128, 16384])
    b1cols_d = din("b1cols", [128, 96])
    b2_d = din("b2", [3, D])
    gw_cols_d = din("gw_cols", [128, 96])
    glast_d = din("glast", [1, 3])
    gb3_d = din("gb3", [1, 3])
    cosq_d = din("cosq_tm", [128, 512])
    sinq_d = din("sinq_tm", [128, 512])
    cosk_d = din("cosk_tm", [128, 1024])
    sink_d = din("sink_tm", [128, 1024])
    idt_d = din("idt", [128, 128], BF16)
    ones16_d = din("ones16r", [1, 128], BF16)
    ones32_d = din("ones32r", [1, 128], F32)
    tri_d = din("tri", [128, 128], BF16)
    pat_d = din("pat", [128, 128], BF16)
    vt_mask_d = din("vt_mask", [128, 32])
    fconsts_d = din("fconsts", [128, 8])

    xn_oth_h = nc.dram_tensor("xn_oth_h", [128, 32768], BF16, kind="Internal")
    xnA_h = nc.dram_tensor("xnA_h", [128, 16384], BF16, kind="Internal")
    xnB_h = nc.dram_tensor("xnB_h", [128, 16384], BF16, kind="Internal")
    av_h = nc.dram_tensor("av_h", [128, 32768], BF16, kind="Internal")
    xa_h = nc.dram_tensor("xa_h", [1024, D], F32, kind="Internal")
    y = nc.dram_tensor("y", [1024, D], F32, kind="ExternalOutput")

    dbg_outs = {}
    def mkdbg(name, shape, dtype=F32):
        if name in dbg:
            dbg_outs[name] = (nc.dram_tensor("dbg_" + name, shape, dtype,
                                             kind="ExternalOutput"), None)
    mkdbg("xnA", [128, 16384], BF16)
    mkdbg("xnB", [128, 16384], BF16)
    mkdbg("xn_oth", [128, 32768], BF16)
    mkdbg("k_sb", [128, 16384], BF16)
    mkdbg("v_sb", [128, 16384], BF16)
    mkdbg("rs_cols", [128, 16])
    mkdbg("gate_sb", [128, 24])
    mkdbg("c_cols", [128, 24])
    mkdbg("misc", [128, 32])
    mkdbg("row_sc", [1, 64])
    mkdbg("av", [128, 32768], BF16)
    mkdbg("xa", [1024, D])

    ctx = contextlib.ExitStack()
    with ctx:
        pe = SemC(ctx, nc, "pe")
        dv = SemC(ctx, nc, "dv")
        ac = SemC(ctx, nc, "ac")
        qx = SemC(ctx, nc, "qx")
        qw = SemC(ctx, nc, "qw")
        qo = SemC(ctx, nc, "qo")
        sgc = SemC(ctx, nc, "sgc")
        skc = SemC(ctx, nc, "skc")
        sxt2 = [SemC(ctx, nc, f"sxt{i}") for i in range(2)]
        soth2 = [SemC(ctx, nc, f"soth{i}") for i in range(2)]
        so5 = SemC(ctx, nc, "so5")
        swk = SemC(ctx, nc, "swk")
        swv2 = [SemC(ctx, nc, f"swv{i}") for i in range(2)]
        swvo2 = [SemC(ctx, nc, f"swvo{i}") for i in range(2)]
        swq = SemC(ctx, nc, "swq")
        sav = SemC(ctx, nc, "sav")
        sp3 = SemC(ctx, nc, "sp3")
        swo2 = [SemC(ctx, nc, f"swo{i}") for i in range(2)]
        sxo2 = [SemC(ctx, nc, f"sxo{i}") for i in range(2)]
        sxa2 = [SemC(ctx, nc, f"sxa{i}") for i in range(2)]
        sxh = SemC(ctx, nc, "sxh")
        sw12 = [SemC(ctx, nc, f"sw1{i}") for i in range(2)]
        sw22 = [SemC(ctx, nc, f"sw2{i}") for i in range(2)]
        sb2s = SemC(ctx, nc, "sb2s")
        sxf2 = [SemC(ctx, nc, f"sxf{i}") for i in range(2)]
        sy2 = [SemC(ctx, nc, f"sy{i}") for i in range(2)]
        ALLSEMS = ([pe, dv, ac, qx, qw, qo, sgc, skc, so5, swk, swq, sav, sp3,
                    sxh, sb2s] + sxt2 + soth2 + swv2 + swvo2 + swo2 + sxo2 +
                   sxa2 + sw12 + sw22 + sxf2 + sy2)

        idt16 = ctx.enter_context(nc.sbuf_tensor("idt16", [128, 128], BF16))
        tri16 = ctx.enter_context(nc.sbuf_tensor("tri16", [128, 128], BF16))
        pat16 = ctx.enter_context(nc.sbuf_tensor("pat16", [128, 128], BF16))
        ones16 = ctx.enter_context(nc.sbuf_tensor("ones16", [1, 128], BF16))
        ones32 = ctx.enter_context(nc.sbuf_tensor("ones32", [1, 128], F32))
        bqkv_cols = ctx.enter_context(nc.sbuf_tensor("bqkv_colsS", [128, 48], F32))
        sbq_cols = ctx.enter_context(nc.sbuf_tensor("sbq_cols", [128, 32], F32))
        vt_mask = ctx.enter_context(nc.sbuf_tensor("vt_maskS", [128, 32], F32))
        bqr16 = ctx.enter_context(nc.sbuf_tensor("bqr16", [1, QKV], BF16))
        rs_cols = ctx.enter_context(nc.sbuf_tensor("rs_colsS", [128, 16], F32))
        gate_sb = ctx.enter_context(nc.sbuf_tensor("gate_sbS", [128, 24], F32))
        c_cols = ctx.enter_context(nc.sbuf_tensor("c_colsS", [128, 24], F32))
        misc = ctx.enter_context(nc.sbuf_tensor("miscS", [128, 32], F32))


        glast_sb = ctx.enter_context(nc.sbuf_tensor("glast_sb", [1, 3], F32))
        gb_sb = ctx.enter_context(nc.sbuf_tensor("gb_sb", [1, 3], F32))
        fc_sb = ctx.enter_context(nc.sbuf_tensor("fc_sb", [128, 8], F32))
        for _i, _v in enumerate([1.0 / D, 1.0e-5, -0.5, SCL]):
            nc.const_aps.register(F32, _v, fc_sb[:, _i:_i + 1])

        blk = ctx.enter_context(nc.Block())
        T, V, A, G, Y = blk.tensor, blk.vector, blk.scalar, blk.gpsimd, blk.sync
        rec = {}

        def phase_barrier():
            tgt = [(s.h, s.n) for s in ALLSEMS if s.n > 0]
            for emit in (T, V, A, G, Y):
                def body(e, tgt=tuple(tgt)):
                    for h, n in tgt:
                        e.wait_ge(h, n)
                emit(body)

        def g_consts(g):
            for dst, src in [(idt16, idt_d), (tri16, tri_d), (pat16, pat_d),
                             (ones16, ones16_d), (ones32, ones32_d),
                             (bqkv_cols, bqkv_cols_d), (vt_mask, vt_mask_d),
                             (bqr16, bqkv_row_d)]:
                g.dma_start(dst[:], src[:]).then_inc(sgc.h, 16)
        G(g_consts)
        sgc.n += 128
        rec["consts"] = sgc.n

        def s_consts(sy):
            for dst, src in [(glast_sb, glast_d), (gb_sb, gb3_d),
                             (fc_sb, fconsts_d)]:
                sy.dma_start(dst[:], src[:]).then_inc(skc.h, 16)
        Y(s_consts)
        skc.n += 48
        rec["sconsts"] = skc.n

        def v_sbq(v):
            v.wait_ge(sgc.h, rec["consts"])
            v.tensor_scalar_mul(sbq_cols[:], bqkv_cols[:, 0:32], SCL).then_inc(dv.h, 1)
        V(v_sbq)
        dv.n += 1

        kv_scope = contextlib.ExitStack()
        xnA = kv_scope.enter_context(nc.sbuf_tensor("xnA", [128, 16384], BF16))
        xnB = kv_scope.enter_context(nc.sbuf_tensor("xnB", [128, 16384], BF16))

        p1a = contextlib.ExitStack()
        with p1a:
            xbuf = [p1a.enter_context(nc.sbuf_tensor(f"xbuf{i}", [128, D], F32)) for i in range(2)]
            xcb = p1a.enter_context(nc.sbuf_tensor("xcb", [128, D], F32))
            sqb = p1a.enter_context(nc.sbuf_tensor("sqb", [128, D], F32))
            xn16 = [p1a.enter_context(nc.sbuf_tensor(f"xn16_{i}", [128, D], BF16)) for i in range(2)]
            oth_st = [p1a.enter_context(nc.sbuf_tensor(f"oth_st{i}", [128, D], BF16)) for i in range(2)]
            stats = p1a.enter_context(nc.sbuf_tensor("stats", [128, 16], F32))
            t0r_sb = p1a.enter_context(nc.sbuf_tensor("t0r_sb", [1, D], F32))
            t0xc = p1a.enter_context(nc.sbuf_tensor("t0xc", [1, D], F32))
            t0c_sb = p1a.enter_context(nc.sbuf_tensor("t0c_sb", [128, 32], F32))
            gw16 = p1a.enter_context(nc.sbuf_tensor("gw16", [128, 96], BF16))
            gw32 = p1a.enter_context(nc.sbuf_tensor("gw32", [128, 96], F32))
            t0n = p1a.enter_context(nc.sbuf_tensor("t0n", [128, 32], F32))
            row_sc = p1a.enter_context(nc.sbuf_tensor("row_scS", [1, 64], F32))
            tr_ps = [p1a.enter_context(nc.psum_tensor(f"tr_ps{i}", [128, 128], BF16)) for i in range(2)]
            g_ps = p1a.enter_context(nc.psum_tensor("g_ps", [128, 3], F32))
            vt_ps = p1a.enter_context(nc.psum_tensor("vt_ps", [1, 2], F32))
            bc_ps = p1a.enter_context(nc.psum_tensor("bc_ps", [128, 8], F32))
            t0_ps = p1a.enter_context(nc.psum_tensor("t0_ps", [1, 4], F32))

            def s_t0r(sy):
                sy.dma_start(t0r_sb[:], t0row_d[:]).then_inc(skc.h, 16)
                sy.dma_start(t0c_sb[:], t0chunk_d[:]).then_inc(skc.h, 16)
            Y(s_t0r)
            skc.n += 32
            rec["t0r"] = skc.n

            def g_gw(g):
                g.dma_start(gw16[:], gw_cols_d[:]).then_inc(sgc.h, 16)
                g.dma_start(gw32[:], gw_cols_d[:]).then_inc(sgc.h, 16)
            G(g_gw)
            sgc.n += 32
            rec["gwload"] = sgc.n

            NS, NM, SSQ, RSTD, SAB = 0, 1, 2, 3, 4
            tr_g = 0

            for L in range(NB):
                if serialize:
                    phase_barrier()
                o = 8 * (L % 2)

                def s_load(sy, L=L):
                    if L >= 2:
                        sy.wait_ge(dv.h, rec[("negmean", L - 2)])
                        sy.wait_ge(ac.h, rec[("xbfree_a", L - 2)])
                    sy.dma_start(xbuf[L % 2][:], x[L * 128:(L + 1) * 128, :]).then_inc(sxt2[L % 2].h, 16)
                Y(s_load)
                sxt2[L % 2].n += 16
                rec[("xload", L)] = sxt2[L % 2].n

                def v_stats(v, L=L, o=o):
                    v.wait_ge(sxt2[L % 2].h, rec[("xload", L)])
                    v.reduce_sum(stats[:, o + NS:o + NS + 1], xbuf[L % 2][:],
                                 axis=AX.X, negate=True).then_inc(dv.h, 1)
                    v.wait_ge(dv.h, dv.n + 1)
                    v.tensor_scalar_mul(stats[:, o + NM:o + NM + 1],
                                        stats[:, o + NS:o + NS + 1],
                                        1.0 / D).then_inc(dv.h, 1)
                V(v_stats)
                dv.n += 2
                rec[("negmean", L)] = dv.n

                def a_norm(s, L=L, o=o):
                    s.wait_ge(dv.h, rec[("negmean", L)])
                    if L >= 1:
                        s.wait_ge(dv.h, rec[("v2done", L - 1)])
                    s.activation(xcb[:], xbuf[L % 2][:], AF.Identity,
                                 bias=stats[:, o + NM:o + NM + 1]).then_inc(ac.h, 1)
                    s.activation(sqb[:], xcb[:], AF.Square,
                                 accum_out=stats[:, o + SSQ:o + SSQ + 1])
                    s.activation(stats[:, o + RSTD:o + RSTD + 1],
                                 stats[:, o + SSQ:o + SSQ + 1],
                                 AF.Ln, scale=1.0 / D, bias=1.0e-5)
                    s.activation(stats[:, o + RSTD:o + RSTD + 1],
                                 stats[:, o + RSTD:o + RSTD + 1],
                                 AF.Exp, scale=-0.5).then_inc(ac.h, 1)
                A(a_norm)
                ac.n += 1
                rec[("xbfree_a", L)] = ac.n
                ac.n += 1
                rec[("anorm", L)] = ac.n

                def v2(v, L=L, o=o):
                    v.wait_ge(ac.h, rec[("anorm", L)])
                    if L >= 2:
                        v.wait_ge(pe.h, rec[("trdone", L - 2)])
                    v.reduce_sum(stats[:, o + SAB:o + SAB + 1], xcb[:], axis=AX.X,
                                 apply_absolute_value=True).then_inc(dv.h, 1)
                    v.tensor_scalar_mul(xn16[L % 2][:], xcb[:],
                                        stats[:, o + RSTD:o + RSTD + 1]).then_inc(dv.h, 1)
                    v.wait_ge(dv.h, dv.n + 1)
                    v.tensor_mul(rs_cols[:, L:L + 1], stats[:, o + SAB:o + SAB + 1],
                                 stats[:, o + RSTD:o + RSTD + 1]).then_inc(dv.h, 1)
                V(v2)
                dv.n += 3
                rec[("rs", L)] = rec[("v2done", L)] = dv.n

                for c in range(32):
                    rec[("evac_g", tr_g + c)] = ac.n + c + 1

                def t_blk(t, L=L, tr_g0=tr_g):
                    t.wait_ge(dv.h, rec[("rs", L)])
                    t.matmul(vt_ps[:], rs_cols[:, L:L + 1], vt_mask[:, 2 * L:2 * L + 2],
                             start=(L == 0), stop=(L == NB - 1)).then_inc(pe.h, 1)
                    for c in range(32):
                        gidx = tr_g0 + c
                        if gidx >= 2:
                            t.wait_ge(ac.h, rec[("evac_g", gidx - 2)])
                        t.transpose(tr_ps[gidx % 2][:],
                                    xn16[L % 2][:, c * 128:(c + 1) * 128],
                                    idt16[:]).then_inc(pe.h, 1)
                T(t_blk)
                pe.n += 1
                rec[("vt", L)] = pe.n
                for c in range(32):
                    pe.n += 1
                    rec[("tr_g", tr_g + c)] = pe.n
                rec[("trdone", L)] = pe.n

                def a_evac(s, L=L, tr_g0=tr_g):
                    for c in range(32):
                        gidx = tr_g0 + c
                        s.wait_ge(pe.h, rec[("tr_g", gidx)])
                        if L >= 10 and c == 0:
                            s.wait_ge(soth2[L % 2].h, rec[("othst", L - 2)])
                        if L < 8:
                            half = xnA if L < 4 else xnB
                            dst = half[:, c * 512 + (L % 4) * 128:
                                       c * 512 + (L % 4) * 128 + 128]
                        else:
                            dst = oth_st[L % 2][:, c * 128:(c + 1) * 128]
                        s.copy(dst, tr_ps[gidx % 2][:]).then_inc(ac.h, 1)
                A(a_evac)
                ac.n += 32
                assert ac.n == rec[("evac_g", tr_g + 31)]
                tr_g += 32
                rec[("evacdone", L)] = ac.n

                if L >= 8:
                    def s_oth(sy, L=L):
                        sy.wait_ge(ac.h, rec[("evacdone", L)])
                        dst = xn_oth_h[:].rearrange("p (c t) -> p c t", t=1024)[
                            :, :, (L - 8) * 128:(L - 8) * 128 + 128]
                        src = oth_st[L % 2][:].rearrange("p (c t) -> p c t", t=128)
                        sy.dma_start(dst, src).then_inc(soth2[L % 2].h, 16)
                    Y(s_oth)
                    soth2[L % 2].n += 16
                    rec[("othst", L)] = soth2[L % 2].n
                else:
                    def t_gate(t, L=L):
                        t.wait_ge(ac.h, rec[("evacdone", L)])
                        t.wait_ge(sgc.h, rec["gwload"])
                        if L >= 1:
                            t.wait_ge(dv.h, rec[("gev", L - 1)])
                        half = xnA if L < 4 else xnB
                        for c in range(32):
                            mm = t.matmul(
                                g_ps[:],
                                half[:, c * 512 + (L % 4) * 128:
                                     c * 512 + (L % 4) * 128 + 128],
                                gw16[:, 3 * c:3 * c + 3],
                                start=(c == 0), stop=(c == 31))
                        mm.then_inc(pe.h, 1)
                    T(t_gate)
                    pe.n += 1
                    rec[("gmm", L)] = pe.n

                    def v_gev(v, L=L):
                        v.wait_ge(pe.h, rec[("gmm", L)])
                        v.tensor_copy(gate_sb[:, 3 * L:3 * L + 3], g_ps[:]).then_inc(dv.h, 1)
                    V(v_gev)
                    dv.n += 1
                    rec[("gev", L)] = dv.n

            def v_ratio(v):
                n0, k = dv.n, 0
                def d(instr):
                    nonlocal k
                    k += 1
                    instr.then_inc(dv.h, 1)
                    v.wait_ge(dv.h, n0 + k)
                v.wait_ge(pe.h, rec[("vt", NB - 1)])
                d(v.tensor_copy(row_sc[0:1, 0:2], vt_ps[:]))
                d(v.tensor_scalar_mul(row_sc[0:1, 2:3], row_sc[0:1, 0:1], 1.0 / (VIS * D)))
                d(v.tensor_scalar_add(row_sc[0:1, 3:4], row_sc[0:1, 2:3], 1.0e-8))
                d(v.reciprocal(row_sc[0:1, 4:5], row_sc[0:1, 3:4]))
                d(v.tensor_scalar_mul(row_sc[0:1, 5:6], row_sc[0:1, 1:2],
                                      1.0 / ((S - VIS) * D)))
                d(v.tensor_mul(row_sc[0:1, 6:7], row_sc[0:1, 5:6], row_sc[0:1, 4:5]))
                v.wait_ge(skc.h, rec["sconsts"])
                d(v.tensor_scalar_mul(row_sc[0:1, 8:11], glast_sb[:], row_sc[0:1, 6:7]))
                d(v.tensor_add(row_sc[0:1, 8:11], row_sc[0:1, 8:11], gb_sb[:]))
                v.wait_ge(skc.h, rec["t0r"])
                d(v.reduce_sum(row_sc[0:1, 12:13], t0r_sb[:], axis=AX.X, negate=True))
                d(v.tensor_scalar_mul(row_sc[0:1, 13:14], row_sc[0:1, 12:13], 1.0 / D))
                d(v.tensor_scalar_add(t0xc[:], t0r_sb[:], row_sc[0:1, 13:14]))
            V(v_ratio)
            dv.n += 11
            rec["t0xc"] = dv.n

            def a_t0(s):
                s.wait_ge(dv.h, rec["t0xc"])
                s.activation(t0r_sb[:], t0xc[:], AF.Square, accum_out=row_sc[0:1, 15:16])
                s.activation(row_sc[0:1, 14:15], row_sc[0:1, 15:16],
                             AF.Ln, scale=1.0 / D, bias=1.0e-5)
                s.activation(row_sc[0:1, 14:15], row_sc[0:1, 14:15],
                             AF.Exp, scale=-0.5).then_inc(ac.h, 1)
            A(a_t0)
            ac.n += 1
            rec["t0rstd"] = ac.n

            def t_t0bc(t):
                t.wait_ge(ac.h, rec["t0rstd"])
                t.matmul(bc_ps[:, 0:2], ones32[:], row_sc[0:1, 13:15],
                         start=True, stop=True)
                t.matmul(bc_ps[:, 2:5], ones32[:], row_sc[0:1, 8:11],
                         start=False, stop=True).then_inc(pe.h, 1)
            T(t_t0bc)
            pe.n += 1
            rec["t0bc"] = pe.n

            def v_t0n(v):
                n0, k = dv.n, 0
                def d(instr):
                    nonlocal k
                    k += 1
                    instr.then_inc(dv.h, 1)
                    v.wait_ge(dv.h, n0 + k)
                v.wait_ge(pe.h, rec["t0bc"])
                d(v.tensor_copy(misc[:, 0:5], bc_ps[:, 0:5]))
                d(v.tensor_scalar_add(t0n[:], t0c_sb[:], misc[:, 0:1]))
                d(v.tensor_scalar_mul(t0n[:], t0n[:], misc[:, 1:2]))
            V(v_t0n)
            dv.n += 3
            rec["t0n"] = dv.n

            def t_t0g(t):
                t.wait_ge(dv.h, rec["t0n"])
                t.wait_ge(sgc.h, rec["gwload"])
                for c in range(32):
                    mm = t.matmul(t0_ps[0:1, 0:3], t0n[:, c:c + 1],
                                  gw32[:, 3 * c:3 * c + 3],
                                  start=(c == 0), stop=(c == 31))
                mm.then_inc(pe.h, 1)
            T(t_t0g)
            pe.n += 1
            rec["t0g"] = pe.n

            def v_t0fin(v):
                n0, k = dv.n, 0
                def d(instr):
                    nonlocal k
                    k += 1
                    instr.then_inc(dv.h, 1)
                    v.wait_ge(dv.h, n0 + k)
                v.wait_ge(pe.h, rec["t0g"])
                d(v.tensor_copy(row_sc[0:1, 16:19], t0_ps[0:1, 0:3]))
                d(v.tensor_add(row_sc[0:1, 20:23], row_sc[0:1, 16:19], row_sc[0:1, 8:11]))
                d(v.reduce_max(row_sc[0:1, 24:25], row_sc[0:1, 20:23],
                               axis=AX.X, negate=True))
            V(v_t0fin)
            dv.n += 3
            rec["t0lg"] = dv.n

            def a_t0e(s):
                s.wait_ge(dv.h, rec["t0lg"])
                s.activation(row_sc[0:1, 25:28], row_sc[0:1, 20:23], AF.Exp,
                             bias=row_sc[0:1, 24:25],
                             accum_out=row_sc[0:1, 28:29]).then_inc(ac.h, 1)
            A(a_t0e)
            ac.n += 1
            rec["t0e"] = ac.n

            def v_t0sel(v):
                n0, k = dv.n, 0
                def d(instr):
                    nonlocal k
                    k += 1
                    instr.then_inc(dv.h, 1)
                    v.wait_ge(dv.h, n0 + k)
                v.wait_ge(ac.h, rec["t0e"])
                d(v.reciprocal(row_sc[0:1, 29:30], row_sc[0:1, 28:29]))
                d(v.tensor_scalar_mul(row_sc[0:1, 30:33], row_sc[0:1, 25:28],
                                      row_sc[0:1, 29:30]))
                d(v.reduce_max(row_sc[0:1, 33:34], row_sc[0:1, 30:33], axis=AX.X))
                d(v.tensor_tensor(row_sc[0:1, 34:35], row_sc[0:1, 30:31],
                                  row_sc[0:1, 31:32], ALU.min))
                d(v.tensor_tensor(row_sc[0:1, 35:36], row_sc[0:1, 30:31],
                                  row_sc[0:1, 31:32], ALU.max))
                d(v.tensor_tensor(row_sc[0:1, 36:37], row_sc[0:1, 35:36],
                                  row_sc[0:1, 32:33], ALU.min))
                d(v.tensor_tensor(row_sc[0:1, 37:38], row_sc[0:1, 34:35],
                                  row_sc[0:1, 36:37], ALU.max))
                d(v.tensor_scalar(row_sc[0:1, 38:41], row_sc[0:1, 30:33],
                                  row_sc[0:1, 33:34], None, ALU.is_equal))
                d(v.tensor_scalar(row_sc[0:1, 41:44], row_sc[0:1, 30:33],
                                  row_sc[0:1, 37:38], None, ALU.is_equal))
            V(v_t0sel)
            dv.n += 9
            rec["t0sel"] = dv.n

            def t_ibc(t):
                t.wait_ge(dv.h, rec["t0sel"])
                t.matmul(bc_ps[:, 0:6], ones32[:], row_sc[0:1, 38:44],
                         start=True, stop=True).then_inc(pe.h, 1)
            T(t_ibc)
            pe.n += 1
            rec["ibc"] = pe.n

            def v_ibc(v):
                v.wait_ge(pe.h, rec["ibc"])
                v.tensor_copy(misc[:, 8:14], bc_ps[:, 0:6]).then_inc(dv.h, 1)
            V(v_ibc)
            dv.n += 1
            rec["ibcv"] = dv.n

            for tt in range(OWN):
                def v_ga(v, tt=tt):
                    n0 = dv.n
                    if tt == 0:
                        v.wait_ge(dv.h, rec["ibcv"])
                        v.wait_ge(dv.h, rec[("gev", OWN - 1)])
                    else:
                        v.wait_ge(ac.h, rec[("gexp", tt - 1)])
                    v.tensor_add(misc[:, 16:19], gate_sb[:, 3 * tt:3 * tt + 3],
                                 misc[:, 2:5]).then_inc(dv.h, 1)
                    v.wait_ge(dv.h, n0 + 1)
                    v.reduce_max(misc[:, 19:20], misc[:, 16:19],
                                 axis=AX.X, negate=True).then_inc(dv.h, 1)
                V(v_ga)
                dv.n += 2
                rec[("glg", tt)] = dv.n

                def a_gexp(s, tt=tt):
                    s.wait_ge(dv.h, rec[("glg", tt)])
                    s.activation(misc[:, 20:23], misc[:, 16:19], AF.Exp,
                                 bias=misc[:, 19:20],
                                 accum_out=misc[:, 23:24]).then_inc(ac.h, 1)
                A(a_gexp)
                ac.n += 1
                rec[("gexp", tt)] = ac.n

                def v_gtv(v, tt=tt):
                    n0, k = dv.n, 0
                    def d(instr):
                        nonlocal k
                        k += 1
                        instr.then_inc(dv.h, 1)
                        v.wait_ge(dv.h, n0 + k)
                    v.wait_ge(ac.h, rec[("gexp", tt)])
                    d(v.reciprocal(misc[:, 24:25], misc[:, 23:24]))
                    d(v.tensor_scalar_mul(misc[:, 25:28], misc[:, 20:23], misc[:, 24:25]))
                    d(v.reduce_max(misc[:, 28:29], misc[:, 25:28], axis=AX.X))
                    d(v.tensor_tensor(misc[:, 29:30], misc[:, 25:26], misc[:, 26:27], ALU.min))
                    d(v.tensor_tensor(misc[:, 30:31], misc[:, 25:26], misc[:, 26:27], ALU.max))
                    d(v.tensor_tensor(misc[:, 31:32], misc[:, 30:31], misc[:, 27:28], ALU.min))
                    d(v.tensor_tensor(misc[:, 29:30], misc[:, 29:30], misc[:, 31:32], ALU.max))
                    d(v.tensor_add(misc[:, 30:31], misc[:, 28:29], misc[:, 29:30]))
                    d(v.tensor_scalar_add(misc[:, 30:31], misc[:, 30:31], 1.0e-8))
                    d(v.reciprocal(misc[:, 31:32], misc[:, 30:31]))
                    d(v.tensor_mul(misc[:, 28:29], misc[:, 28:29], misc[:, 31:32]))
                    d(v.tensor_mul(misc[:, 29:30], misc[:, 29:30], misc[:, 31:32]))
                    for e in range(3):
                        d(v.tensor_mul(misc[:, 30:31], misc[:, 28:29], misc[:, 8 + e:9 + e]))
                        d(v.scalar_tensor_tensor(
                            c_cols[:, 3 * tt + e:3 * tt + e + 1],
                            misc[:, 29:30], misc[:, 11 + e:12 + e],
                            misc[:, 30:31], ALU.mult, ALU.add))
                V(v_gtv)
                dv.n += 18
                rec[("gtv", tt)] = dv.n

        phase_barrier()

        k_sb = kv_scope.enter_context(nc.sbuf_tensor("k_sb", [128, 16384], BF16))

        p1b = contextlib.ExitStack()
        with p1b:
            ps_v = [p1b.enter_context(nc.psum_tensor(f"ps_v{i}", [128, 512], F32)) for i in range(2)]

            p1bk = contextlib.ExitStack()
            oth512k = p1bk.enter_context(nc.sbuf_tensor("oth512k", [128, 16384], BF16))
            cosk = p1bk.enter_context(nc.sbuf_tensor("coskS", [128, 1024], F32))
            sink = p1bk.enter_context(nc.sbuf_tensor("sinkS", [128, 1024], F32))
            wk_sb = p1bk.enter_context(nc.sbuf_tensor("wk0", [128, 16384], BF16))
            kst = [p1bk.enter_context(nc.sbuf_tensor(f"kst{i}", [128, 512], F32)) for i in range(2)]
            ktm = [p1bk.enter_context(nc.sbuf_tensor(f"ktm{i}", [128, 512], BF16)) for i in range(2)]
            ur = p1bk.enter_context(nc.sbuf_tensor("ur", [128, 256], F32))
            ui = p1bk.enter_context(nc.sbuf_tensor("ui", [128, 256], F32))
            trk_ps = [p1bk.enter_context(nc.psum_tensor(f"trk{i}", [128, 128], BF16)) for i in range(2)]

            def hview(ap, w):
                return ap.rearrange("p (h x) -> p h x", x=w)

            def bcast4(ap2d):
                return bass.AP(ap2d.tensor, ap2d.offset,
                               [list(ap2d.ap[0]), [0, 4], list(ap2d.ap[1])])

            def s_cos(sy):
                sy.dma_start(cosk[:], cosk_d[:]).then_inc(skc.h, 16)
                sy.dma_start(sink[:], sink_d[:]).then_inc(skc.h, 16)
            Y(s_cos)
            skc.n += 32
            rec["cosk"] = skc.n

            kv_i = 0
            ktr_g = 0
            for kb in range(2):
                def g_wk(g, kb=kb):
                    if kb >= 1:
                        g.wait_ge(pe.h, rec[("kmm", kv_i - 1)])
                    g.dma_start(wk_sb[:], wkL2_d[kb]).then_inc(swk.h, 16)
                G(g_wk)
                swk.n += 16
                rec[("wk", kb)] = swk.n

                for tt in range(16):
                    if tt >= 8 and (tt - 8) % 4 == 0:
                        oh = (tt - 8) // 4
                        def s_o512(sy, oh=oh):
                            prev = rec.get(("o512_user",))
                            if prev is not None:
                                sy.wait_ge(pe.h, prev)
                            src = xn_oth_h[:].rearrange("p (c t) -> p c t", t=1024)[
                                :, :, oh * 512: oh * 512 + 512]
                            sy.dma_start(oth512k[:].rearrange("p (c t) -> p c t", t=512),
                                         src).then_inc(so5.h, 16)
                        Y(s_o512)
                        so5.n += 16
                        rec[("o512k", kb, oh)] = so5.n

                    def t_k(t, kb=kb, tt=tt, kv_i=kv_i):
                        t.wait_ge(swk.h, rec[("wk", kb)])
                        if tt >= 8:
                            t.wait_ge(so5.h, rec[("o512k", kb, (tt - 8) // 4)])
                        if kv_i >= 2:
                            t.wait_ge(ac.h, rec[("kev", kv_i - 2)])
                        for dc in range(32):
                            if tt < 8:
                                half = xnA if tt < 4 else xnB
                                xsrc = half[:, dc * 512 + (tt % 4) * 128:
                                            dc * 512 + (tt % 4) * 128 + 128]
                            else:
                                t4 = (tt - 8) % 4
                                xsrc = oth512k[:, dc * 512 + t4 * 128:
                                               dc * 512 + t4 * 128 + 128]
                            t.matmul(ps_v[kv_i % 2][:], xsrc,
                                     wk_sb[:, dc * 512:(dc + 1) * 512],
                                     start=(dc == 0), stop=False)
                        mm = t.matmul(ps_v[kv_i % 2][:], ones16[:],
                                      bqr16[0:1, D + kb * 512: D + (kb + 1) * 512],
                                      start=False, stop=True)
                        mm.then_inc(pe.h, 1)
                    T(t_k)
                    pe.n += 1
                    rec[("kmm", kv_i)] = pe.n
                    if tt >= 8:
                        rec[("o512_user",)] = pe.n

                    def a_k(s, kv_i=kv_i):
                        s.wait_ge(pe.h, rec[("kmm", kv_i)])
                        if kv_i >= 2:
                            s.wait_ge(dv.h, rec[("krope", kv_i - 2)])
                        s.activation(kst[kv_i % 2][:], ps_v[kv_i % 2][:],
                                     AF.Identity).then_inc(ac.h, 1)
                    A(a_k)
                    ac.n += 1
                    rec[("kev", kv_i)] = ac.n

                    def v_krope(v, tt=tt, kv_i=kv_i):
                        v.wait_ge(ac.h, rec[("kev", kv_i)])
                        v.wait_ge(skc.h, rec["cosk"])
                        if kv_i >= 2:
                            v.wait_ge(pe.h, rec[("ktrdone", kv_i - 2)])
                        stg = hview(kst[kv_i % 2][:], 128)
                        r, i = stg[:, :, 0:64], stg[:, :, 64:128]
                        cs = bcast4(cosk[:, tt * 64:(tt + 1) * 64])
                        sn = bcast4(sink[:, tt * 64:(tt + 1) * 64])
                        kt = hview(ktm[kv_i % 2][:], 128)
                        urv = hview(ur[:], 64)
                        uiv = hview(ui[:], 64)
                        v.tensor_tensor(urv, r, cs, ALU.mult)
                        v.tensor_tensor(uiv, i, sn, ALU.mult)
                        v.tensor_tensor(kt[:, :, 0:64], urv, uiv, ALU.subtract)
                        v.tensor_tensor(urv, r, sn, ALU.mult)
                        v.tensor_tensor(uiv, i, cs, ALU.mult)
                        v.tensor_tensor(kt[:, :, 64:128], urv, uiv, ALU.add).then_inc(dv.h, 1)
                    V(v_krope)
                    dv.n += 1
                    rec[("krope", kv_i)] = dv.n

                    for h in range(4):
                        rec[("ktrev_pre", ktr_g + h)] = None
                    def t_ktr(t, kb=kb, tt=tt, kv_i=kv_i, ktr_g0=ktr_g):
                        t.wait_ge(dv.h, rec[("krope", kv_i)])
                        for h in range(4):
                            gi = ktr_g0 + h
                            if gi >= 2:
                                t.wait_ge(ac.h, rec[("ktrev", gi - 2)])
                            t.transpose(trk_ps[gi % 2][:],
                                        ktm[kv_i % 2][:, h * 128:(h + 1) * 128],
                                        idt16[:]).then_inc(pe.h, 1)
                    for h in range(4):
                        rec[("ktrev", ktr_g + h)] = ac.n + h + 1
                    T(t_ktr)
                    for h in range(4):
                        pe.n += 1
                        rec[("ktr", ktr_g + h)] = pe.n
                    rec[("ktrdone", kv_i)] = pe.n

                    def a_ktr(s, kb=kb, tt=tt, ktr_g0=ktr_g):
                        for h in range(4):
                            gi = ktr_g0 + h
                            s.wait_ge(pe.h, rec[("ktr", gi)])
                            g128 = (4 * kb + h) * 2048
                            s.copy(k_sb[:, g128 + tt * 128: g128 + (tt + 1) * 128],
                                   trk_ps[gi % 2][:]).then_inc(ac.h, 1)
                    A(a_ktr)
                    ac.n += 4
                    assert ac.n == rec[("ktrev", ktr_g + 3)]
                    ktr_g += 4
                    kv_i += 1

            p1bk.close()
            phase_barrier()

            v_sb = kv_scope.enter_context(nc.sbuf_tensor("v_sb", [128, 16384], BF16))
            p1bv = contextlib.ExitStack()
            wv_sb = [p1bv.enter_context(nc.sbuf_tensor(f"wv{i}", [128, 16384], BF16)) for i in range(2)]
            vg = 0
            for vb in range(2):
                def g_wv(g, vb=vb):
                    g.dma_start(wv_sb[vb % 2][:], wvL_d[vb]).then_inc(swv2[vb % 2].h, 16)
                G(g_wv)
                swv2[vb % 2].n += 16
                rec[("wv", vb)] = swv2[vb % 2].n

                for tt in range(8):
                    def t_v(t, vb=vb, tt=tt, vg=vg):
                        t.wait_ge(swv2[vb % 2].h, rec[("wv", vb)])
                        if vg >= 2:
                            t.wait_ge(ac.h, rec[("vev", vg - 2)])
                        half = xnA if tt < 4 else xnB
                        for dc in range(32):
                            t.matmul(ps_v[vg % 2][:],
                                     half[:, dc * 512 + (tt % 4) * 128:
                                          dc * 512 + (tt % 4) * 128 + 128],
                                     wv_sb[vb % 2][:, dc * 512:(dc + 1) * 512],
                                     start=(dc == 0), stop=False)
                        mm = t.matmul(ps_v[vg % 2][:], ones16[:],
                                      bqr16[0:1, 5120 + vb * 512: 5120 + (vb + 1) * 512],
                                      start=False, stop=True)
                        mm.then_inc(pe.h, 1)
                    T(t_v)
                    pe.n += 1
                    rec[("vmm", vg)] = pe.n

                    def a_v(s, vb=vb, tt=tt, vg=vg):
                        s.wait_ge(pe.h, rec[("vmm", vg)])
                        s.copy(v_sb[:, tt * 1024 + vb * 512: tt * 1024 + (vb + 1) * 512],
                               ps_v[vg % 2][:]).then_inc(ac.h, 1)
                    A(a_v)
                    ac.n += 1
                    rec[("vev", vg)] = ac.n
                    vg += 1

            p1bv.close()
            phase_barrier()

            oth512 = p1b.enter_context(nc.sbuf_tensor("oth512", [128, 16384], BF16))
            wvo_sb = [p1b.enter_context(nc.sbuf_tensor(f"wvo{i}", [128, 8192], BF16)) for i in range(2)]
            for vb in range(4):
                def g_wvo(g, vb=vb):
                    if vb >= 2:
                        g.wait_ge(pe.h, rec[("wvo_user", vb - 2)])
                    g.dma_start(wvo_sb[vb % 2][:], wvoL_d[vb]).then_inc(swvo2[vb % 2].h, 16)
                G(g_wvo)
                swvo2[vb % 2].n += 16
                rec[("wvo", vb)] = swvo2[vb % 2].n

                for oh in range(2):
                    def s_o512v(sy, vb=vb, oh=oh):
                        prev = rec.get(("o512v_user",))
                        if prev is not None:
                            sy.wait_ge(pe.h, prev)
                        src = xn_oth_h[:].rearrange("p (c t) -> p c t", t=1024)[
                            :, :, oh * 512: oh * 512 + 512]
                        sy.dma_start(oth512[:].rearrange("p (c t) -> p c t", t=512),
                                     src).then_inc(so5.h, 16)
                    Y(s_o512v)
                    so5.n += 16
                    rec[("vo512", vb, oh)] = so5.n

                    for t4 in range(4):
                        tt = 8 + oh * 4 + t4
                        def t_vo(t, vb=vb, oh=oh, t4=t4, vg=vg):
                            t.wait_ge(swvo2[vb % 2].h, rec[("wvo", vb)])
                            t.wait_ge(so5.h, rec[("vo512", vb, oh)])
                            if vg >= 2:
                                t.wait_ge(ac.h, rec[("vev", vg - 2)])
                            for dc in range(32):
                                t.matmul(ps_v[vg % 2][:, 0:256],
                                         oth512[:, dc * 512 + t4 * 128:
                                                dc * 512 + t4 * 128 + 128],
                                         wvo_sb[vb % 2][:, dc * 256:(dc + 1) * 256],
                                         start=(dc == 0), stop=False)
                            mm = t.matmul(ps_v[vg % 2][:, 0:256], ones16[:],
                                          bqr16[0:1, 5120 + vb * 256: 5120 + (vb + 1) * 256],
                                          start=False, stop=True)
                            mm.then_inc(pe.h, 1)
                        T(t_vo)
                        pe.n += 1
                        rec[("vmm", vg)] = pe.n
                        rec[("o512v_user",)] = pe.n
                        rec[("wvo_user", vb)] = pe.n

                        def a_vo(s, tt=tt, vb=vb, vg=vg):
                            s.wait_ge(pe.h, rec[("vmm", vg)])
                            s.copy(v_sb[:, tt * 1024 + vb * 256: tt * 1024 + (vb + 1) * 256],
                                   ps_v[vg % 2][:, 0:256]).then_inc(ac.h, 1)
                        A(a_vo)
                        ac.n += 1
                        rec[("vev", vg)] = ac.n
                        vg += 1

        phase_barrier()

        if stop_after not in ("p1",):
            p2 = contextlib.ExitStack()
            with p2:
                cosq = p2.enter_context(nc.sbuf_tensor("cosqS", [128, 512], F32))
                sinq = p2.enter_context(nc.sbuf_tensor("sinqS", [128, 512], F32))
                wq_sb = [p2.enter_context(nc.sbuf_tensor("wq0", [128, 16384], BF16))] * 2
                qst = [p2.enter_context(nc.sbuf_tensor("qst0", [128, 512], F32))] * 2
                qtm = [p2.enter_context(nc.sbuf_tensor("qtm0", [128, 512], BF16))] * 2
                q_fm = [p2.enter_context(nc.sbuf_tensor(f"qfm{i}", [128, 4096], BF16)) for i in range(2)]
                urq = p2.enter_context(nc.sbuf_tensor("urq", [128, 256], F32))
                uiq = p2.enter_context(nc.sbuf_tensor("uiq", [128, 256], F32))
                p_sb = p2.enter_context(nc.sbuf_tensor("p_sb", [128, 2048], BF16))
                pt_sb = [p2.enter_context(nc.sbuf_tensor(f"pt{i}", [128, 512], BF16)) for i in range(2)]
                av_st = [p2.enter_context(nc.sbuf_tensor("avst0", [128, 1024], BF16))] * 2
                astat = p2.enter_context(nc.sbuf_tensor("astat", [128, 16], F32))
                ps_s = p2.enter_context(nc.psum_tensor("ps_s", [128, 2048], F32))
                ps_q = p2.enter_context(nc.psum_tensor("ps_q", [128, 512], F32))
                trp = [p2.enter_context(nc.psum_tensor(f"trp{i}", [128, 512], BF16)) for i in range(2)]
                av_ps = p2.enter_context(nc.psum_tensor("av_ps", [128, 128], F32))

                def s_cosq(sy):
                    sy.dma_start(cosq[:], cosq_d[:]).then_inc(skc.h, 16)
                    sy.dma_start(sinq[:], sinq_d[:]).then_inc(skc.h, 16)
                Y(s_cosq)
                skc.n += 32
                rec["cosq"] = skc.n

                def ps_prev(head, t_):
                    if t_ in (0, 1):
                        return (head - 1, 7) if head >= 1 else None
                    if t_ in (2, 3):
                        return (head, t_ - 2)
                    if t_ == 4:
                        return (head, 3)
                    return (head, t_ - 1)

                def ps_base(t_):
                    return ((t_ % 2) * 1024, (t_ % 2) * 1024 + 512) if t_ <= 3 \
                        else (0, 1024)

                trp_i = 0
                qg = 0

                def emit_transpose(t, src_slice, gi):
                    if gi >= 2:
                        t.wait_ge(dv.h, rec[("trpev", gi - 2)])
                    t.transpose(trp[gi % 2][:, 0:128], src_slice, idt16[:]).then_inc(pe.h, 1)

                for qb in range(8):
                    def g_wq(g, qb=qb):
                        if qb >= 1:
                            g.wait_ge(pe.h, rec[("wq_user", qb - 1)])
                        g.dma_start(wq_sb[qb % 2][:], wqL2_d[qb]).then_inc(swq.h, 16)
                    G(g_wq)
                    swq.n += 16
                    rec[("wq", qb)] = swq.n

                    for tt in range(8):
                        def t_q(t, qb=qb, tt=tt, qg=qg):
                            t.wait_ge(swq.h, rec[("wq", qb)])
                            if qg >= 2:
                                t.wait_ge(dv.h, rec[("qev", qg - 2)])
                            half = xnA if tt < 4 else xnB
                            for dc in range(32):
                                t.matmul(ps_q[:],
                                         half[:, dc * 512 + (tt % 4) * 128:
                                              dc * 512 + (tt % 4) * 128 + 128],
                                         wq_sb[qb % 2][:, dc * 512:(dc + 1) * 512],
                                         start=(dc == 0), stop=False)
                            mm = t.matmul(ps_q[:], ones16[:],
                                          bqr16[0:1, qb * 512:(qb + 1) * 512],
                                          start=False, stop=True)
                            mm.then_inc(pe.h, 1)
                        T(t_q)
                        pe.n += 1
                        rec[("qmm", qg)] = pe.n

                        def v_qev(v, qg=qg):
                            v.wait_ge(pe.h, rec[("qmm", qg)])
                            if qg >= 1:
                                v.wait_ge(dv.h, rec[("qrope", qg - 1)])
                            v.tensor_scalar_mul(qst[qg % 2][:], ps_q[:], SCL).then_inc(dv.h, 1)
                        V(v_qev)
                        dv.n += 1
                        rec[("qev", qg)] = dv.n

                        def v_qrope(v, tt=tt, qg=qg):
                            v.wait_ge(skc.h, rec["cosq"])
                            if qg >= 1:
                                v.wait_ge(pe.h, rec[("qtrdone", qg - 1)])
                            stg = qst[qg % 2][:].rearrange("p (h x) -> p h x", x=128)
                            r, i = stg[:, :, 0:64], stg[:, :, 64:128]
                            csq = cosq[:, tt * 64:(tt + 1) * 64]
                            snq = sinq[:, tt * 64:(tt + 1) * 64]
                            cs = bass.AP(csq.tensor, csq.offset,
                                         [list(csq.ap[0]), [0, 4], list(csq.ap[1])])
                            sn = bass.AP(snq.tensor, snq.offset,
                                         [list(snq.ap[0]), [0, 4], list(snq.ap[1])])
                            qt = qtm[qg % 2][:].rearrange("p (h x) -> p h x", x=128)
                            urv = urq[:].rearrange("p (h x) -> p h x", x=64)
                            uiv = uiq[:].rearrange("p (h x) -> p h x", x=64)
                            v.tensor_tensor(urv, r, cs, ALU.mult)
                            v.tensor_tensor(uiv, i, sn, ALU.mult)
                            v.tensor_tensor(qt[:, :, 0:64], urv, uiv, ALU.subtract)
                            v.tensor_tensor(urv, r, sn, ALU.mult)
                            v.tensor_tensor(uiv, i, cs, ALU.mult)
                            v.tensor_tensor(qt[:, :, 64:128], urv, uiv, ALU.add).then_inc(dv.h, 1)
                        V(v_qrope)
                        dv.n += 1
                        rec[("qrope", qg)] = dv.n

                        for h in range(4):
                            rec[("trpev", trp_i + h)] = dv.n + h + 1
                        def t_qtr(t, qg=qg, trp_i0=trp_i):
                            t.wait_ge(dv.h, rec[("qrope", qg)])
                            for h in range(4):
                                emit_transpose(t, qtm[qg % 2][:, h * 128:(h + 1) * 128],
                                               trp_i0 + h)
                        T(t_qtr)
                        for h in range(4):
                            pe.n += 1
                            rec[("trp", trp_i + h)] = pe.n
                        rec[("qtrdone", qg)] = pe.n

                        def v_qtr(v, qb=qb, tt=tt, trp_i0=trp_i):
                            for h in range(4):
                                gi = trp_i0 + h
                                v.wait_ge(pe.h, rec[("trp", gi)])
                                v.tensor_copy(q_fm[qb % 2][:, h * 1024 + tt * 128:
                                              h * 1024 + (tt + 1) * 128],
                                              trp[gi % 2][:, 0:128]).then_inc(dv.h, 1)
                        V(v_qtr)
                        dv.n += 4
                        assert dv.n == rec[("trpev", trp_i + 3)]
                        trp_i += 4
                        qg += 1
                    rec[("qfm_ready", qb)] = dv.n

                    for h4 in range(4):
                        head = qb * 4 + h4
                        g = head // GRP
                        for t_ in range(9):
                            if t_ < 8:
                                def t_s2(t, qb=qb, h4=h4, g=g, t_=t_, head=head):
                                    if t_ == 0:
                                        t.wait_ge(dv.h, rec[("qfm_ready", qb)])
                                    pv = ps_prev(head, t_)
                                    if pv is not None:
                                        t.wait_ge(ac.h, rec[("sexp", pv[0], pv[1])])
                                    L = (t_ + 1) * 128
                                    bo, be = ps_base(t_)
                                    qsl = q_fm[qb % 2][:, h4 * 1024 + t_ * 128:
                                                       h4 * 1024 + (t_ + 1) * 128]
                                    last = None
                                    for kbase, pb in ((0, bo), (1024, be)):
                                        c0 = 0
                                        while c0 < L:
                                            c1 = min(L, (c0 // 512 + 1) * 512)
                                            last = t.matmul(ps_s[:, pb + c0:pb + c1], qsl,
                                                            k_sb[:, g * 2048 + kbase + c0:
                                                                 g * 2048 + kbase + c1],
                                                            start=(c0 % 512 == 0), stop=False)
                                            c0 = c1
                                        mk = tri16 if kbase == 0 else pat16
                                        last = t.matmul(ps_s[:, pb + t_ * 128:pb + L],
                                                        idt16[:], mk[:],
                                                        start=False, stop=True)
                                    last.then_inc(pe.h, 1)
                                T(t_s2)
                                pe.n += 1
                                rec[("smm", head, t_)] = pe.n

                            if t_ >= 1:
                                tp = t_ - 1
                                ntile = 2 * (tp + 1)
                                ngrp = (ntile + 3) // 4
                                tgi0 = rec.get(("tgi",), 0)
                                _b = dv.n + (1 if t_ < 8 else 0)
                                for gg in range(ngrp):
                                    rec[("gptev", tgi0 + gg)] = _b + gg + 1
                                rec[("avev", head, tp)] = _b + ngrp + 1

                                def t_av(t, head=head, g=g, tp=tp, tgi0=tgi0):
                                    t.wait_ge(dv.h, rec[("p2", head, tp)])
                                    ntile = 2 * (tp + 1)
                                    ngrp = (ntile + 3) // 4
                                    bo, be = ps_base(tp)

                                    def grp_tr(gg):
                                        gi = tgi0 + gg
                                        if gi >= 2:
                                            t.wait_ge(dv.h, rec[("gptev", gi - 2)])
                                        lo, hi = gg * 4, min(ntile, gg * 4 + 4)
                                        for j in range(lo, hi):
                                            own = j <= tp
                                            src_c = bo + j * 128 if own else be + (j - tp - 1) * 128
                                            mm = t.transpose(
                                                trp[gi % 2][:, (j - lo) * 128:(j - lo) * 128 + 128],
                                                p_sb[:, src_c:src_c + 128], idt16[:])
                                        mm.then_inc(pe.h, 1)

                                    def grp_av(gg, last):
                                        gi = tgi0 + gg
                                        t.wait_ge(dv.h, rec[("gptev", gi)])
                                        lo, hi = gg * 4, min(ntile, gg * 4 + 4)
                                        for j in range(lo, hi):
                                            own = j <= tp
                                            kt = j if own else 8 + (j - tp - 1)
                                            mm = t.matmul(
                                                av_ps[:],
                                                v_sb[:, kt * 1024 + g * 128:
                                                     kt * 1024 + (g + 1) * 128],
                                                pt_sb[gi % 2][:, (j - lo) * 128:(j - lo) * 128 + 128],
                                                start=(j == 0),
                                                stop=(last and j == hi - 1))
                                        if last:
                                            mm.then_inc(pe.h, 1)

                                    grp_tr(0)
                                    for gg in range(1, ngrp):
                                        grp_tr(gg)
                                        grp_av(gg - 1, False)
                                    grp_av(ngrp - 1, True)
                                T(t_av)
                                for gg in range(ngrp):
                                    pe.n += 1
                                    rec[("gtr", tgi0 + gg)] = pe.n
                                pe.n += 1
                                rec[("avmm", head, tp)] = pe.n
                                rec[("trdone", "att", head, tp)] = rec[("gtr", tgi0 + ngrp - 1)]

                            if t_ < 8:
                                def v_sm(v, head=head, t_=t_):
                                    v.wait_ge(pe.h, rec[("smm", head, t_)])
                                    L = (t_ + 1) * 128
                                    bo, be = ps_base(t_)
                                    step = be - bo
                                    a0 = ps_s[:, bo:bo + 2 * step]
                                    vv = a0.rearrange("p (b c) -> p b c", b=2)[:, :, 0:L]
                                    ao = 8 * (t_ % 2)
                                    v.reduce_max(astat[:, ao + 2:ao + 3], vv,
                                                 axis=AX.XY, negate=True).then_inc(dv.h, 1)
                                V(v_sm)
                                dv.n += 1
                                rec[("negm", head, t_)] = dv.n

                            if t_ >= 1:
                                def v_pt(v, head=head, tp=tp, tgi0=tgi0):
                                    ntile = 2 * (tp + 1)
                                    ngrp = (ntile + 3) // 4
                                    for gg in range(ngrp):
                                        gi = tgi0 + gg
                                        w = (min(ntile, gg * 4 + 4) - gg * 4) * 128
                                        v.wait_ge(pe.h, rec[("gtr", gi)])
                                        v.tensor_copy(pt_sb[gi % 2][:, 0:w],
                                                      trp[gi % 2][:, 0:w]).then_inc(dv.h, 1)
                                    v.wait_ge(pe.h, rec[("avmm", head, tp)])
                                    if tp == 0 and head >= 1:
                                        v.wait_ge(sav.h, rec[("avst", head - 1)])
                                    v.tensor_copy(av_st[head % 2][:, tp * 128:(tp + 1) * 128],
                                                  av_ps[:]).then_inc(dv.h, 1)
                                V(v_pt)
                                dv.n += ngrp + 1
                                assert dv.n == rec[("avev", head, tp)]
                                rec[("tgi",)] = tgi0 + ngrp

                            if t_ < 8:
                                def a_exp(s, head=head, t_=t_):
                                    s.wait_ge(dv.h, rec[("negm", head, t_)])
                                    pv = ps_prev(head, t_)
                                    if pv is not None:
                                        s.wait_ge(dv.h, rec[("p2", pv[0], pv[1])])
                                        s.wait_ge(pe.h, rec[("trdone", "att", pv[0], pv[1])])
                                    L = (t_ + 1) * 128
                                    bo, be = ps_base(t_)
                                    step = be - bo
                                    vin = ps_s[:, bo:bo + 2 * step].rearrange(
                                        "p (b c) -> p b c", b=2)[:, :, 0:L]
                                    vout = p_sb[:, bo:bo + 2 * step].rearrange(
                                        "p (b c) -> p b c", b=2)[:, :, 0:L]
                                    ao = 8 * (t_ % 2)
                                    s.activation(vout, vin, AF.Exp,
                                                 bias=astat[:, ao + 2:ao + 3],
                                                 accum_out=astat[:, ao + 5:ao + 6]).then_inc(ac.h, 1)
                                A(a_exp)
                                ac.n += 1
                                rec[("sexp", head, t_)] = ac.n

                                def v_p2(v, head=head, t_=t_):
                                    n0 = dv.n
                                    v.wait_ge(ac.h, rec[("sexp", head, t_)])
                                    L = (t_ + 1) * 128
                                    ao = 8 * (t_ % 2)
                                    v.reciprocal(astat[:, ao + 6:ao + 7],
                                                 astat[:, ao + 5:ao + 6]).then_inc(dv.h, 1)
                                    v.wait_ge(dv.h, n0 + 1)
                                    bo, be = ps_base(t_)
                                    step = be - bo
                                    vio = p_sb[:, bo:bo + 2 * step].rearrange(
                                        "p (b c) -> p b c", b=2)[:, :, 0:L]
                                    v.tensor_scalar_mul(vio, vio,
                                                        astat[:, ao + 6:ao + 7]).then_inc(dv.h, 1)
                                V(v_p2)
                                dv.n += 2
                                rec[("p2", head, t_)] = dv.n

                        def s_av(sy, head=head):
                            sy.wait_ge(dv.h, rec[("avev", head, 7)])
                            sy.dma_start(av_h[:, head * 1024:(head + 1) * 1024],
                                         av_st[head % 2][:]).then_inc(sav.h, 16)
                        Y(s_av)
                        sav.n += 16
                        rec[("avst", head)] = sav.n

                    rec[("wq_user", qb)] = pe.n

                def s_xnb(sy):
                    sy.dma_start(xnA_h[:], xnA[:]).then_inc(sav.h, 16)
                    sy.dma_start(xnB_h[:], xnB[:]).then_inc(sav.h, 16)
                Y(s_xnb)
                sav.n += 32
                rec["xnbounce"] = sav.n

            phase_barrier()
            kv_scope.close()

        if stop_after not in ("p1", "attn"):
            p3 = contextlib.ExitStack()
            with p3:
                av_sb = p3.enter_context(nc.sbuf_tensor("av_sb", [128, 32768], BF16))
                wo_sb = [p3.enter_context(nc.sbuf_tensor(f"wo{i}", [128, 16384], BF16)) for i in range(2)]
                xo_sb = [p3.enter_context(nc.sbuf_tensor(f"xo{i}", [128, 512], F32)) for i in range(2)]
                xa_sb = [p3.enter_context(nc.sbuf_tensor(f"xas{i}", [128, 512], F32)) for i in range(2)]
                bo16 = p3.enter_context(nc.sbuf_tensor("bo16", [1, D], BF16))
                po_ps = [p3.enter_context(nc.psum_tensor(f"po{i}", [128, 512], F32)) for i in range(3)]

                def g_p3c(g):
                    g.dma_start(bo16[:], bo_row_d[:]).then_inc(sp3.h, 16)
                    g.dma_start(av_sb[:], av_h[:]).then_inc(sp3.h, 16)
                G(g_p3c)
                sp3.n += 32
                rec["p3c"] = sp3.n

                og = 0
                for jg in range(8):
                    def g_wo(g, jg=jg):
                        if jg >= 2:
                            g.wait_ge(pe.h, rec[("wo_user", jg - 2)])
                        g.dma_start(wo_sb[jg % 2][:], woL_d[jg]).then_inc(swo2[jg % 2].h, 16)
                    G(g_wo)
                    swo2[jg % 2].n += 16
                    rec[("wo", jg)] = swo2[jg % 2].n

                    for tt in range(8):
                        def s_xo(sy, jg=jg, tt=tt, og=og):
                            if og >= 2:
                                sy.wait_ge(dv.h, rec[("xaev", og - 2)])
                            sy.dma_start(xo_sb[og % 2][:],
                                         x[tt * 128:(tt + 1) * 128,
                                           jg * 512:(jg + 1) * 512]).then_inc(sxo2[og % 2].h, 16)
                        Y(s_xo)
                        sxo2[og % 2].n += 16
                        rec[("xo", og)] = sxo2[og % 2].n

                        def t_o(t, jg=jg, tt=tt, og=og):
                            t.wait_ge(swo2[jg % 2].h, rec[("wo", jg)])
                            t.wait_ge(sp3.h, rec["p3c"])
                            if og >= 3:
                                t.wait_ge(dv.h, rec[("xaev", og - 3)])
                            for dc in range(32):
                                t.matmul(po_ps[og % 3][:],
                                         av_sb[:, dc * 1024 + tt * 128:
                                               dc * 1024 + (tt + 1) * 128],
                                         wo_sb[jg % 2][:, dc * 512:(dc + 1) * 512],
                                         start=(dc == 0), stop=False)
                            mm = t.matmul(po_ps[og % 3][:], ones16[:],
                                          bo16[0:1, jg * 512:(jg + 1) * 512],
                                          start=False, stop=True)
                            mm.then_inc(pe.h, 1)
                        T(t_o)
                        pe.n += 1
                        rec[("omm", og)] = pe.n

                        def v_xa(v, jg=jg, tt=tt, og=og):
                            v.wait_ge(pe.h, rec[("omm", og)])
                            v.wait_ge(sxo2[og % 2].h, rec[("xo", og)])
                            if og >= 2:
                                v.wait_ge(sxa2[og % 2].h, rec[("xast", og - 2)])
                            v.tensor_add(xa_sb[og % 2][:], po_ps[og % 3][:],
                                         xo_sb[og % 2][:]).then_inc(dv.h, 1)
                        V(v_xa)
                        dv.n += 1
                        rec[("xaev", og)] = dv.n

                        def s_xa(sy, jg=jg, tt=tt, og=og):
                            sy.wait_ge(dv.h, rec[("xaev", og)])
                            sy.dma_start(xa_h[tt * 128:(tt + 1) * 128,
                                              jg * 512:(jg + 1) * 512],
                                         xa_sb[og % 2][:]).then_inc(sxa2[og % 2].h, 16)
                        Y(s_xa)
                        sxa2[og % 2].n += 16
                        rec[("xast", og)] = sxa2[og % 2].n
                        og += 1
                    rec[("wo_user", jg)] = pe.n

            phase_barrier()

        if stop_after not in ("p1", "attn", "outproj"):
            p4 = contextlib.ExitStack()
            with p4:
                b1c_sb = p4.enter_context(nc.sbuf_tensor("b1c_sb", [128, 96], F32))
                act_sb = p4.enter_context(nc.sbuf_tensor("act_sb", [128, 49152], BF16))
                acc_sb = p4.enter_context(nc.sbuf_tensor("acc_sb", [128, 2048], F32))
                xf_sb = [p4.enter_context(nc.sbuf_tensor(f"xf{i}", [128, 512], F32)) for i in range(2)]
                rs_st = [p4.enter_context(nc.sbuf_tensor(f"rsst{i}", [128, 512], F32)) for i in range(2)]
                ftmp = p4.enter_context(nc.sbuf_tensor("ftmp", [128, 512], F32))
                pf_ps = [p4.enter_context(nc.psum_tensor(f"pf{i}", [128, 512], F32)) for i in range(3)]
                p2_ps = [p4.enter_context(nc.psum_tensor(f"p2p{i}", [128, 512], F32)) for i in range(3)]

                def g_p4c(g):
                    g.dma_start(b1c_sb[:], b1cols_d[:]).then_inc(sgc.h, 16)
                G(g_p4c)
                sgc.n += 16
                rec["p4c"] = sgc.n

                AFS = [AF.Tanh, AF.Gelu, AF.Silu]
                w1g = 0
                f1g = 0
                f2g = 0
                for H in range(2):
                    p4a = contextlib.ExitStack()
                    xnh = p4a.enter_context(nc.sbuf_tensor(f"xnh{H}", [128, 16384], BF16))
                    w1_sb = [p4a.enter_context(nc.sbuf_tensor(f"w1b{H}_{i}", [128, D], BF16)) for i in range(2)]
                    def s_xnh(sy, H=H):
                        if H >= 1:
                            sy.wait_ge(pe.h, rec[("ffn1_done", H - 1)])
                        sy.dma_start(xnh[:], (xnA_h if H == 0 else xnB_h)[:]).then_inc(sxh.h, 16)
                    Y(s_xnh)
                    sxh.n += 16
                    rec[("xnh", H)] = sxh.n

                    for e in range(3):
                        for fc in range(32):
                            def g_w1(g, H=H, e=e, fc=fc, w1g=w1g):
                                if w1g >= 2:
                                    g.wait_ge(pe.h, rec[("f1mm", w1g - 2)])
                                g.dma_start(w1_sb[w1g % 2][:], w1L_d[e, fc]).then_inc(sw12[w1g % 2].h, 16)
                            G(g_w1)
                            sw12[w1g % 2].n += 16
                            rec[("w1", w1g)] = sw12[w1g % 2].n

                            def t_f1(t, H=H, e=e, fc=fc, w1g=w1g, f1g=f1g):
                                t.wait_ge(sw12[w1g % 2].h, rec[("w1", w1g)])
                                t.wait_ge(sxh.h, rec[("xnh", H)])
                                t.wait_ge(sgc.h, rec["p4c"])
                                if f1g >= 3:
                                    t.wait_ge(ac.h, rec[("f1ev", f1g - 3)])
                                for dc in range(32):
                                    mm = t.matmul(pf_ps[f1g % 3][:],
                                                  w1_sb[w1g % 2][:, dc * 128:(dc + 1) * 128],
                                                  xnh[:, dc * 512:(dc + 1) * 512],
                                                  start=(dc == 0), stop=(dc == 31))
                                mm.then_inc(pe.h, 1)
                            T(t_f1)
                            pe.n += 1
                            rec[("f1mm", w1g)] = pe.n

                            def a_f1(s, e=e, fc=fc, f1g=f1g, w1g=w1g):
                                s.wait_ge(pe.h, rec[("f1mm", w1g)])
                                s.activation(act_sb[:, e * 16384 + fc * 512:
                                                    e * 16384 + (fc + 1) * 512],
                                             pf_ps[f1g % 3][:], AFS[e],
                                             bias=b1c_sb[:, e * 32 + fc:e * 32 + fc + 1]
                                             ).then_inc(ac.h, 1)
                            A(a_f1)
                            ac.n += 1
                            rec[("f1ev", f1g)] = ac.n
                            f1g += 1
                            w1g += 1
                    rec[("ffn1_done", H)] = pe.n
                    phase_barrier()
                    p4a.close()
                    p4b = contextlib.ExitStack()
                    w2_sb = [p4b.enter_context(nc.sbuf_tensor(f"w2b{H}_{i}", [128, 16384], BF16)) for i in range(2)]
                    b2c16 = p4b.enter_context(nc.sbuf_tensor(f"b2c{H}", [1, D], BF16))

                    for jg in range(8):
                        for e in range(3):
                            w2i = (H * 8 + jg) * 3 + e
                            def g_w2(g, e=e, jg=jg, w2i=w2i, H=H):
                                if w2i >= 2 and ("f2_w2user", w2i - 2) in rec:
                                    g.wait_ge(pe.h, rec[("f2_w2user", w2i - 2)])
                                g.dma_start(w2_sb[w2i % 2][:], w2L_d[e, jg]).then_inc(sw22[w2i % 2].h, 16)
                                if w2i >= 1 and ("f2_w2user", w2i - 1) in rec:
                                    g.wait_ge(pe.h, rec[("f2_w2user", w2i - 1)])
                                g.dma_start(b2c16[:], b2_d[e:e + 1, :]).then_inc(sb2s.h, 16)
                            G(g_w2)
                            sw22[w2i % 2].n += 16
                            sb2s.n += 16
                            rec[("w2", w2i)] = sw22[w2i % 2].n
                            rec[("b2l", w2i)] = sb2s.n

                            for tt in range(4):
                                if e == 0:
                                    def s_xf(sy, H=H, jg=jg, tt=tt, f2g=f2g):
                                        if f2g >= 2:
                                            sy.wait_ge(dv.h, rec[("f2acc", f2g - 2)])
                                        sy.dma_start(
                                            xf_sb[f2g % 2][:],
                                            xa_h[H * 512 + tt * 128: H * 512 + (tt + 1) * 128,
                                                 jg * 512:(jg + 1) * 512]).then_inc(sxf2[tt % 2].h, 16)
                                    Y(s_xf)
                                    sxf2[tt % 2].n += 16
                                    rec[("xf", H, jg, tt)] = sxf2[tt % 2].n

                                def t_f2(t, H=H, e=e, jg=jg, tt=tt, w2i=w2i, f2g=f2g):
                                    t.wait_ge(sw22[w2i % 2].h, rec[("w2", w2i)])
                                    t.wait_ge(sb2s.h, rec[("b2l", w2i)])
                                    t.wait_ge(ac.h, rec[("f1ev", (H * 3 + e) * 32 + 31)])
                                    if f2g >= 3:
                                        t.wait_ge(dv.h, rec[("f2acc", f2g - 3)])
                                    for fc in range(32):
                                        t.matmul(p2_ps[f2g % 3][:],
                                                 act_sb[:, e * 16384 + fc * 512 + tt * 128:
                                                        e * 16384 + fc * 512 + tt * 128 + 128],
                                                 w2_sb[w2i % 2][:, fc * 512:(fc + 1) * 512],
                                                 start=(fc == 0), stop=False)
                                    mm = t.matmul(p2_ps[f2g % 3][:], ones16[:],
                                                  b2c16[0:1, jg * 512:(jg + 1) * 512],
                                                  start=False, stop=True)
                                    mm.then_inc(pe.h, 1)
                                T(t_f2)
                                pe.n += 1
                                rec[("f2mm", f2g)] = pe.n
                                rec[("f2_w2user", w2i)] = pe.n

                                def v_f2(v, H=H, e=e, jg=jg, tt=tt, f2g=f2g):
                                    v.wait_ge(pe.h, rec[("f2mm", f2g)])
                                    ccol = c_cols[:, 3 * (H * 4 + tt) + e:
                                                  3 * (H * 4 + tt) + e + 1]
                                    asl = acc_sb[:, tt * 512:(tt + 1) * 512]
                                    if e == 0:
                                        v.wait_ge(sxf2[tt % 2].h, rec[("xf", H, jg, tt)])
                                        v.scalar_tensor_tensor(
                                            asl, p2_ps[f2g % 3][:], ccol,
                                            xf_sb[f2g % 2][:],
                                            ALU.mult, ALU.add).then_inc(dv.h, 1)
                                    elif e == 1:
                                        v.scalar_tensor_tensor(
                                            asl, p2_ps[f2g % 3][:], ccol, asl,
                                            ALU.mult, ALU.add).then_inc(dv.h, 1)
                                    else:
                                        pv = rec.get(("yst", H, jg, tt - 2)) or \
                                             rec.get(("yst", H, jg - 1, tt + 2))
                                        if pv:
                                            v.wait_ge(sy2[tt % 2].h, pv)
                                        v.scalar_tensor_tensor(
                                            rs_st[tt % 2][:], p2_ps[f2g % 3][:], ccol,
                                            asl, ALU.mult, ALU.add).then_inc(dv.h, 1)
                                V(v_f2)
                                dv.n += 1
                                rec[("f2acc", f2g)] = dv.n

                                if e == 2:
                                    def s_y(sy, H=H, jg=jg, tt=tt, f2g=f2g):
                                        sy.wait_ge(dv.h, rec[("f2acc", f2g)])
                                        sy.dma_start(
                                            y[H * 512 + tt * 128: H * 512 + (tt + 1) * 128,
                                              jg * 512:(jg + 1) * 512],
                                            rs_st[tt % 2][:]).then_inc(sy2[tt % 2].h, 16)
                                    Y(s_y)
                                    sy2[tt % 2].n += 16
                                    rec[("yst", H, jg, tt)] = sy2[tt % 2].n
                                f2g += 1
                    phase_barrier()
                    p4b.close()

        def s_dump(sy):
            sy.wait_ge(pe.h, pe.n)
            sy.wait_ge(dv.h, dv.n)
            sy.wait_ge(ac.h, ac.n)
            sb_map = {"xnA": xnA, "xnB": xnB, "k_sb": k_sb, "v_sb": v_sb,
                      "rs_cols": rs_cols, "gate_sb": gate_sb, "c_cols": c_cols,
                      "misc": misc}
            dr_map = {"xn_oth": xn_oth_h, "av": av_h, "xa": xa_h}
            for name, (dst, _) in dbg_outs.items():
                src = sb_map.get(name) or dr_map.get(name)
                sy.dma_start(dst[:], src[:]).then_inc(qo.h, 16)
                qo.n += 16
            for s_ in ALLSEMS:
                if s_.n:
                    sy.wait_ge(s_.h, s_.n)
        Y(s_dump)
        kv_scope.close()
    return nc, dbg_outs


def kernel(**inputs):
    in_maps, metas = _prep_inputs(inputs)
    nc, _ = build_program(stop_after="all")
    res = run_bass_kernel_spmd(nc, in_maps, core_ids=list(range(8)))
    out = np.zeros((B, S, D), np.float32)
    for c in range(8):
        yc = res.results[c]["y"]
        for t, g in enumerate(metas[c]["own_g"]):
            out[metas[c]["b"], g * 128:(g + 1) * 128] = yc[t * 128:(t + 1) * 128]
    return out


# version 08 (final)
# speedup vs baseline: 1.2621x; 1.0404x over previous
import contextlib
import math

import numpy as np
import ml_dtypes

import concourse.bass as bass
import concourse.mybir as mybir
from concourse.bass_utils import run_bass_kernel_spmd

F32 = mybir.dt.float32
BF16 = mybir.dt.bfloat16
AF = mybir.ActivationFunctionType
ALU = mybir.AluOpType
AX = mybir.AxisListType

B, S, D = 4, 2048, 4096
HQ, HKV, HD, GRP = 32, 8, 128, 4
NB = 16
OWN = 8
VIS = 614
QKV = 6144
SCL = 1.0 / math.sqrt(HD)
NEG = -1.0e20

BF = ml_dtypes.bfloat16


def _perm_qkv():
    ph = np.concatenate([np.arange(0, HD, 2), np.arange(1, HD, 2)])
    perm = np.arange(QKV)
    for h in range(HQ):
        perm[h * HD:(h + 1) * HD] = h * HD + ph
    for g in range(HKV):
        base = HQ * HD + g * HD
        perm[base:base + HD] = base + ph
    return perm


def _blocks(w, bw):
    d, n = w.shape[0], w.shape[1] // bw
    return np.ascontiguousarray(
        w.reshape(32, 128, n, bw).transpose(2, 1, 0, 3).reshape(n, 128, 32 * bw))


def _rope_tables(pos):
    inv = (1.0 / (np.float32(10000.0) **
                  (np.arange(0, HD, 2, dtype=np.float32) / np.float32(HD))))
    ang = pos.astype(np.float32)[None, :] * inv.astype(np.float32)[:, None]
    return np.cos(ang).astype(np.float32), np.sin(ang).astype(np.float32)


def _prep_inputs(inp):
    x = np.asarray(inp["L0_attn_x"], np.float32).reshape(B, S, D)
    perm = _perm_qkv()
    wqkvT = np.asarray(inp["L0_attn_w_qkv"], np.float32).T[:, perm]
    bqkv = np.asarray(inp["L0_attn_b_qkv"], np.float32)[perm]
    bqkv_cols = np.ascontiguousarray(bqkv.reshape(48, 128).T)
    bv_row = np.ascontiguousarray(
        np.asarray(inp["L0_attn_b_qkv"], np.float32)[HQ * HD + HKV * HD:][None, :])
    wqL2 = _blocks(wqkvT[:, :D], 512)
    wkL2 = _blocks(wqkvT[:, D:D + 1024], 512)
    wvL = _blocks(wqkvT[:, D + 1024:], 512)
    wvoL = _blocks(wqkvT[:, D + 1024:], 256)
    bqkv_row = np.ascontiguousarray(bqkv[None, :])
    woL = _blocks(np.asarray(inp["L0_attn_w_o"], np.float32).T, 512)
    bo_row = np.ascontiguousarray(np.asarray(inp["L0_attn_b_o"], np.float32)[None, :])
    w1 = np.asarray(inp["L0_ffn_w1"], np.float32)
    w2 = np.asarray(inp["L0_ffn_w2"], np.float32)
    w1L = np.stack([_blocks(w1[e], 128) for e in range(3)])
    w2L = np.stack([_blocks(w2[e], 512) for e in range(3)])
    b1cols = np.ascontiguousarray(np.asarray(inp["L0_ffn_b1"], np.float32)
                                  .reshape(3, 32, 128).transpose(2, 0, 1).reshape(128, 96))
    b2 = np.ascontiguousarray(np.asarray(inp["L0_ffn_b2"], np.float32))
    gw = np.asarray(inp["L0_ffn_gate_w"], np.float32)
    gw_cols = np.ascontiguousarray(gw[:D].reshape(32, 128, 3)
                                   .transpose(1, 0, 2).reshape(128, 96))
    glast = np.ascontiguousarray(gw[D][None, :])
    gb3 = np.ascontiguousarray(np.asarray(inp["L0_ffn_gate_b"], np.float32)[None, :])

    idt = np.eye(128, dtype=BF)
    ones16r = np.ones((1, 128), dtype=BF)
    ones32r = np.ones((1, 128), dtype=np.float32)
    qi, ki = np.arange(128)[:, None], np.arange(128)[None, :]
    tri = np.where(ki <= qi, np.float32(0.0), np.float32(NEG)).astype(BF)

    shared = {
        "wqL2": wqL2, "wkL2": wkL2, "wvL": wvL, "wvoL": wvoL, "woL": woL,
        "bqkv_cols": bqkv_cols, "bqkv_row": bqkv_row, "bo_row": bo_row,
        "w1L": w1L, "w2L": w2L, "b1cols": b1cols, "b2": b2,
        "gw_cols": gw_cols, "glast": glast, "gb3": gb3,
        "idt": idt, "ones16r": ones16r, "ones32r": ones32r, "tri": tri,
        "fconsts": np.tile(np.array([[1.0 / D, 1.0e-5, -0.5, SCL,
                                      0.0, 0.0, 0.0, 0.0]], np.float32), (128, 1)),
    }
    in_maps, metas = [], []
    for c in range(8):
        b, par = c // 2, c % 2
        own_g = [2 * t + par for t in range(OWN)]
        oth_g = [2 * t + 1 - par for t in range(OWN)]
        lay = own_g + oth_g
        rows = np.concatenate([np.arange(g * 128, (g + 1) * 128) for g in lay])
        pos = rows.astype(np.float32)
        cosk, sink = _rope_tables(pos)
        def _tm(tab, ntile):
            return np.ascontiguousarray(
                tab.T.reshape(ntile, 128, 64).transpose(1, 0, 2).reshape(128, ntile * 64))
        cosk_tm, sink_tm = _tm(cosk, 16), _tm(sink, 16)
        cosq_tm, sinq_tm = _tm(cosk[:, :1024], 8), _tm(sink[:, :1024], 8)
        vm = (pos < VIS).astype(np.float32)
        vt = np.zeros((128, 32), np.float32)
        for L in range(NB):
            vt[:, 2 * L] = vm[L * 128:(L + 1) * 128]
            vt[:, 2 * L + 1] = 1.0 - vm[L * 128:(L + 1) * 128]
        m = dict(shared)
        m.update({
            "x": np.ascontiguousarray(x[b][rows]),
            "t0row": np.ascontiguousarray(x[b, 0][None, :]),
            "t0chunk": np.ascontiguousarray(x[b, 0].reshape(32, 128).T),
            "cosq_tm": cosq_tm, "sinq_tm": sinq_tm,
            "cosk_tm": cosk_tm, "sink_tm": sink_tm,
            "pat": np.full((128, 128), np.float32(NEG) if par == 0 else np.float32(0.0), dtype=BF),
            "vt_mask": vt,
        })
        in_maps.append(m)
        metas.append({"b": b, "own_g": own_g})
    return in_maps, metas


class SemC:
    def __init__(self, ctx, nc, name):
        self.h = ctx.enter_context(nc.semaphore(name))
        self.n = 0


def build_program(stop_after="all", debug=(), serialize=False):
    nc = bass.Bass("TRN2", target_bir_lowering=False)
    dbg = set(debug)

    DT = {}
    def din(name, shape, dtype=F32):
        DT[name] = nc.dram_tensor(name, shape, dtype, kind="ExternalInput")
        return DT[name]

    x = din("x", [S, D])
    t0row_d = din("t0row", [1, D])
    t0chunk_d = din("t0chunk", [128, 32])
    wqL2_d = din("wqL2", [8, 128, 16384])
    wkL2_d = din("wkL2", [2, 128, 16384])
    wvL_d = din("wvL", [2, 128, 16384])
    wvoL_d = din("wvoL", [4, 128, 8192])
    woL_d = din("woL", [8, 128, 16384])
    bqkv_cols_d = din("bqkv_cols", [128, 48])
    bqkv_row_d = din("bqkv_row", [1, QKV])
    bo_row_d = din("bo_row", [1, D])
    w1L_d = din("w1L", [3, 32, 128, D])
    w2L_d = din("w2L", [3, 8, 128, 16384])
    b1cols_d = din("b1cols", [128, 96])
    b2_d = din("b2", [3, D])
    gw_cols_d = din("gw_cols", [128, 96])
    glast_d = din("glast", [1, 3])
    gb3_d = din("gb3", [1, 3])
    cosq_d = din("cosq_tm", [128, 512])
    sinq_d = din("sinq_tm", [128, 512])
    cosk_d = din("cosk_tm", [128, 1024])
    sink_d = din("sink_tm", [128, 1024])
    idt_d = din("idt", [128, 128], BF16)
    ones16_d = din("ones16r", [1, 128], BF16)
    ones32_d = din("ones32r", [1, 128], F32)
    tri_d = din("tri", [128, 128], BF16)
    pat_d = din("pat", [128, 128], BF16)
    vt_mask_d = din("vt_mask", [128, 32])
    fconsts_d = din("fconsts", [128, 8])

    xn_oth_h = nc.dram_tensor("xn_oth_h", [128, 32768], BF16, kind="Internal")
    xnA_h = nc.dram_tensor("xnA_h", [128, 16384], BF16, kind="Internal")
    xnB_h = nc.dram_tensor("xnB_h", [128, 16384], BF16, kind="Internal")
    av_h = nc.dram_tensor("av_h", [128, 32768], BF16, kind="Internal")
    xa_h = nc.dram_tensor("xa_h", [1024, D], F32, kind="Internal")
    y = nc.dram_tensor("y", [1024, D], F32, kind="ExternalOutput")

    dbg_outs = {}
    def mkdbg(name, shape, dtype=F32):
        if name in dbg:
            dbg_outs[name] = (nc.dram_tensor("dbg_" + name, shape, dtype,
                                             kind="ExternalOutput"), None)
    mkdbg("xnA", [128, 16384], BF16)
    mkdbg("xnB", [128, 16384], BF16)
    mkdbg("xn_oth", [128, 32768], BF16)
    mkdbg("k_sb", [128, 16384], BF16)
    mkdbg("v_sb", [128, 16384], BF16)
    mkdbg("rs_cols", [128, 16])
    mkdbg("gate_sb", [128, 24])
    mkdbg("c_cols", [128, 24])
    mkdbg("misc", [128, 32])
    mkdbg("row_sc", [1, 64])
    mkdbg("av", [128, 32768], BF16)
    mkdbg("xa", [1024, D])

    ctx = contextlib.ExitStack()
    with ctx:
        pe = SemC(ctx, nc, "pe")
        dv = SemC(ctx, nc, "dv")
        ac = SemC(ctx, nc, "ac")
        qx = SemC(ctx, nc, "qx")
        qw = SemC(ctx, nc, "qw")
        qo = SemC(ctx, nc, "qo")
        sgc = SemC(ctx, nc, "sgc")
        skc = SemC(ctx, nc, "skc")
        sxt2 = [SemC(ctx, nc, f"sxt{i}") for i in range(2)]
        soth2 = [SemC(ctx, nc, f"soth{i}") for i in range(2)]
        so5 = SemC(ctx, nc, "so5")
        swk = SemC(ctx, nc, "swk")
        swv2 = [SemC(ctx, nc, f"swv{i}") for i in range(2)]
        swvo2 = [SemC(ctx, nc, f"swvo{i}") for i in range(2)]
        swq = SemC(ctx, nc, "swq")
        sav = SemC(ctx, nc, "sav")
        sp3 = SemC(ctx, nc, "sp3")
        swo2 = [SemC(ctx, nc, f"swo{i}") for i in range(2)]
        sxo2 = [SemC(ctx, nc, f"sxo{i}") for i in range(2)]
        sxa2 = [SemC(ctx, nc, f"sxa{i}") for i in range(2)]
        sxh = SemC(ctx, nc, "sxh")
        sw12 = [SemC(ctx, nc, f"sw1{i}") for i in range(3)]
        sw22 = [SemC(ctx, nc, f"sw2{i}") for i in range(2)]
        sb2s = SemC(ctx, nc, "sb2s")
        sxf2 = [SemC(ctx, nc, f"sxf{i}") for i in range(2)]
        sy2 = [SemC(ctx, nc, f"sy{i}") for i in range(2)]
        ALLSEMS = ([pe, dv, ac, qx, qw, qo, sgc, skc, so5, swk, swq, sav, sp3,
                    sxh, sb2s] + sxt2 + soth2 + swv2 + swvo2 + swo2 + sxo2 +
                   sxa2 + sw12 + sw22 + sxf2 + sy2)

        idt16 = ctx.enter_context(nc.sbuf_tensor("idt16", [128, 128], BF16))
        tri16 = ctx.enter_context(nc.sbuf_tensor("tri16", [128, 128], BF16))
        pat16 = ctx.enter_context(nc.sbuf_tensor("pat16", [128, 128], BF16))
        ones16 = ctx.enter_context(nc.sbuf_tensor("ones16", [1, 128], BF16))
        ones32 = ctx.enter_context(nc.sbuf_tensor("ones32", [1, 128], F32))
        bqkv_cols = ctx.enter_context(nc.sbuf_tensor("bqkv_colsS", [128, 48], F32))
        sbq_cols = ctx.enter_context(nc.sbuf_tensor("sbq_cols", [128, 32], F32))
        vt_mask = ctx.enter_context(nc.sbuf_tensor("vt_maskS", [128, 32], F32))
        bqr16 = ctx.enter_context(nc.sbuf_tensor("bqr16", [1, QKV], BF16))
        rs_cols = ctx.enter_context(nc.sbuf_tensor("rs_colsS", [128, 16], F32))
        gate_sb = ctx.enter_context(nc.sbuf_tensor("gate_sbS", [128, 24], F32))
        c_cols = ctx.enter_context(nc.sbuf_tensor("c_colsS", [128, 24], F32))
        misc = ctx.enter_context(nc.sbuf_tensor("miscS", [128, 32], F32))


        glast_sb = ctx.enter_context(nc.sbuf_tensor("glast_sb", [1, 3], F32))
        gb_sb = ctx.enter_context(nc.sbuf_tensor("gb_sb", [1, 3], F32))
        fc_sb = ctx.enter_context(nc.sbuf_tensor("fc_sb", [128, 8], F32))
        for _i, _v in enumerate([1.0 / D, 1.0e-5, -0.5, SCL]):
            nc.const_aps.register(F32, _v, fc_sb[:, _i:_i + 1])

        blk = ctx.enter_context(nc.Block())
        T, V, A, G, Y = blk.tensor, blk.vector, blk.scalar, blk.gpsimd, blk.sync
        rec = {}

        def phase_barrier():
            tgt = [(s.h, s.n) for s in ALLSEMS if s.n > 0]
            for emit in (T, V, A, G, Y):
                def body(e, tgt=tuple(tgt)):
                    for h, n in tgt:
                        e.wait_ge(h, n)
                emit(body)

        def g_consts(g):
            for dst, src in [(idt16, idt_d), (tri16, tri_d), (pat16, pat_d),
                             (ones16, ones16_d), (ones32, ones32_d),
                             (bqkv_cols, bqkv_cols_d), (vt_mask, vt_mask_d),
                             (bqr16, bqkv_row_d)]:
                g.dma_start(dst[:], src[:]).then_inc(sgc.h, 16)
        G(g_consts)
        sgc.n += 128
        rec["consts"] = sgc.n

        def s_consts(sy):
            for dst, src in [(glast_sb, glast_d), (gb_sb, gb3_d),
                             (fc_sb, fconsts_d)]:
                sy.dma_start(dst[:], src[:]).then_inc(skc.h, 16)
        Y(s_consts)
        skc.n += 48
        rec["sconsts"] = skc.n

        def v_sbq(v):
            v.wait_ge(sgc.h, rec["consts"])
            v.tensor_scalar_mul(sbq_cols[:], bqkv_cols[:, 0:32], SCL).then_inc(dv.h, 1)
        V(v_sbq)
        dv.n += 1

        kv_scope = contextlib.ExitStack()
        xnA = kv_scope.enter_context(nc.sbuf_tensor("xnA", [128, 16384], BF16))
        xnB = kv_scope.enter_context(nc.sbuf_tensor("xnB", [128, 16384], BF16))

        p1a = contextlib.ExitStack()
        with p1a:
            xbuf = [p1a.enter_context(nc.sbuf_tensor(f"xbuf{i}", [128, D], F32)) for i in range(2)]
            xcb = p1a.enter_context(nc.sbuf_tensor("xcb", [128, D], F32))
            sqb = p1a.enter_context(nc.sbuf_tensor("sqb", [128, D], F32))
            xn16 = [p1a.enter_context(nc.sbuf_tensor(f"xn16_{i}", [128, D], BF16)) for i in range(2)]
            oth_st = [p1a.enter_context(nc.sbuf_tensor(f"oth_st{i}", [128, D], BF16)) for i in range(2)]
            stats = p1a.enter_context(nc.sbuf_tensor("stats", [128, 16], F32))
            t0r_sb = p1a.enter_context(nc.sbuf_tensor("t0r_sb", [1, D], F32))
            t0xc = p1a.enter_context(nc.sbuf_tensor("t0xc", [1, D], F32))
            t0c_sb = p1a.enter_context(nc.sbuf_tensor("t0c_sb", [128, 32], F32))
            gw16 = p1a.enter_context(nc.sbuf_tensor("gw16", [128, 96], BF16))
            gw32 = p1a.enter_context(nc.sbuf_tensor("gw32", [128, 96], F32))
            t0n = p1a.enter_context(nc.sbuf_tensor("t0n", [128, 32], F32))
            row_sc = p1a.enter_context(nc.sbuf_tensor("row_scS", [1, 64], F32))
            tr_ps = [p1a.enter_context(nc.psum_tensor(f"tr_ps{i}", [128, 128], BF16)) for i in range(2)]
            g_ps = p1a.enter_context(nc.psum_tensor("g_ps", [128, 3], F32))
            vt_ps = p1a.enter_context(nc.psum_tensor("vt_ps", [1, 2], F32))
            bc_ps = p1a.enter_context(nc.psum_tensor("bc_ps", [128, 8], F32))
            t0_ps = p1a.enter_context(nc.psum_tensor("t0_ps", [1, 4], F32))

            def s_t0r(sy):
                sy.dma_start(t0r_sb[:], t0row_d[:]).then_inc(skc.h, 16)
                sy.dma_start(t0c_sb[:], t0chunk_d[:]).then_inc(skc.h, 16)
            Y(s_t0r)
            skc.n += 32
            rec["t0r"] = skc.n

            def g_gw(g):
                g.dma_start(gw16[:], gw_cols_d[:]).then_inc(sgc.h, 16)
                g.dma_start(gw32[:], gw_cols_d[:]).then_inc(sgc.h, 16)
            G(g_gw)
            sgc.n += 32
            rec["gwload"] = sgc.n

            NS, NM, SSQ, RSTD, SAB = 0, 1, 2, 3, 4
            tr_g = 0

            for L in range(NB):
                if serialize:
                    phase_barrier()
                o = 8 * (L % 2)

                def s_load(sy, L=L):
                    if L >= 2:
                        sy.wait_ge(dv.h, rec[("negmean", L - 2)])
                        sy.wait_ge(ac.h, rec[("xbfree_a", L - 2)])
                    sy.dma_start(xbuf[L % 2][:], x[L * 128:(L + 1) * 128, :]).then_inc(sxt2[L % 2].h, 16)
                Y(s_load)
                sxt2[L % 2].n += 16
                rec[("xload", L)] = sxt2[L % 2].n

                def v_stats(v, L=L, o=o):
                    v.wait_ge(sxt2[L % 2].h, rec[("xload", L)])
                    v.reduce_sum(stats[:, o + NS:o + NS + 1], xbuf[L % 2][:],
                                 axis=AX.X, negate=True).then_inc(dv.h, 1)
                    v.wait_ge(dv.h, dv.n + 1)
                    v.tensor_scalar_mul(stats[:, o + NM:o + NM + 1],
                                        stats[:, o + NS:o + NS + 1],
                                        1.0 / D).then_inc(dv.h, 1)
                V(v_stats)
                dv.n += 2
                rec[("negmean", L)] = dv.n

                def a_norm(s, L=L, o=o):
                    s.wait_ge(dv.h, rec[("negmean", L)])
                    if L >= 1:
                        s.wait_ge(dv.h, rec[("v2done", L - 1)])
                    s.activation(xcb[:], xbuf[L % 2][:], AF.Identity,
                                 bias=stats[:, o + NM:o + NM + 1]).then_inc(ac.h, 1)
                    s.activation(sqb[:], xcb[:], AF.Square,
                                 accum_out=stats[:, o + SSQ:o + SSQ + 1])
                    s.activation(stats[:, o + RSTD:o + RSTD + 1],
                                 stats[:, o + SSQ:o + SSQ + 1],
                                 AF.Ln, scale=1.0 / D, bias=1.0e-5)
                    s.activation(stats[:, o + RSTD:o + RSTD + 1],
                                 stats[:, o + RSTD:o + RSTD + 1],
                                 AF.Exp, scale=-0.5).then_inc(ac.h, 1)
                A(a_norm)
                ac.n += 1
                rec[("xbfree_a", L)] = ac.n
                ac.n += 1
                rec[("anorm", L)] = ac.n

                def v2(v, L=L, o=o):
                    v.wait_ge(ac.h, rec[("anorm", L)])
                    if L >= 2:
                        v.wait_ge(pe.h, rec[("trdone", L - 2)])
                    v.reduce_sum(stats[:, o + SAB:o + SAB + 1], xcb[:], axis=AX.X,
                                 apply_absolute_value=True).then_inc(dv.h, 1)
                    v.tensor_scalar_mul(xn16[L % 2][:], xcb[:],
                                        stats[:, o + RSTD:o + RSTD + 1]).then_inc(dv.h, 1)
                    v.wait_ge(dv.h, dv.n + 1)
                    v.tensor_mul(rs_cols[:, L:L + 1], stats[:, o + SAB:o + SAB + 1],
                                 stats[:, o + RSTD:o + RSTD + 1]).then_inc(dv.h, 1)
                V(v2)
                dv.n += 3
                rec[("rs", L)] = rec[("v2done", L)] = dv.n

                for c in range(32):
                    rec[("evac_g", tr_g + c)] = ac.n + c + 1

                def t_blk(t, L=L, tr_g0=tr_g):
                    t.wait_ge(dv.h, rec[("rs", L)])
                    t.matmul(vt_ps[:], rs_cols[:, L:L + 1], vt_mask[:, 2 * L:2 * L + 2],
                             start=(L == 0), stop=(L == NB - 1)).then_inc(pe.h, 1)
                    for c in range(32):
                        gidx = tr_g0 + c
                        if gidx >= 2:
                            t.wait_ge(ac.h, rec[("evac_g", gidx - 2)])
                        t.transpose(tr_ps[gidx % 2][:],
                                    xn16[L % 2][:, c * 128:(c + 1) * 128],
                                    idt16[:]).then_inc(pe.h, 1)
                T(t_blk)
                pe.n += 1
                rec[("vt", L)] = pe.n
                for c in range(32):
                    pe.n += 1
                    rec[("tr_g", tr_g + c)] = pe.n
                rec[("trdone", L)] = pe.n

                def a_evac(s, L=L, tr_g0=tr_g):
                    for c in range(32):
                        gidx = tr_g0 + c
                        s.wait_ge(pe.h, rec[("tr_g", gidx)])
                        if L >= 10 and c == 0:
                            s.wait_ge(soth2[L % 2].h, rec[("othst", L - 2)])
                        if L < 8:
                            half = xnA if L < 4 else xnB
                            dst = half[:, c * 512 + (L % 4) * 128:
                                       c * 512 + (L % 4) * 128 + 128]
                        else:
                            dst = oth_st[L % 2][:, c * 128:(c + 1) * 128]
                        s.copy(dst, tr_ps[gidx % 2][:]).then_inc(ac.h, 1)
                A(a_evac)
                ac.n += 32
                assert ac.n == rec[("evac_g", tr_g + 31)]
                tr_g += 32
                rec[("evacdone", L)] = ac.n

                if L >= 8:
                    def s_oth(sy, L=L):
                        sy.wait_ge(ac.h, rec[("evacdone", L)])
                        dst = xn_oth_h[:].rearrange("p (c t) -> p c t", t=1024)[
                            :, :, (L - 8) * 128:(L - 8) * 128 + 128]
                        src = oth_st[L % 2][:].rearrange("p (c t) -> p c t", t=128)
                        sy.dma_start(dst, src).then_inc(soth2[L % 2].h, 16)
                    Y(s_oth)
                    soth2[L % 2].n += 16
                    rec[("othst", L)] = soth2[L % 2].n
                else:
                    def t_gate(t, L=L):
                        t.wait_ge(ac.h, rec[("evacdone", L)])
                        t.wait_ge(sgc.h, rec["gwload"])
                        if L >= 1:
                            t.wait_ge(dv.h, rec[("gev", L - 1)])
                        half = xnA if L < 4 else xnB
                        for c in range(32):
                            mm = t.matmul(
                                g_ps[:],
                                half[:, c * 512 + (L % 4) * 128:
                                     c * 512 + (L % 4) * 128 + 128],
                                gw16[:, 3 * c:3 * c + 3],
                                start=(c == 0), stop=(c == 31))
                        mm.then_inc(pe.h, 1)
                    T(t_gate)
                    pe.n += 1
                    rec[("gmm", L)] = pe.n

                    def v_gev(v, L=L):
                        v.wait_ge(pe.h, rec[("gmm", L)])
                        v.tensor_copy(gate_sb[:, 3 * L:3 * L + 3], g_ps[:]).then_inc(dv.h, 1)
                    V(v_gev)
                    dv.n += 1
                    rec[("gev", L)] = dv.n

            def v_ratio(v):
                n0, k = dv.n, 0
                def d(instr):
                    nonlocal k
                    k += 1
                    instr.then_inc(dv.h, 1)
                    v.wait_ge(dv.h, n0 + k)
                v.wait_ge(pe.h, rec[("vt", NB - 1)])
                d(v.tensor_copy(row_sc[0:1, 0:2], vt_ps[:]))
                d(v.tensor_scalar_mul(row_sc[0:1, 2:3], row_sc[0:1, 0:1], 1.0 / (VIS * D)))
                d(v.tensor_scalar_add(row_sc[0:1, 3:4], row_sc[0:1, 2:3], 1.0e-8))
                d(v.reciprocal(row_sc[0:1, 4:5], row_sc[0:1, 3:4]))
                d(v.tensor_scalar_mul(row_sc[0:1, 5:6], row_sc[0:1, 1:2],
                                      1.0 / ((S - VIS) * D)))
                d(v.tensor_mul(row_sc[0:1, 6:7], row_sc[0:1, 5:6], row_sc[0:1, 4:5]))
                v.wait_ge(skc.h, rec["sconsts"])
                d(v.tensor_scalar_mul(row_sc[0:1, 8:11], glast_sb[:], row_sc[0:1, 6:7]))
                d(v.tensor_add(row_sc[0:1, 8:11], row_sc[0:1, 8:11], gb_sb[:]))
                v.wait_ge(skc.h, rec["t0r"])
                d(v.reduce_sum(row_sc[0:1, 12:13], t0r_sb[:], axis=AX.X, negate=True))
                d(v.tensor_scalar_mul(row_sc[0:1, 13:14], row_sc[0:1, 12:13], 1.0 / D))
                d(v.tensor_scalar_add(t0xc[:], t0r_sb[:], row_sc[0:1, 13:14]))
            V(v_ratio)
            dv.n += 11
            rec["t0xc"] = dv.n

            def a_t0(s):
                s.wait_ge(dv.h, rec["t0xc"])
                s.activation(t0r_sb[:], t0xc[:], AF.Square, accum_out=row_sc[0:1, 15:16])
                s.activation(row_sc[0:1, 14:15], row_sc[0:1, 15:16],
                             AF.Ln, scale=1.0 / D, bias=1.0e-5)
                s.activation(row_sc[0:1, 14:15], row_sc[0:1, 14:15],
                             AF.Exp, scale=-0.5).then_inc(ac.h, 1)
            A(a_t0)
            ac.n += 1
            rec["t0rstd"] = ac.n

            def t_t0bc(t):
                t.wait_ge(ac.h, rec["t0rstd"])
                t.matmul(bc_ps[:, 0:2], ones32[:], row_sc[0:1, 13:15],
                         start=True, stop=True)
                t.matmul(bc_ps[:, 2:5], ones32[:], row_sc[0:1, 8:11],
                         start=False, stop=True).then_inc(pe.h, 1)
            T(t_t0bc)
            pe.n += 1
            rec["t0bc"] = pe.n

            def v_t0n(v):
                n0, k = dv.n, 0
                def d(instr):
                    nonlocal k
                    k += 1
                    instr.then_inc(dv.h, 1)
                    v.wait_ge(dv.h, n0 + k)
                v.wait_ge(pe.h, rec["t0bc"])
                d(v.tensor_copy(misc[:, 0:5], bc_ps[:, 0:5]))
                d(v.tensor_scalar_add(t0n[:], t0c_sb[:], misc[:, 0:1]))
                d(v.tensor_scalar_mul(t0n[:], t0n[:], misc[:, 1:2]))
            V(v_t0n)
            dv.n += 3
            rec["t0n"] = dv.n

            def t_t0g(t):
                t.wait_ge(dv.h, rec["t0n"])
                t.wait_ge(sgc.h, rec["gwload"])
                for c in range(32):
                    mm = t.matmul(t0_ps[0:1, 0:3], t0n[:, c:c + 1],
                                  gw32[:, 3 * c:3 * c + 3],
                                  start=(c == 0), stop=(c == 31))
                mm.then_inc(pe.h, 1)
            T(t_t0g)
            pe.n += 1
            rec["t0g"] = pe.n

            def v_t0fin(v):
                n0, k = dv.n, 0
                def d(instr):
                    nonlocal k
                    k += 1
                    instr.then_inc(dv.h, 1)
                    v.wait_ge(dv.h, n0 + k)
                v.wait_ge(pe.h, rec["t0g"])
                d(v.tensor_copy(row_sc[0:1, 16:19], t0_ps[0:1, 0:3]))
                d(v.tensor_add(row_sc[0:1, 20:23], row_sc[0:1, 16:19], row_sc[0:1, 8:11]))
                d(v.reduce_max(row_sc[0:1, 24:25], row_sc[0:1, 20:23],
                               axis=AX.X, negate=True))
            V(v_t0fin)
            dv.n += 3
            rec["t0lg"] = dv.n

            def a_t0e(s):
                s.wait_ge(dv.h, rec["t0lg"])
                s.activation(row_sc[0:1, 25:28], row_sc[0:1, 20:23], AF.Exp,
                             bias=row_sc[0:1, 24:25],
                             accum_out=row_sc[0:1, 28:29]).then_inc(ac.h, 1)
            A(a_t0e)
            ac.n += 1
            rec["t0e"] = ac.n

            def v_t0sel(v):
                n0, k = dv.n, 0
                def d(instr):
                    nonlocal k
                    k += 1
                    instr.then_inc(dv.h, 1)
                    v.wait_ge(dv.h, n0 + k)
                v.wait_ge(ac.h, rec["t0e"])
                d(v.reciprocal(row_sc[0:1, 29:30], row_sc[0:1, 28:29]))
                d(v.tensor_scalar_mul(row_sc[0:1, 30:33], row_sc[0:1, 25:28],
                                      row_sc[0:1, 29:30]))
                d(v.reduce_max(row_sc[0:1, 33:34], row_sc[0:1, 30:33], axis=AX.X))
                d(v.tensor_tensor(row_sc[0:1, 34:35], row_sc[0:1, 30:31],
                                  row_sc[0:1, 31:32], ALU.min))
                d(v.tensor_tensor(row_sc[0:1, 35:36], row_sc[0:1, 30:31],
                                  row_sc[0:1, 31:32], ALU.max))
                d(v.tensor_tensor(row_sc[0:1, 36:37], row_sc[0:1, 35:36],
                                  row_sc[0:1, 32:33], ALU.min))
                d(v.tensor_tensor(row_sc[0:1, 37:38], row_sc[0:1, 34:35],
                                  row_sc[0:1, 36:37], ALU.max))
                d(v.tensor_scalar(row_sc[0:1, 38:41], row_sc[0:1, 30:33],
                                  row_sc[0:1, 33:34], None, ALU.is_equal))
                d(v.tensor_scalar(row_sc[0:1, 41:44], row_sc[0:1, 30:33],
                                  row_sc[0:1, 37:38], None, ALU.is_equal))
            V(v_t0sel)
            dv.n += 9
            rec["t0sel"] = dv.n

            def t_ibc(t):
                t.wait_ge(dv.h, rec["t0sel"])
                t.matmul(bc_ps[:, 0:6], ones32[:], row_sc[0:1, 38:44],
                         start=True, stop=True).then_inc(pe.h, 1)
            T(t_ibc)
            pe.n += 1
            rec["ibc"] = pe.n

            def v_ibc(v):
                v.wait_ge(pe.h, rec["ibc"])
                v.tensor_copy(misc[:, 8:14], bc_ps[:, 0:6]).then_inc(dv.h, 1)
            V(v_ibc)
            dv.n += 1
            rec["ibcv"] = dv.n

            for tt in range(OWN):
                def v_ga(v, tt=tt):
                    n0 = dv.n
                    if tt == 0:
                        v.wait_ge(dv.h, rec["ibcv"])
                        v.wait_ge(dv.h, rec[("gev", OWN - 1)])
                    else:
                        v.wait_ge(ac.h, rec[("gexp", tt - 1)])
                    v.tensor_add(misc[:, 16:19], gate_sb[:, 3 * tt:3 * tt + 3],
                                 misc[:, 2:5]).then_inc(dv.h, 1)
                    v.wait_ge(dv.h, n0 + 1)
                    v.reduce_max(misc[:, 19:20], misc[:, 16:19],
                                 axis=AX.X, negate=True).then_inc(dv.h, 1)
                V(v_ga)
                dv.n += 2
                rec[("glg", tt)] = dv.n

                def a_gexp(s, tt=tt):
                    s.wait_ge(dv.h, rec[("glg", tt)])
                    s.activation(misc[:, 20:23], misc[:, 16:19], AF.Exp,
                                 bias=misc[:, 19:20],
                                 accum_out=misc[:, 23:24]).then_inc(ac.h, 1)
                A(a_gexp)
                ac.n += 1
                rec[("gexp", tt)] = ac.n

                def v_gtv(v, tt=tt):
                    n0, k = dv.n, 0
                    def d(instr):
                        nonlocal k
                        k += 1
                        instr.then_inc(dv.h, 1)
                        v.wait_ge(dv.h, n0 + k)
                    v.wait_ge(ac.h, rec[("gexp", tt)])
                    d(v.reciprocal(misc[:, 24:25], misc[:, 23:24]))
                    d(v.tensor_scalar_mul(misc[:, 25:28], misc[:, 20:23], misc[:, 24:25]))
                    d(v.reduce_max(misc[:, 28:29], misc[:, 25:28], axis=AX.X))
                    d(v.tensor_tensor(misc[:, 29:30], misc[:, 25:26], misc[:, 26:27], ALU.min))
                    d(v.tensor_tensor(misc[:, 30:31], misc[:, 25:26], misc[:, 26:27], ALU.max))
                    d(v.tensor_tensor(misc[:, 31:32], misc[:, 30:31], misc[:, 27:28], ALU.min))
                    d(v.tensor_tensor(misc[:, 29:30], misc[:, 29:30], misc[:, 31:32], ALU.max))
                    d(v.tensor_add(misc[:, 30:31], misc[:, 28:29], misc[:, 29:30]))
                    d(v.tensor_scalar_add(misc[:, 30:31], misc[:, 30:31], 1.0e-8))
                    d(v.reciprocal(misc[:, 31:32], misc[:, 30:31]))
                    d(v.tensor_mul(misc[:, 28:29], misc[:, 28:29], misc[:, 31:32]))
                    d(v.tensor_mul(misc[:, 29:30], misc[:, 29:30], misc[:, 31:32]))
                    for e in range(3):
                        d(v.tensor_mul(misc[:, 30:31], misc[:, 28:29], misc[:, 8 + e:9 + e]))
                        d(v.scalar_tensor_tensor(
                            c_cols[:, 3 * tt + e:3 * tt + e + 1],
                            misc[:, 29:30], misc[:, 11 + e:12 + e],
                            misc[:, 30:31], ALU.mult, ALU.add))
                V(v_gtv)
                dv.n += 18
                rec[("gtv", tt)] = dv.n

        phase_barrier()

        k_sb = kv_scope.enter_context(nc.sbuf_tensor("k_sb", [128, 16384], BF16))

        p1b = contextlib.ExitStack()
        with p1b:
            ps_v = [p1b.enter_context(nc.psum_tensor(f"ps_v{i}", [128, 512], F32)) for i in range(2)]

            p1bk = contextlib.ExitStack()
            oth512k = p1bk.enter_context(nc.sbuf_tensor("oth512k", [128, 16384], BF16))
            cosk = p1bk.enter_context(nc.sbuf_tensor("coskS", [128, 1024], F32))
            sink = p1bk.enter_context(nc.sbuf_tensor("sinkS", [128, 1024], F32))
            wk_sb = p1bk.enter_context(nc.sbuf_tensor("wk0", [128, 16384], BF16))
            kst = [p1bk.enter_context(nc.sbuf_tensor(f"kst{i}", [128, 512], F32)) for i in range(2)]
            ktm = [p1bk.enter_context(nc.sbuf_tensor(f"ktm{i}", [128, 512], BF16)) for i in range(2)]
            ur = p1bk.enter_context(nc.sbuf_tensor("ur", [128, 256], F32))
            ui = p1bk.enter_context(nc.sbuf_tensor("ui", [128, 256], F32))
            trk_ps = [p1bk.enter_context(nc.psum_tensor(f"trk{i}", [128, 128], BF16)) for i in range(2)]

            def hview(ap, w):
                return ap.rearrange("p (h x) -> p h x", x=w)

            def bcast4(ap2d):
                return bass.AP(ap2d.tensor, ap2d.offset,
                               [list(ap2d.ap[0]), [0, 4], list(ap2d.ap[1])])

            def s_cos(sy):
                sy.dma_start(cosk[:], cosk_d[:]).then_inc(skc.h, 16)
                sy.dma_start(sink[:], sink_d[:]).then_inc(skc.h, 16)
            Y(s_cos)
            skc.n += 32
            rec["cosk"] = skc.n

            kv_i = 0
            ktr_g = 0
            for kb in range(2):
                def g_wk(g, kb=kb):
                    if kb >= 1:
                        g.wait_ge(pe.h, rec[("kmm", kv_i - 1)])
                    g.dma_start(wk_sb[:], wkL2_d[kb]).then_inc(swk.h, 16)
                G(g_wk)
                swk.n += 16
                rec[("wk", kb)] = swk.n

                for tt in range(16):
                    if tt >= 8 and (tt - 8) % 4 == 0:
                        oh = (tt - 8) // 4
                        def s_o512(sy, oh=oh):
                            prev = rec.get(("o512_user",))
                            if prev is not None:
                                sy.wait_ge(pe.h, prev)
                            src = xn_oth_h[:].rearrange("p (c t) -> p c t", t=1024)[
                                :, :, oh * 512: oh * 512 + 512]
                            sy.dma_start(oth512k[:].rearrange("p (c t) -> p c t", t=512),
                                         src).then_inc(so5.h, 16)
                        Y(s_o512)
                        so5.n += 16
                        rec[("o512k", kb, oh)] = so5.n

                    def t_k(t, kb=kb, tt=tt, kv_i=kv_i):
                        t.wait_ge(swk.h, rec[("wk", kb)])
                        if tt >= 8:
                            t.wait_ge(so5.h, rec[("o512k", kb, (tt - 8) // 4)])
                        if kv_i >= 2:
                            t.wait_ge(ac.h, rec[("kev", kv_i - 2)])
                        for dc in range(32):
                            if tt < 8:
                                half = xnA if tt < 4 else xnB
                                xsrc = half[:, dc * 512 + (tt % 4) * 128:
                                            dc * 512 + (tt % 4) * 128 + 128]
                            else:
                                t4 = (tt - 8) % 4
                                xsrc = oth512k[:, dc * 512 + t4 * 128:
                                               dc * 512 + t4 * 128 + 128]
                            t.matmul(ps_v[kv_i % 2][:], xsrc,
                                     wk_sb[:, dc * 512:(dc + 1) * 512],
                                     start=(dc == 0), stop=False)
                        mm = t.matmul(ps_v[kv_i % 2][:], ones16[:],
                                      bqr16[0:1, D + kb * 512: D + (kb + 1) * 512],
                                      start=False, stop=True)
                        mm.then_inc(pe.h, 1)
                    T(t_k)
                    pe.n += 1
                    rec[("kmm", kv_i)] = pe.n
                    if tt >= 8:
                        rec[("o512_user",)] = pe.n

                    def a_k(s, kv_i=kv_i):
                        s.wait_ge(pe.h, rec[("kmm", kv_i)])
                        if kv_i >= 2:
                            s.wait_ge(dv.h, rec[("krope", kv_i - 2)])
                        s.activation(kst[kv_i % 2][:], ps_v[kv_i % 2][:],
                                     AF.Identity).then_inc(ac.h, 1)
                    A(a_k)
                    ac.n += 1
                    rec[("kev", kv_i)] = ac.n

                    def v_krope(v, tt=tt, kv_i=kv_i):
                        v.wait_ge(ac.h, rec[("kev", kv_i)])
                        v.wait_ge(skc.h, rec["cosk"])
                        if kv_i >= 2:
                            v.wait_ge(pe.h, rec[("ktrdone", kv_i - 2)])
                        stg = hview(kst[kv_i % 2][:], 128)
                        r, i = stg[:, :, 0:64], stg[:, :, 64:128]
                        cs = bcast4(cosk[:, tt * 64:(tt + 1) * 64])
                        sn = bcast4(sink[:, tt * 64:(tt + 1) * 64])
                        kt = hview(ktm[kv_i % 2][:], 128)
                        urv = hview(ur[:], 64)
                        uiv = hview(ui[:], 64)
                        v.tensor_tensor(urv, r, cs, ALU.mult)
                        v.tensor_tensor(uiv, i, sn, ALU.mult)
                        v.tensor_tensor(kt[:, :, 0:64], urv, uiv, ALU.subtract)
                        v.tensor_tensor(urv, r, sn, ALU.mult)
                        v.tensor_tensor(uiv, i, cs, ALU.mult)
                        v.tensor_tensor(kt[:, :, 64:128], urv, uiv, ALU.add).then_inc(dv.h, 1)
                    V(v_krope)
                    dv.n += 1
                    rec[("krope", kv_i)] = dv.n

                    for h in range(4):
                        rec[("ktrev_pre", ktr_g + h)] = None
                    def t_ktr(t, kb=kb, tt=tt, kv_i=kv_i, ktr_g0=ktr_g):
                        t.wait_ge(dv.h, rec[("krope", kv_i)])
                        for h in range(4):
                            gi = ktr_g0 + h
                            if gi >= 2:
                                t.wait_ge(ac.h, rec[("ktrev", gi - 2)])
                            t.transpose(trk_ps[gi % 2][:],
                                        ktm[kv_i % 2][:, h * 128:(h + 1) * 128],
                                        idt16[:]).then_inc(pe.h, 1)
                    for h in range(4):
                        rec[("ktrev", ktr_g + h)] = ac.n + h + 1
                    T(t_ktr)
                    for h in range(4):
                        pe.n += 1
                        rec[("ktr", ktr_g + h)] = pe.n
                    rec[("ktrdone", kv_i)] = pe.n

                    def a_ktr(s, kb=kb, tt=tt, ktr_g0=ktr_g):
                        for h in range(4):
                            gi = ktr_g0 + h
                            s.wait_ge(pe.h, rec[("ktr", gi)])
                            g128 = (4 * kb + h) * 2048
                            s.copy(k_sb[:, g128 + tt * 128: g128 + (tt + 1) * 128],
                                   trk_ps[gi % 2][:]).then_inc(ac.h, 1)
                    A(a_ktr)
                    ac.n += 4
                    assert ac.n == rec[("ktrev", ktr_g + 3)]
                    ktr_g += 4
                    kv_i += 1

            p1bk.close()
            phase_barrier()

            v_sb = kv_scope.enter_context(nc.sbuf_tensor("v_sb", [128, 16384], BF16))
            p1bv = contextlib.ExitStack()
            wv_sb = [p1bv.enter_context(nc.sbuf_tensor(f"wv{i}", [128, 16384], BF16)) for i in range(2)]
            vg = 0
            for vb in range(2):
                def g_wv(g, vb=vb):
                    g.dma_start(wv_sb[vb % 2][:], wvL_d[vb]).then_inc(swv2[vb % 2].h, 16)
                G(g_wv)
                swv2[vb % 2].n += 16
                rec[("wv", vb)] = swv2[vb % 2].n

                for tt in range(8):
                    def t_v(t, vb=vb, tt=tt, vg=vg):
                        t.wait_ge(swv2[vb % 2].h, rec[("wv", vb)])
                        if vg >= 2:
                            t.wait_ge(ac.h, rec[("vev", vg - 2)])
                        half = xnA if tt < 4 else xnB
                        for dc in range(32):
                            t.matmul(ps_v[vg % 2][:],
                                     half[:, dc * 512 + (tt % 4) * 128:
                                          dc * 512 + (tt % 4) * 128 + 128],
                                     wv_sb[vb % 2][:, dc * 512:(dc + 1) * 512],
                                     start=(dc == 0), stop=False)
                        mm = t.matmul(ps_v[vg % 2][:], ones16[:],
                                      bqr16[0:1, 5120 + vb * 512: 5120 + (vb + 1) * 512],
                                      start=False, stop=True)
                        mm.then_inc(pe.h, 1)
                    T(t_v)
                    pe.n += 1
                    rec[("vmm", vg)] = pe.n

                    def a_v(s, vb=vb, tt=tt, vg=vg):
                        s.wait_ge(pe.h, rec[("vmm", vg)])
                        s.copy(v_sb[:, tt * 1024 + vb * 512: tt * 1024 + (vb + 1) * 512],
                               ps_v[vg % 2][:]).then_inc(ac.h, 1)
                    A(a_v)
                    ac.n += 1
                    rec[("vev", vg)] = ac.n
                    vg += 1

            p1bv.close()
            phase_barrier()

            oth512 = p1b.enter_context(nc.sbuf_tensor("oth512", [128, 16384], BF16))
            wvo_sb = [p1b.enter_context(nc.sbuf_tensor(f"wvo{i}", [128, 8192], BF16)) for i in range(2)]
            for vb in range(4):
                def g_wvo(g, vb=vb):
                    if vb >= 2:
                        g.wait_ge(pe.h, rec[("wvo_user", vb - 2)])
                    g.dma_start(wvo_sb[vb % 2][:], wvoL_d[vb]).then_inc(swvo2[vb % 2].h, 16)
                G(g_wvo)
                swvo2[vb % 2].n += 16
                rec[("wvo", vb)] = swvo2[vb % 2].n

                for oh in range(2):
                    def s_o512v(sy, vb=vb, oh=oh):
                        prev = rec.get(("o512v_user",))
                        if prev is not None:
                            sy.wait_ge(pe.h, prev)
                        src = xn_oth_h[:].rearrange("p (c t) -> p c t", t=1024)[
                            :, :, oh * 512: oh * 512 + 512]
                        sy.dma_start(oth512[:].rearrange("p (c t) -> p c t", t=512),
                                     src).then_inc(so5.h, 16)
                    Y(s_o512v)
                    so5.n += 16
                    rec[("vo512", vb, oh)] = so5.n

                    for t4 in range(4):
                        tt = 8 + oh * 4 + t4
                        def t_vo(t, vb=vb, oh=oh, t4=t4, vg=vg):
                            t.wait_ge(swvo2[vb % 2].h, rec[("wvo", vb)])
                            t.wait_ge(so5.h, rec[("vo512", vb, oh)])
                            if vg >= 2:
                                t.wait_ge(ac.h, rec[("vev", vg - 2)])
                            for dc in range(32):
                                t.matmul(ps_v[vg % 2][:, 0:256],
                                         oth512[:, dc * 512 + t4 * 128:
                                                dc * 512 + t4 * 128 + 128],
                                         wvo_sb[vb % 2][:, dc * 256:(dc + 1) * 256],
                                         start=(dc == 0), stop=False)
                            mm = t.matmul(ps_v[vg % 2][:, 0:256], ones16[:],
                                          bqr16[0:1, 5120 + vb * 256: 5120 + (vb + 1) * 256],
                                          start=False, stop=True)
                            mm.then_inc(pe.h, 1)
                        T(t_vo)
                        pe.n += 1
                        rec[("vmm", vg)] = pe.n
                        rec[("o512v_user",)] = pe.n
                        rec[("wvo_user", vb)] = pe.n

                        def a_vo(s, tt=tt, vb=vb, vg=vg):
                            s.wait_ge(pe.h, rec[("vmm", vg)])
                            s.copy(v_sb[:, tt * 1024 + vb * 256: tt * 1024 + (vb + 1) * 256],
                                   ps_v[vg % 2][:, 0:256]).then_inc(ac.h, 1)
                        A(a_vo)
                        ac.n += 1
                        rec[("vev", vg)] = ac.n
                        vg += 1

        phase_barrier()

        if stop_after not in ("p1",):
            p2 = contextlib.ExitStack()
            with p2:
                cosq = p2.enter_context(nc.sbuf_tensor("cosqS", [128, 512], F32))
                sinq = p2.enter_context(nc.sbuf_tensor("sinqS", [128, 512], F32))
                wq_sb = [p2.enter_context(nc.sbuf_tensor("wq0", [128, 16384], BF16))] * 2
                qst = [p2.enter_context(nc.sbuf_tensor("qst0", [128, 512], F32))] * 2
                qtm = [p2.enter_context(nc.sbuf_tensor("qtm0", [128, 512], BF16))] * 2
                q_fm = [p2.enter_context(nc.sbuf_tensor(f"qfm{i}", [128, 4096], BF16)) for i in range(2)]
                urq = p2.enter_context(nc.sbuf_tensor("urq", [128, 256], F32))
                uiq = p2.enter_context(nc.sbuf_tensor("uiq", [128, 256], F32))
                p_sb = p2.enter_context(nc.sbuf_tensor("p_sb", [128, 2048], BF16))
                pt_sb = [p2.enter_context(nc.sbuf_tensor(f"pt{i}", [128, 512], BF16)) for i in range(2)]
                av_st = [p2.enter_context(nc.sbuf_tensor("avst0", [128, 1024], BF16))] * 2
                astat = p2.enter_context(nc.sbuf_tensor("astat", [128, 16], F32))
                ps_s = p2.enter_context(nc.psum_tensor("ps_s", [128, 2048], F32))
                ps_q = p2.enter_context(nc.psum_tensor("ps_q", [128, 512], F32))
                trp = [p2.enter_context(nc.psum_tensor(f"trp{i}", [128, 512], BF16)) for i in range(2)]
                av_ps = p2.enter_context(nc.psum_tensor("av_ps", [128, 128], F32))

                def s_cosq(sy):
                    sy.dma_start(cosq[:], cosq_d[:]).then_inc(skc.h, 16)
                    sy.dma_start(sinq[:], sinq_d[:]).then_inc(skc.h, 16)
                Y(s_cosq)
                skc.n += 32
                rec["cosq"] = skc.n

                def ps_prev(head, t_):
                    if t_ in (0, 1):
                        return (head - 1, 7) if head >= 1 else None
                    if t_ in (2, 3):
                        return (head, t_ - 2)
                    if t_ == 4:
                        return (head, 3)
                    return (head, t_ - 1)

                def ps_base(t_):
                    return ((t_ % 2) * 1024, (t_ % 2) * 1024 + 512) if t_ <= 3 \
                        else (0, 1024)

                trp_i = 0
                qg = 0

                def emit_transpose(t, src_slice, gi):
                    if gi >= 2:
                        t.wait_ge(dv.h, rec[("trpev", gi - 2)])
                    t.transpose(trp[gi % 2][:, 0:128], src_slice, idt16[:]).then_inc(pe.h, 1)

                for qb in range(8):
                    def g_wq(g, qb=qb):
                        if qb >= 1:
                            g.wait_ge(pe.h, rec[("wq_user", qb - 1)])
                        g.dma_start(wq_sb[qb % 2][:], wqL2_d[qb]).then_inc(swq.h, 16)
                    G(g_wq)
                    swq.n += 16
                    rec[("wq", qb)] = swq.n

                    for tt in range(8):
                        def t_q(t, qb=qb, tt=tt, qg=qg):
                            t.wait_ge(swq.h, rec[("wq", qb)])
                            if qg >= 2:
                                t.wait_ge(dv.h, rec[("qev", qg - 2)])
                            half = xnA if tt < 4 else xnB
                            for dc in range(32):
                                t.matmul(ps_q[:],
                                         half[:, dc * 512 + (tt % 4) * 128:
                                              dc * 512 + (tt % 4) * 128 + 128],
                                         wq_sb[qb % 2][:, dc * 512:(dc + 1) * 512],
                                         start=(dc == 0), stop=False)
                            mm = t.matmul(ps_q[:], ones16[:],
                                          bqr16[0:1, qb * 512:(qb + 1) * 512],
                                          start=False, stop=True)
                            mm.then_inc(pe.h, 1)
                        T(t_q)
                        pe.n += 1
                        rec[("qmm", qg)] = pe.n

                        def v_qev(v, qg=qg):
                            v.wait_ge(pe.h, rec[("qmm", qg)])
                            if qg >= 1:
                                v.wait_ge(dv.h, rec[("qrope", qg - 1)])
                            v.tensor_scalar_mul(qst[qg % 2][:], ps_q[:], SCL).then_inc(dv.h, 1)
                        V(v_qev)
                        dv.n += 1
                        rec[("qev", qg)] = dv.n

                        def v_qrope(v, tt=tt, qg=qg):
                            v.wait_ge(skc.h, rec["cosq"])
                            if qg >= 1:
                                v.wait_ge(pe.h, rec[("qtrdone", qg - 1)])
                            stg = qst[qg % 2][:].rearrange("p (h x) -> p h x", x=128)
                            r, i = stg[:, :, 0:64], stg[:, :, 64:128]
                            csq = cosq[:, tt * 64:(tt + 1) * 64]
                            snq = sinq[:, tt * 64:(tt + 1) * 64]
                            cs = bass.AP(csq.tensor, csq.offset,
                                         [list(csq.ap[0]), [0, 4], list(csq.ap[1])])
                            sn = bass.AP(snq.tensor, snq.offset,
                                         [list(snq.ap[0]), [0, 4], list(snq.ap[1])])
                            qt = qtm[qg % 2][:].rearrange("p (h x) -> p h x", x=128)
                            urv = urq[:].rearrange("p (h x) -> p h x", x=64)
                            uiv = uiq[:].rearrange("p (h x) -> p h x", x=64)
                            v.tensor_tensor(urv, r, cs, ALU.mult)
                            v.tensor_tensor(uiv, i, sn, ALU.mult)
                            v.tensor_tensor(qt[:, :, 0:64], urv, uiv, ALU.subtract)
                            v.tensor_tensor(urv, r, sn, ALU.mult)
                            v.tensor_tensor(uiv, i, cs, ALU.mult)
                            v.tensor_tensor(qt[:, :, 64:128], urv, uiv, ALU.add).then_inc(dv.h, 1)
                        V(v_qrope)
                        dv.n += 1
                        rec[("qrope", qg)] = dv.n

                        for h in range(4):
                            rec[("trpev", trp_i + h)] = dv.n + h + 1
                        def t_qtr(t, qg=qg, trp_i0=trp_i):
                            t.wait_ge(dv.h, rec[("qrope", qg)])
                            for h in range(4):
                                emit_transpose(t, qtm[qg % 2][:, h * 128:(h + 1) * 128],
                                               trp_i0 + h)
                        T(t_qtr)
                        for h in range(4):
                            pe.n += 1
                            rec[("trp", trp_i + h)] = pe.n
                        rec[("qtrdone", qg)] = pe.n

                        def v_qtr(v, qb=qb, tt=tt, trp_i0=trp_i):
                            for h in range(4):
                                gi = trp_i0 + h
                                v.wait_ge(pe.h, rec[("trp", gi)])
                                v.tensor_copy(q_fm[qb % 2][:, h * 1024 + tt * 128:
                                              h * 1024 + (tt + 1) * 128],
                                              trp[gi % 2][:, 0:128]).then_inc(dv.h, 1)
                        V(v_qtr)
                        dv.n += 4
                        assert dv.n == rec[("trpev", trp_i + 3)]
                        trp_i += 4
                        qg += 1
                    rec[("qfm_ready", qb)] = dv.n

                    for h4 in range(4):
                        head = qb * 4 + h4
                        g = head // GRP
                        for t_ in range(9):
                            if t_ < 8:
                                def t_s2(t, qb=qb, h4=h4, g=g, t_=t_, head=head):
                                    if t_ == 0:
                                        t.wait_ge(dv.h, rec[("qfm_ready", qb)])
                                    pv = ps_prev(head, t_)
                                    if pv is not None:
                                        t.wait_ge(ac.h, rec[("sexp", pv[0], pv[1])])
                                    L = (t_ + 1) * 128
                                    bo, be = ps_base(t_)
                                    qsl = q_fm[qb % 2][:, h4 * 1024 + t_ * 128:
                                                       h4 * 1024 + (t_ + 1) * 128]
                                    last = None
                                    for kbase, pb in ((0, bo), (1024, be)):
                                        c0 = 0
                                        while c0 < L:
                                            c1 = min(L, (c0 // 512 + 1) * 512)
                                            last = t.matmul(ps_s[:, pb + c0:pb + c1], qsl,
                                                            k_sb[:, g * 2048 + kbase + c0:
                                                                 g * 2048 + kbase + c1],
                                                            start=(c0 % 512 == 0), stop=False)
                                            c0 = c1
                                        mk = tri16 if kbase == 0 else pat16
                                        last = t.matmul(ps_s[:, pb + t_ * 128:pb + L],
                                                        idt16[:], mk[:],
                                                        start=False, stop=True)
                                    last.then_inc(pe.h, 1)
                                T(t_s2)
                                pe.n += 1
                                rec[("smm", head, t_)] = pe.n

                            if t_ >= 1:
                                tp = t_ - 1
                                ntile = 2 * (tp + 1)
                                ngrp = (ntile + 3) // 4
                                tgi0 = rec.get(("tgi",), 0)
                                _b = dv.n + (1 if t_ < 8 else 0)
                                for gg in range(ngrp):
                                    rec[("gptev", tgi0 + gg)] = _b + gg + 1
                                rec[("avev", head, tp)] = _b + ngrp + 1

                                def t_av(t, head=head, g=g, tp=tp, tgi0=tgi0):
                                    t.wait_ge(dv.h, rec[("p2", head, tp)])
                                    ntile = 2 * (tp + 1)
                                    ngrp = (ntile + 3) // 4
                                    bo, be = ps_base(tp)

                                    def grp_tr(gg):
                                        gi = tgi0 + gg
                                        if gi >= 2:
                                            t.wait_ge(dv.h, rec[("gptev", gi - 2)])
                                        lo, hi = gg * 4, min(ntile, gg * 4 + 4)
                                        for j in range(lo, hi):
                                            own = j <= tp
                                            src_c = bo + j * 128 if own else be + (j - tp - 1) * 128
                                            mm = t.transpose(
                                                trp[gi % 2][:, (j - lo) * 128:(j - lo) * 128 + 128],
                                                p_sb[:, src_c:src_c + 128], idt16[:])
                                        mm.then_inc(pe.h, 1)

                                    def grp_av(gg, last):
                                        gi = tgi0 + gg
                                        t.wait_ge(dv.h, rec[("gptev", gi)])
                                        lo, hi = gg * 4, min(ntile, gg * 4 + 4)
                                        for j in range(lo, hi):
                                            own = j <= tp
                                            kt = j if own else 8 + (j - tp - 1)
                                            mm = t.matmul(
                                                av_ps[:],
                                                v_sb[:, kt * 1024 + g * 128:
                                                     kt * 1024 + (g + 1) * 128],
                                                pt_sb[gi % 2][:, (j - lo) * 128:(j - lo) * 128 + 128],
                                                start=(j == 0),
                                                stop=(last and j == hi - 1))
                                        if last:
                                            mm.then_inc(pe.h, 1)

                                    grp_tr(0)
                                    for gg in range(1, ngrp):
                                        grp_tr(gg)
                                        grp_av(gg - 1, False)
                                    grp_av(ngrp - 1, True)
                                T(t_av)
                                for gg in range(ngrp):
                                    pe.n += 1
                                    rec[("gtr", tgi0 + gg)] = pe.n
                                pe.n += 1
                                rec[("avmm", head, tp)] = pe.n
                                rec[("trdone", "att", head, tp)] = rec[("gtr", tgi0 + ngrp - 1)]

                            if t_ < 8:
                                def v_sm(v, head=head, t_=t_):
                                    v.wait_ge(pe.h, rec[("smm", head, t_)])
                                    L = (t_ + 1) * 128
                                    bo, be = ps_base(t_)
                                    step = be - bo
                                    a0 = ps_s[:, bo:bo + 2 * step]
                                    vv = a0.rearrange("p (b c) -> p b c", b=2)[:, :, 0:L]
                                    ao = 8 * (t_ % 2)
                                    v.reduce_max(astat[:, ao + 2:ao + 3], vv,
                                                 axis=AX.XY, negate=True).then_inc(dv.h, 1)
                                V(v_sm)
                                dv.n += 1
                                rec[("negm", head, t_)] = dv.n

                            if t_ >= 1:
                                def v_pt(v, head=head, tp=tp, tgi0=tgi0):
                                    ntile = 2 * (tp + 1)
                                    ngrp = (ntile + 3) // 4
                                    for gg in range(ngrp):
                                        gi = tgi0 + gg
                                        w = (min(ntile, gg * 4 + 4) - gg * 4) * 128
                                        v.wait_ge(pe.h, rec[("gtr", gi)])
                                        v.tensor_copy(pt_sb[gi % 2][:, 0:w],
                                                      trp[gi % 2][:, 0:w]).then_inc(dv.h, 1)
                                    v.wait_ge(pe.h, rec[("avmm", head, tp)])
                                    if tp == 0 and head >= 1:
                                        v.wait_ge(sav.h, rec[("avst", head - 1)])
                                    v.tensor_copy(av_st[head % 2][:, tp * 128:(tp + 1) * 128],
                                                  av_ps[:]).then_inc(dv.h, 1)
                                V(v_pt)
                                dv.n += ngrp + 1
                                assert dv.n == rec[("avev", head, tp)]
                                rec[("tgi",)] = tgi0 + ngrp

                            if t_ < 8:
                                def a_exp(s, head=head, t_=t_):
                                    s.wait_ge(dv.h, rec[("negm", head, t_)])
                                    pv = ps_prev(head, t_)
                                    if pv is not None:
                                        s.wait_ge(dv.h, rec[("p2", pv[0], pv[1])])
                                        s.wait_ge(pe.h, rec[("trdone", "att", pv[0], pv[1])])
                                    L = (t_ + 1) * 128
                                    bo, be = ps_base(t_)
                                    step = be - bo
                                    vin = ps_s[:, bo:bo + 2 * step].rearrange(
                                        "p (b c) -> p b c", b=2)[:, :, 0:L]
                                    vout = p_sb[:, bo:bo + 2 * step].rearrange(
                                        "p (b c) -> p b c", b=2)[:, :, 0:L]
                                    ao = 8 * (t_ % 2)
                                    s.activation(vout, vin, AF.Exp,
                                                 bias=astat[:, ao + 2:ao + 3],
                                                 accum_out=astat[:, ao + 5:ao + 6]).then_inc(ac.h, 1)
                                A(a_exp)
                                ac.n += 1
                                rec[("sexp", head, t_)] = ac.n

                                def v_p2(v, head=head, t_=t_):
                                    n0 = dv.n
                                    v.wait_ge(ac.h, rec[("sexp", head, t_)])
                                    L = (t_ + 1) * 128
                                    ao = 8 * (t_ % 2)
                                    v.reciprocal(astat[:, ao + 6:ao + 7],
                                                 astat[:, ao + 5:ao + 6]).then_inc(dv.h, 1)
                                    v.wait_ge(dv.h, n0 + 1)
                                    bo, be = ps_base(t_)
                                    step = be - bo
                                    vio = p_sb[:, bo:bo + 2 * step].rearrange(
                                        "p (b c) -> p b c", b=2)[:, :, 0:L]
                                    v.tensor_scalar_mul(vio, vio,
                                                        astat[:, ao + 6:ao + 7]).then_inc(dv.h, 1)
                                V(v_p2)
                                dv.n += 2
                                rec[("p2", head, t_)] = dv.n

                        def s_av(sy, head=head):
                            sy.wait_ge(dv.h, rec[("avev", head, 7)])
                            sy.dma_start(av_h[:, head * 1024:(head + 1) * 1024],
                                         av_st[head % 2][:]).then_inc(sav.h, 16)
                        Y(s_av)
                        sav.n += 16
                        rec[("avst", head)] = sav.n

                    rec[("wq_user", qb)] = pe.n

                def s_xnb(sy):
                    sy.dma_start(xnA_h[:], xnA[:]).then_inc(sav.h, 16)
                    sy.dma_start(xnB_h[:], xnB[:]).then_inc(sav.h, 16)
                Y(s_xnb)
                sav.n += 32
                rec["xnbounce"] = sav.n

            phase_barrier()
            kv_scope.close()

        if stop_after not in ("p1", "attn"):
            p3 = contextlib.ExitStack()
            with p3:
                av_sb = p3.enter_context(nc.sbuf_tensor("av_sb", [128, 32768], BF16))
                wo_sb = [p3.enter_context(nc.sbuf_tensor(f"wo{i}", [128, 16384], BF16)) for i in range(2)]
                xo_sb = [p3.enter_context(nc.sbuf_tensor(f"xo{i}", [128, 512], F32)) for i in range(2)]
                xa_sb = [p3.enter_context(nc.sbuf_tensor(f"xas{i}", [128, 512], F32)) for i in range(2)]
                bo16 = p3.enter_context(nc.sbuf_tensor("bo16", [1, D], BF16))
                po_ps = [p3.enter_context(nc.psum_tensor(f"po{i}", [128, 512], F32)) for i in range(3)]

                def g_p3c(g):
                    g.dma_start(bo16[:], bo_row_d[:]).then_inc(sp3.h, 16)
                    g.dma_start(av_sb[:], av_h[:]).then_inc(sp3.h, 16)
                G(g_p3c)
                sp3.n += 32
                rec["p3c"] = sp3.n

                og = 0
                for jg in range(8):
                    def g_wo(g, jg=jg):
                        if jg >= 2:
                            g.wait_ge(pe.h, rec[("wo_user", jg - 2)])
                        g.dma_start(wo_sb[jg % 2][:], woL_d[jg]).then_inc(swo2[jg % 2].h, 16)
                    G(g_wo)
                    swo2[jg % 2].n += 16
                    rec[("wo", jg)] = swo2[jg % 2].n

                    for tt in range(8):
                        def s_xo(sy, jg=jg, tt=tt, og=og):
                            if og >= 2:
                                sy.wait_ge(dv.h, rec[("xaev", og - 2)])
                            sy.dma_start(xo_sb[og % 2][:],
                                         x[tt * 128:(tt + 1) * 128,
                                           jg * 512:(jg + 1) * 512]).then_inc(sxo2[og % 2].h, 16)
                        Y(s_xo)
                        sxo2[og % 2].n += 16
                        rec[("xo", og)] = sxo2[og % 2].n

                        def t_o(t, jg=jg, tt=tt, og=og):
                            t.wait_ge(swo2[jg % 2].h, rec[("wo", jg)])
                            t.wait_ge(sp3.h, rec["p3c"])
                            if og >= 3:
                                t.wait_ge(dv.h, rec[("xaev", og - 3)])
                            for dc in range(32):
                                t.matmul(po_ps[og % 3][:],
                                         av_sb[:, dc * 1024 + tt * 128:
                                               dc * 1024 + (tt + 1) * 128],
                                         wo_sb[jg % 2][:, dc * 512:(dc + 1) * 512],
                                         start=(dc == 0), stop=False)
                            mm = t.matmul(po_ps[og % 3][:], ones16[:],
                                          bo16[0:1, jg * 512:(jg + 1) * 512],
                                          start=False, stop=True)
                            mm.then_inc(pe.h, 1)
                        T(t_o)
                        pe.n += 1
                        rec[("omm", og)] = pe.n

                        def v_xa(v, jg=jg, tt=tt, og=og):
                            v.wait_ge(pe.h, rec[("omm", og)])
                            v.wait_ge(sxo2[og % 2].h, rec[("xo", og)])
                            if og >= 2:
                                v.wait_ge(sxa2[og % 2].h, rec[("xast", og - 2)])
                            v.tensor_add(xa_sb[og % 2][:], po_ps[og % 3][:],
                                         xo_sb[og % 2][:]).then_inc(dv.h, 1)
                        V(v_xa)
                        dv.n += 1
                        rec[("xaev", og)] = dv.n

                        def s_xa(sy, jg=jg, tt=tt, og=og):
                            sy.wait_ge(dv.h, rec[("xaev", og)])
                            sy.dma_start(xa_h[tt * 128:(tt + 1) * 128,
                                              jg * 512:(jg + 1) * 512],
                                         xa_sb[og % 2][:]).then_inc(sxa2[og % 2].h, 16)
                        Y(s_xa)
                        sxa2[og % 2].n += 16
                        rec[("xast", og)] = sxa2[og % 2].n
                        og += 1
                    rec[("wo_user", jg)] = pe.n

            phase_barrier()

        if stop_after not in ("p1", "attn", "outproj"):
            p4 = contextlib.ExitStack()
            with p4:
                b1c_sb = p4.enter_context(nc.sbuf_tensor("b1c_sb", [128, 96], F32))
                act_sb = p4.enter_context(nc.sbuf_tensor("act_sb", [128, 49152], BF16))
                acc_sb = p4.enter_context(nc.sbuf_tensor("acc_sb", [128, 2048], F32))
                xf_sb = [p4.enter_context(nc.sbuf_tensor(f"xf{i}", [128, 512], F32)) for i in range(2)]
                rs_st = [p4.enter_context(nc.sbuf_tensor(f"rsst{i}", [128, 512], F32)) for i in range(2)]
                ftmp = p4.enter_context(nc.sbuf_tensor("ftmp", [128, 512], F32))
                pf_ps = [p4.enter_context(nc.psum_tensor(f"pf{i}", [128, 512], F32)) for i in range(3)]
                p2_ps = [p4.enter_context(nc.psum_tensor(f"p2p{i}", [128, 512], F32)) for i in range(3)]

                def g_p4c(g):
                    g.dma_start(b1c_sb[:], b1cols_d[:]).then_inc(sgc.h, 16)
                G(g_p4c)
                sgc.n += 16
                rec["p4c"] = sgc.n

                AFS = [AF.Tanh, AF.Gelu, AF.Silu]
                w1g = 0
                f1g = 0
                f2g = 0
                for H in range(2):
                    p4a = contextlib.ExitStack()
                    xnh = p4a.enter_context(nc.sbuf_tensor(f"xnh{H}", [128, 16384], BF16))
                    w1_sb = [p4a.enter_context(nc.sbuf_tensor(f"w1b{H}_{i}", [128, D], BF16)) for i in range(3)]
                    def s_xnh(sy, H=H):
                        if H >= 1:
                            sy.wait_ge(pe.h, rec[("ffn1_done", H - 1)])
                        sy.dma_start(xnh[:], (xnA_h if H == 0 else xnB_h)[:]).then_inc(sxh.h, 16)
                    Y(s_xnh)
                    sxh.n += 16
                    rec[("xnh", H)] = sxh.n

                    for e in range(3):
                        for fc in range(32):
                            def g_w1(g, H=H, e=e, fc=fc, w1g=w1g):
                                if w1g >= 3:
                                    g.wait_ge(pe.h, rec[("f1mm", w1g - 3)])
                                g.dma_start(w1_sb[w1g % 3][:], w1L_d[e, fc]).then_inc(sw12[w1g % 3].h, 16)
                            G(g_w1)
                            sw12[w1g % 3].n += 16
                            rec[("w1", w1g)] = sw12[w1g % 3].n

                            def t_f1(t, H=H, e=e, fc=fc, w1g=w1g, f1g=f1g):
                                t.wait_ge(sw12[w1g % 3].h, rec[("w1", w1g)])
                                t.wait_ge(sxh.h, rec[("xnh", H)])
                                t.wait_ge(sgc.h, rec["p4c"])
                                if f1g >= 3:
                                    t.wait_ge(ac.h, rec[("f1ev", f1g - 3)])
                                for dc in range(32):
                                    mm = t.matmul(pf_ps[f1g % 3][:],
                                                  w1_sb[w1g % 3][:, dc * 128:(dc + 1) * 128],
                                                  xnh[:, dc * 512:(dc + 1) * 512],
                                                  start=(dc == 0), stop=(dc == 31))
                                mm.then_inc(pe.h, 1)
                            T(t_f1)
                            pe.n += 1
                            rec[("f1mm", w1g)] = pe.n

                            def a_f1(s, e=e, fc=fc, f1g=f1g, w1g=w1g):
                                s.wait_ge(pe.h, rec[("f1mm", w1g)])
                                s.activation(act_sb[:, e * 16384 + fc * 512:
                                                    e * 16384 + (fc + 1) * 512],
                                             pf_ps[f1g % 3][:], AFS[e],
                                             bias=b1c_sb[:, e * 32 + fc:e * 32 + fc + 1]
                                             ).then_inc(ac.h, 1)
                            A(a_f1)
                            ac.n += 1
                            rec[("f1ev", f1g)] = ac.n
                            f1g += 1
                            w1g += 1
                    rec[("ffn1_done", H)] = pe.n
                    phase_barrier()
                    p4a.close()
                    p4b = contextlib.ExitStack()
                    w2_sb = [p4b.enter_context(nc.sbuf_tensor(f"w2b{H}_{i}", [128, 16384], BF16)) for i in range(2)]
                    b2c16 = p4b.enter_context(nc.sbuf_tensor(f"b2c{H}", [1, D], BF16))

                    for jg in range(8):
                        for e in range(3):
                            w2i = (H * 8 + jg) * 3 + e
                            def g_w2(g, e=e, jg=jg, w2i=w2i, H=H):
                                if w2i >= 2 and ("f2_w2user", w2i - 2) in rec:
                                    g.wait_ge(pe.h, rec[("f2_w2user", w2i - 2)])
                                g.dma_start(w2_sb[w2i % 2][:], w2L_d[e, jg]).then_inc(sw22[w2i % 2].h, 16)
                                if w2i >= 1 and ("f2_w2user", w2i - 1) in rec:
                                    g.wait_ge(pe.h, rec[("f2_w2user", w2i - 1)])
                                g.dma_start(b2c16[:], b2_d[e:e + 1, :]).then_inc(sb2s.h, 16)
                            G(g_w2)
                            sw22[w2i % 2].n += 16
                            sb2s.n += 16
                            rec[("w2", w2i)] = sw22[w2i % 2].n
                            rec[("b2l", w2i)] = sb2s.n

                            for tt in range(4):
                                if e == 0:
                                    def s_xf(sy, H=H, jg=jg, tt=tt, f2g=f2g):
                                        if f2g >= 2:
                                            sy.wait_ge(dv.h, rec[("f2acc", f2g - 2)])
                                        sy.dma_start(
                                            xf_sb[f2g % 2][:],
                                            xa_h[H * 512 + tt * 128: H * 512 + (tt + 1) * 128,
                                                 jg * 512:(jg + 1) * 512]).then_inc(sxf2[tt % 2].h, 16)
                                    Y(s_xf)
                                    sxf2[tt % 2].n += 16
                                    rec[("xf", H, jg, tt)] = sxf2[tt % 2].n

                                def t_f2(t, H=H, e=e, jg=jg, tt=tt, w2i=w2i, f2g=f2g):
                                    t.wait_ge(sw22[w2i % 2].h, rec[("w2", w2i)])
                                    t.wait_ge(sb2s.h, rec[("b2l", w2i)])
                                    t.wait_ge(ac.h, rec[("f1ev", (H * 3 + e) * 32 + 31)])
                                    if f2g >= 3:
                                        t.wait_ge(dv.h, rec[("f2acc", f2g - 3)])
                                    for fc in range(32):
                                        t.matmul(p2_ps[f2g % 3][:],
                                                 act_sb[:, e * 16384 + fc * 512 + tt * 128:
                                                        e * 16384 + fc * 512 + tt * 128 + 128],
                                                 w2_sb[w2i % 2][:, fc * 512:(fc + 1) * 512],
                                                 start=(fc == 0), stop=False)
                                    mm = t.matmul(p2_ps[f2g % 3][:], ones16[:],
                                                  b2c16[0:1, jg * 512:(jg + 1) * 512],
                                                  start=False, stop=True)
                                    mm.then_inc(pe.h, 1)
                                T(t_f2)
                                pe.n += 1
                                rec[("f2mm", f2g)] = pe.n
                                rec[("f2_w2user", w2i)] = pe.n

                                def v_f2(v, H=H, e=e, jg=jg, tt=tt, f2g=f2g):
                                    v.wait_ge(pe.h, rec[("f2mm", f2g)])
                                    ccol = c_cols[:, 3 * (H * 4 + tt) + e:
                                                  3 * (H * 4 + tt) + e + 1]
                                    asl = acc_sb[:, tt * 512:(tt + 1) * 512]
                                    if e == 0:
                                        v.wait_ge(sxf2[tt % 2].h, rec[("xf", H, jg, tt)])
                                        v.scalar_tensor_tensor(
                                            asl, p2_ps[f2g % 3][:], ccol,
                                            xf_sb[f2g % 2][:],
                                            ALU.mult, ALU.add).then_inc(dv.h, 1)
                                    elif e == 1:
                                        v.scalar_tensor_tensor(
                                            asl, p2_ps[f2g % 3][:], ccol, asl,
                                            ALU.mult, ALU.add).then_inc(dv.h, 1)
                                    else:
                                        pv = rec.get(("yst", H, jg, tt - 2)) or \
                                             rec.get(("yst", H, jg - 1, tt + 2))
                                        if pv:
                                            v.wait_ge(sy2[tt % 2].h, pv)
                                        v.scalar_tensor_tensor(
                                            rs_st[tt % 2][:], p2_ps[f2g % 3][:], ccol,
                                            asl, ALU.mult, ALU.add).then_inc(dv.h, 1)
                                V(v_f2)
                                dv.n += 1
                                rec[("f2acc", f2g)] = dv.n

                                if e == 2:
                                    def s_y(sy, H=H, jg=jg, tt=tt, f2g=f2g):
                                        sy.wait_ge(dv.h, rec[("f2acc", f2g)])
                                        sy.dma_start(
                                            y[H * 512 + tt * 128: H * 512 + (tt + 1) * 128,
                                              jg * 512:(jg + 1) * 512],
                                            rs_st[tt % 2][:]).then_inc(sy2[tt % 2].h, 16)
                                    Y(s_y)
                                    sy2[tt % 2].n += 16
                                    rec[("yst", H, jg, tt)] = sy2[tt % 2].n
                                f2g += 1
                    phase_barrier()
                    p4b.close()

        def s_dump(sy):
            sy.wait_ge(pe.h, pe.n)
            sy.wait_ge(dv.h, dv.n)
            sy.wait_ge(ac.h, ac.n)
            sb_map = {"xnA": xnA, "xnB": xnB, "k_sb": k_sb, "v_sb": v_sb,
                      "rs_cols": rs_cols, "gate_sb": gate_sb, "c_cols": c_cols,
                      "misc": misc}
            dr_map = {"xn_oth": xn_oth_h, "av": av_h, "xa": xa_h}
            for name, (dst, _) in dbg_outs.items():
                src = sb_map.get(name) or dr_map.get(name)
                sy.dma_start(dst[:], src[:]).then_inc(qo.h, 16)
                qo.n += 16
            for s_ in ALLSEMS:
                if s_.n:
                    sy.wait_ge(s_.h, s_.n)
        Y(s_dump)
        kv_scope.close()
    return nc, dbg_outs


def kernel(**inputs):
    in_maps, metas = _prep_inputs(inputs)
    nc, _ = build_program(stop_after="all")
    res = run_bass_kernel_spmd(nc, in_maps, core_ids=list(range(8)))
    out = np.zeros((B, S, D), np.float32)
    for c in range(8):
        yc = res.results[c]["y"]
        for t, g in enumerate(metas[c]["own_g"]):
            out[metas[c]["b"], g * 128:(g + 1) * 128] = yc[t * 128:(t + 1) * 128]
    return out
